# version 15
# baseline (speedup 1.0000x reference)
import contextlib
import numpy as np
import ml_dtypes
import concourse.bass as bass
import concourse.mybir as mybir
from concourse.bass_utils import run_bass_kernel_spmd

F32 = mybir.dt.float32
BF16 = mybir.dt.bfloat16
AF = mybir.ActivationFunctionType
ALU = mybir.AluOpType
AX = mybir.AxisListType

D = 1024
S = 4096
B = 4
H, DH = 16, 64
TOK = 2048
NCH = 8
BF = ml_dtypes.bfloat16
EPS = 1e-5


class Arena:
    def __init__(self, nc, name="arena", cols=47400):
        self.cm = nc.sbuf_tensor(name, [128, cols], F32)
        self.cols = cols

    def __enter__(self):
        self.h = self.cm.__enter__()
        return self

    def __exit__(self, *a):
        return self.cm.__exit__(*a)

    def view(self, off, shape, dtype=F32, base=0):
        parts = shape[0]
        free = 1
        for s_ in shape[1:]:
            free *= s_
        fw = free if dtype == F32 else (free + 1) // 2
        assert off + fw <= self.cols, f"arena overflow {off}+{fw}>{self.cols}"
        assert base + parts <= 128
        ap = self.h[base:base + parts, off:off + fw]
        if dtype != F32:
            ap = ap.bitcast(dtype)
        if len(shape) > 2:
            names = " ".join(f"d{i}" for i in range(1, len(shape)))
            ap = ap.rearrange(f"p ({names}) -> p {names}",
                              **{f"d{i}": shape[i] for i in range(1, len(shape))})
        return ap


class Emitter:

    ENGINES = ("sync", "vector", "tensor", "scalar", "gpsimd")
    SELF_SER = ("vector", "scalar", "gpsimd")

    def __init__(self, nc, es):
        self.nc = nc
        self.ops = {e: [] for e in self.ENGINES}
        self.cnt = {e: 0 for e in self.ENGINES}
        self.sems = {}
        self.es = es
        self.prefix = ""
        for e in self.ENGINES:
            self.sems[e] = es.enter_context(nc.semaphore("sem_" + e))

    def qname(self, name):
        return "q_" + self.prefix + name

    def qcnt(self, name):
        return self.cnt.get(self.qname(name), 0)

    def op(self, engine, build, waits=(), dma=False):
        sems = self.sems
        waits = [((("q_" + self.prefix + e_[2:]) if e_.startswith("q_") else e_), v)
                 for (e_, v) in waits if callable(v) or v > 0]
        if (engine in self.SELF_SER) and not dma and self.cnt[engine] > 0:
            waits.append((engine, self.cnt[engine]))
        if dma:
            ch = "q_" + self.prefix + (dma if isinstance(dma, str) else engine)
            if ch not in self.sems:
                self.sems[ch] = self.es.enter_context(self.nc.semaphore(ch))
                self.cnt[ch] = 0
            self.cnt[ch] += 16
            myinc = 16
            mysem = sems[ch]
            key = ch
        else:
            self.cnt[engine] += 1
            myinc = 1
            mysem = sems[engine]
            key = engine

        def f(eng):
            for e_, v in waits:
                v = v() if callable(v) else v
                if v > 0:
                    eng.wait_ge(sems[e_], v)
            inst = build(eng)
            if inst is not None:
                inst.then_inc(mysem, myinc)
        self.ops[engine].append(f)
        return self.cnt[key]

    def wait(self, engine, on, val):
        if on.startswith("q_"):
            on = "q_" + self.prefix + on[2:]
        sems = self.sems
        self.ops[engine].append(lambda eng: eng.wait_ge(sems[on], val))

    def barrier(self):
        snap = dict(self.cnt)
        for e in self.ENGINES:
            sems = self.sems
            def f(eng, snap=snap, e=e):
                for k, v in snap.items():
                    if v > 0 and k != e:
                        eng.wait_ge(sems[k], v)
            self.ops[e].append(f)

    def build(self, blk):
        for name in self.ENGINES:
            lst = self.ops[name]
            if not lst:
                continue

            def body(eng, lst=lst):
                for f in lst:
                    f(eng)
            getattr(blk, name)(body)


def _fm(x):
    return np.ascontiguousarray(np.asarray(x).T)


def mk_dr_A(nc, fused=False):
    ki = "ExternalInput"
    ko = "Internal" if fused else "ExternalOutput"
    d = {nm: nc.dram_tensor(nm, [D, D], F32, kind=ki).ap()
         for nm in ("Wq", "Wk", "Wv", "Wo")}
    d["x_fm"] = nc.dram_tensor("x_fm", [D, S], F32, kind=ki).ap()
    d["xq_fm"] = nc.dram_tensor("xq_fm", [D, TOK], F32, kind=ki).ap()
    d["bo"] = nc.dram_tensor("bo", [128, NCH], F32, kind=ki).ap()
    d["stpat"] = nc.dram_tensor("stpat", [128, 130], F32, kind=ki).ap()
    d["consts"] = nc.dram_tensor("consts", [128, 68 if not fused else 132], F32, kind=ki).ap()
    d["x1_fm"] = nc.dram_tensor("x1_fm", [D, TOK], F32, kind=ko).ap()
    d["k_hbm"] = nc.dram_tensor("k_hbm", [D, S], BF16, kind="Internal").ap()
    d["v_hbm"] = nc.dram_tensor("v_hbm", [S, D], BF16, kind="Internal").ap()
    d["r_hbm"] = nc.dram_tensor("r_hbm", [TOK], F32, kind="Internal").ap()
    d["s_hbm"] = nc.dram_tensor("s_hbm", [2, 512], F32, kind="Internal").ap()
    return d


def emit_A(nc, em, ar, P, DR, fused=False):
    dx, dxq = DR["x_fm"], DR["xq_fm"]
    dW = {nm: DR[nm] for nm in ("Wq", "Wk", "Wv", "Wo")}
    dbo, dst, dcst = DR["bo"], DR["stpat"], DR["consts"]
    dout = DR["x1_fm"]
    k_hbm, v_hbm, r_hbm, s_hbm = DR["k_hbm"], DR["v_hbm"], DR["r_hbm"], DR["s_hbm"]
    if True:
        q_sb = ar.view(0, [128, NCH, TOK], BF16)
        onorm = ar.view(8192, [128, NCH, TOK], BF16)
        bo_sb = ar.view(16384, [128, NCH], F32)
        stp = ar.view(16392, [128, 130], F32)
        cst = ar.view(16522, [128, 68], F32)
        SCR = 16704
        wq_sb = ar.view(SCR, [128, NCH, D], BF16)
        wk_sb = ar.view(SCR + 4096, [128, NCH, D], BF16)
        wv_sb = ar.view(SCR + 8192, [128, NCH, D], BF16)
        w_stage = ar.view(SCR + 12288, [128, 4, D], F32)
        x_stage = ar.view(SCR + 16384, [128, NCH, 512], F32)
        xbf = ar.view(SCR + 20480, [128, 2, NCH, 512], BF16)
        kv_out = ar.view(SCR + 24576, [128, 2, 512], BF16)
        kh = [ar.view(SCR, [64, S], BF16, base=0),
              ar.view(SCR, [64, S], BF16, base=64)]
        vt = ar.view(SCR + 2048, [128, 2, 32, 65], BF16)
        pt = ar.view(SCR + 4160, [128, 2, 1024], BF16)
        avt = ar.view(SCR + 5184, [65, 2, 1024], F32)
        rcp_t = ar.view(SCR + 7232, [1, 1024], F32)
        rcp_h = ar.view(SCR + 8256, [1, 1024], F32)
        rrep = ar.view(SCR + 9280, [64, 2, 1024], F32)
        resid = ar.view(SCR, [128, NCH, TOK], F32)
        P3 = 33088
        wo_sb = ar.view(P3, [128, NCH, D], BF16)
        wo_stage = ar.view(P3 + 4096, [128, 4, D], F32)
        xq_st = ar.view(P3 + 8192, [128, 2, 512], F32)
        sqt = ar.view(P3 + 9216, [128, 512], F32)
        mrow = ar.view(P3 + 9728, [1, 512], F32)
        s2row = ar.view(P3 + 10240, [1, 512], F32)
        msq = ar.view(P3 + 10752, [1, 512], F32)
        varr = ar.view(P3 + 11264, [1, 512], F32)
        rstd_t = ar.view(P3 + 11776, [1, 512], F32)
        mrs_t = ar.view(P3 + 12288, [1, 512], F32)
        lnrep = ar.view(P3 + 12800, [128, 2, 512], F32)

        PA, PB = P["PA"], P["PB"]
        xre = dx.rearrange("(c p) t -> p c t", p=128)
        xqre = dxq.rearrange("(c p) t -> p c t", p=128)
        vre = v_hbm.rearrange("(kc p) d -> p kc d", p=128)

        em.op("sync", lambda e: e.dma_start(out=stp, in_=dst), dma="ald")
        em.op("sync", lambda e: e.dma_start(out=cst, in_=dcst), dma="ald")
        m_init = em.op("sync", lambda e: e.dma_start(out=bo_sb, in_=dbo), dma="ald")

        W_SBS = {"Wq": wq_sb, "Wk": wk_sb, "Wv": wv_sb}
        m_cast = 0
        for i, (nm, hf) in enumerate([(n_, h_) for n_ in ("Wq", "Wk", "Wv") for h_ in range(2)]):
            m_ld = em.op("sync",
                         lambda e, nm=nm, hf=hf: e.dma_start(
                             out=w_stage,
                             in_=dW[nm].rearrange("(c p) o -> p c o", p=128)[:, hf * 4:hf * 4 + 4, :]),
                         waits=[("vector", m_cast)], dma="ald")
            m_cast = em.op("vector",
                           lambda e, nm=nm, hf=hf: e.tensor_copy(
                               out=W_SBS[nm][:, hf * 4:hf * 4 + 4, :], in_=w_stage),
                           waits=[("q_ald", m_ld)])

        g = 0
        m_tile = {}
        m_evac = {}
        m_store = {}
        chunk_start = []
        for phase, srcre, n_outer in (("KV", xre, 8), ("Q", xqre, 4)):
            for tc in range(n_outer):
                ci = len(chunk_start)
                chunk_start.append(g)
                m_ld = em.op("sync",
                             lambda e, srcre=srcre, tc=tc: e.dma_start(
                                 out=x_stage, in_=srcre[:, :, tc * 512:(tc + 1) * 512]),
                             waits=[("vector", m_cast)], dma="ald")
                par = ci % 2
                tw = m_tile.get(chunk_start[ci - 1] - 1 + 16, 0) if ci >= 2 else 0
                tw = m_tile.get(chunk_start[ci - 1] - 1, 0) if ci >= 2 else 0
                m_cast = em.op("vector",
                               lambda e, par=par: e.tensor_copy(out=xbf[:, par], in_=x_stage),
                               waits=[("q_ald", m_ld), ("tensor", tw)])
                tiles = ([("K", oc, 0) for oc in range(8)]
                         + [("V", t_, d_) for t_ in range(4) for d_ in range(2)]
                         ) if phase == "KV" else [("Q", oc, 0) for oc in range(8)]
                for (kind, a_, b_) in tiles:
                    bank = g % 2

                    def mk_mm(e, kind=kind, a_=a_, b_=b_, par=par, bank=bank):
                        inst = None
                        for kc in range(8):
                            if kind == "K":
                                lhsT = wk_sb[:, kc, a_ * 128:(a_ + 1) * 128]
                                rhs = xbf[:, par, kc, :]
                            elif kind == "V":
                                lhsT = xbf[:, par, kc, a_ * 128:(a_ + 1) * 128]
                                rhs = wv_sb[:, kc, b_ * 512:(b_ + 1) * 512]
                            else:
                                lhsT = wq_sb[:, kc, a_ * 128:(a_ + 1) * 128]
                                rhs = xbf[:, par, kc, :]
                            inst = e.matmul(PA[:, bank * 512:bank * 512 + 512], lhsT, rhs,
                                            start=(kc == 0), stop=(kc == 7))
                        return inst
                    m_tile[g] = em.op("tensor", mk_mm,
                                      waits=[("vector", m_cast),
                                             ("vector", m_evac.get(g - 2, 0))])
                    if kind == "Q":
                        m_evac[g] = em.op("vector",
                                          lambda e, a_=a_, tc=tc, bank=bank: e.tensor_copy(
                                              out=q_sb[:, a_, tc * 512:(tc + 1) * 512],
                                              in_=PA[:, bank * 512:bank * 512 + 512]),
                                          waits=[("tensor", m_tile[g])])
                    else:
                        m_evac[g] = em.op("vector",
                                          lambda e, bank=bank: e.tensor_copy(
                                              out=kv_out[:, bank],
                                              in_=PA[:, bank * 512:bank * 512 + 512]),
                                          waits=[("tensor", m_tile[g]),
                                                 ("q_akv%d" % (g % 2), m_store.get(g - 2, 0))])
                        if kind == "K":
                            def mk_st(e, a_=a_, tc=tc, bank=bank):
                                return e.dma_start(
                                    out=k_hbm[a_ * 128:(a_ + 1) * 128, tc * 512:(tc + 1) * 512],
                                    in_=kv_out[:, bank])
                        else:
                            def mk_st(e, a_=a_, b_=b_, tc=tc, bank=bank):
                                return e.dma_start(
                                    out=v_hbm[tc * 512 + a_ * 128: tc * 512 + (a_ + 1) * 128,
                                              b_ * 512:(b_ + 1) * 512],
                                    in_=kv_out[:, bank])
                        m_store[g] = em.op("sync", mk_st,
                                           waits=[("vector", m_evac[g])],
                                           dma="akv%d" % (g % 2))
                    g += 1
        NT = g
        m_p1_t = em.cnt["tensor"]
        m_p1_v = em.cnt["vector"]
        m_p1_g = em.cnt["gpsimd"]

        m_vtone = em.op("vector", lambda e: e.memset(vt[:, :, :, 64:65], 1.0),
                        waits=[("tensor", m_p1_t)])

        pend_rep = [None]

        def flush_rep():
            if pend_rep[0] is not None:
                fn, w_, key = pend_rep[0]
                m_rep[key] = em.op("tensor", fn, waits=w_)
                pend_rep[0] = None

        m_sc = {}
        m_exp = {}
        m_av = {}
        m_evA = {}
        m_rcp = {}
        m_rep = {}
        m_on = {}
        m_pref = {}
        for h in range(H):
            pw = [("tensor", m_sc.get((h - 2, 1, 31), 0)),
                  ("tensor", m_av.get((h - 2, 1, 31), 0))]
            if h < 2:
                pw += [("tensor", m_p1_t), ("vector", m_vtone),
                       ("q_akv0", em.qcnt("akv0")),
                       ("q_akv1", em.qcnt("akv1"))]
            em.op("sync",
                  lambda e, h=h: e.dma_start(out=kh[h % 2][:, :], in_=k_hbm[h * 64:(h + 1) * 64, :]),
                  waits=pw, dma="apf%d" % (h % 2))
            m_pref[h] = em.op("sync",
                              lambda e, h=h: e.dma_start(out=vt[:, h % 2, :, 0:64],
                                                         in_=vre[:, :, h * 64:(h + 1) * 64]),
                              dma="apf%d" % (h % 2))
            hb = (h % 2) * 64
            hc = h // 2
            for qh in range(2):
                for kc in range(32):
                    sidx = h * 64 + qh * 32 + kc
                    sb = sidx % 2
                    prev = (h * 64 + qh * 32 + kc - 2)
                    ph, pq, pk = prev // 64, (prev % 64) // 32, prev % 32
                    scw = [("scalar", m_exp.get((ph, pq, pk), 0))]
                    if kc == 0 and qh == 0:
                        scw.append(("q_apf%d" % (h % 2), m_pref[h]))

                    def mk_sc(e, h=h, qh=qh, kc=kc, sb=sb, hb=hb, hc=hc):
                        inst = None
                        for ha in range(2):
                            inst = e.matmul(
                                PA[:, sb * 1024 + ha * 512: sb * 1024 + (ha + 1) * 512],
                                kh[h % 2][:, kc * 128:(kc + 1) * 128],
                                q_sb[hb:hb + 64, hc, qh * 1024 + ha * 512: qh * 1024 + (ha + 1) * 512],
                                start=True, stop=True, skip_group_check=True)
                        return inst
                    m_sc[(h, qh, kc)] = em.op("tensor", mk_sc, waits=scw)
                    if kc == 2:
                        flush_rep()
                    p1 = (h * 64 + qh * 32 + kc - 1)
                    m_exp[(h, qh, kc)] = em.op(
                        "scalar",
                        lambda e, sb=sb: e.activation(
                            out=pt[:, sb, :], in_=PA[:, sb * 1024:(sb + 1) * 1024],
                            func=AF.Exp, scale=0.125),
                        waits=[("tensor", m_sc[(h, qh, kc)]),
                               ("tensor", m_av.get((p1 // 64, (p1 % 64) // 32, p1 % 32), 0))])
                    avw = [("scalar", m_exp[(h, qh, kc)])]
                    if kc == 0:
                        avw.append(("vector", m_on.get((h - 1, qh), 0)))

                    def mk_av(e, h=h, qh=qh, kc=kc, sb=sb):
                        inst = None
                        for ha in range(2):
                            inst = e.matmul(
                                PB[:, qh * 1024 + ha * 512: qh * 1024 + (ha + 1) * 512],
                                vt[:, h % 2, kc, :],
                                pt[:, sb, ha * 512:(ha + 1) * 512],
                                start=(kc == 0), stop=(kc == 31), skip_group_check=True)
                        return inst
                    m_av[(h, qh, kc)] = em.op("tensor", mk_av, waits=avw)
                m_evA[(h, qh)] = em.op(
                    "vector",
                    lambda e, qh=qh: e.tensor_copy(out=avt[:, qh, :],
                                                   in_=PB[:, qh * 1024:(qh + 1) * 1024]),
                    waits=[("tensor", m_av[(h, qh, 31)]),
                           ("scalar", m_rcp.get((h - 1, qh), 0))])
                em.op("scalar",
                      lambda e, qh=qh: e.activation(out=rcp_t[:, :], in_=avt[64:65, qh, :],
                                                    func=AF.Ln),
                      waits=[("vector", m_evA[(h, qh)]),
                             ("tensor", m_rep.get((h, qh - 1), m_rep.get((h - 1, 1), 0)))])
                m_rcp[(h, qh)] = em.op(
                    "scalar",
                    lambda e: e.activation(out=rcp_h[:, :], in_=rcp_t[:, :],
                                           func=AF.Exp, scale=-1.0))

                def mk_rep(e, qh=qh):
                    inst = None
                    for ha in range(2):
                        inst = e.matmul(
                            PB[0:64, qh * 1024 + ha * 512: qh * 1024 + (ha + 1) * 512],
                            cst[0:1, 4:68],
                            rcp_h[0:1, ha * 512:(ha + 1) * 512],
                            start=True, stop=True, skip_group_check=True)
                    return inst
                pend_rep[0] = (mk_rep, [("scalar", m_rcp[(h, qh)]),
                                        ("vector", m_evA[(h, qh)])], (h, qh))
                m_on[(h, qh)] = em.op(
                    "vector",
                    lambda e, qh=qh, hb=hb, hc=hc: e.tensor_tensor(
                        out=onorm[hb:hb + 64, hc, qh * 1024:(qh + 1) * 1024],
                        in0=avt[0:64, qh, :], in1=PB[0:64, qh * 1024:(qh + 1) * 1024],
                        op=ALU.mult),
                    waits=[("tensor", (lambda k=(h, qh): m_rep[k]))])
        flush_rep()

        m_p2_t = em.cnt["tensor"]
        m_p2_v = em.cnt["vector"]
        m_wo_cast = 0
        for hf in range(2):
            m_ld = em.op("sync",
                         lambda e, hf=hf: e.dma_start(
                             out=wo_stage,
                             in_=dW["Wo"].rearrange("(c p) o -> p c o", p=128)[:, hf * 4:hf * 4 + 4, :]),
                         waits=[("tensor", m_p2_t), ("vector", m_p2_v),
                                ("vector", m_wo_cast)], dma="ald")
            m_wo_cast = em.op("vector",
                              lambda e, hf=hf: e.tensor_copy(
                                  out=wo_sb[:, hf * 4:hf * 4 + 4, :], in_=wo_stage),
                              waits=[("q_ald", m_ld)])
        m_wot = {}
        m_res = {}
        m_sq = {}
        m_stat = {}
        m_lnfree = 0
        m_xq = {}
        m_out = 0
        for j in range(32):
            qc, oc = divmod(j, 8)
            m_xq[j] = em.op("sync",
                            lambda e, qc=qc, oc=oc, j=j: e.dma_start(
                                out=xq_st[:, j % 2, :], in_=xqre[:, oc, qc * 512:(qc + 1) * 512]),
                            waits=[("vector", m_res.get(j - 2, 0)),
                                   ("vector", m_wo_cast if j == 0 else 0)],
                            dma="axq%d" % (j % 2))
            m_wot[j] = em.op("tensor",
                             lambda e, qc=qc, oc=oc, j=j: [
                                 e.matmul(PA[:, (j % 2) * 512:(j % 2) * 512 + 512],
                                          wo_sb[:, kc, oc * 128:(oc + 1) * 128],
                                          onorm[:, kc, qc * 512:(qc + 1) * 512],
                                          start=(kc == 0), stop=(kc == 7))
                                 for kc in range(8)][-1],
                             waits=[("vector", m_wo_cast),
                                    ("vector", m_res.get(j - 2, 0))])
            em.op("vector",
                  lambda e, qc=qc, oc=oc, j=j: e.tensor_scalar(
                      out=resid[:, oc, qc * 512:(qc + 1) * 512],
                      in0=PA[:, (j % 2) * 512:(j % 2) * 512 + 512],
                      scalar1=bo_sb[:, oc:oc + 1], scalar2=None, op0=ALU.add),
                  waits=[("tensor", m_wot[j]), ("q_axq%d" % (j % 2), m_xq[j])])
            m_res[j] = em.op("vector",
                             lambda e, qc=qc, oc=oc, j=j: e.tensor_tensor(
                                 out=resid[:, oc, qc * 512:(qc + 1) * 512],
                                 in0=resid[:, oc, qc * 512:(qc + 1) * 512],
                                 in1=xq_st[:, j % 2, :], op=ALU.add))
            m_sq[j] = em.op("scalar",
                            lambda e, qc=qc, oc=oc: e.activation(
                                out=sqt, in_=resid[:, oc, qc * 512:(qc + 1) * 512],
                                func=AF.Square),
                            waits=[("vector", m_res[j]),
                                   ("tensor", m_stat.get(j - 1, 0))])

            def mk_stat(e, qc=qc, oc=oc):
                e.matmul(PB[:, 0:512], stp[:, 0:65],
                         resid[:, oc, qc * 512:(qc + 1) * 512],
                         start=(oc == 0), stop=False, skip_group_check=True)
                return e.matmul(PB[:, 0:512], stp[:, 65:130], sqt,
                                start=False, stop=(oc == 7), skip_group_check=True)
            m_stat[j] = em.op("tensor", mk_stat,
                              waits=[("scalar", m_sq[j]),
                                     ("vector", m_lnfree if oc == 0 else 0)])
            if oc == 7:
                m1 = em.op("vector",
                           lambda e: e.tensor_scalar(out=mrow, in0=PB[0:1, 0:512],
                                                     scalar1=1.0 / D, scalar2=None,
                                                     op0=ALU.mult),
                           waits=[("tensor", m_stat[j])])
                m_lnfree = em.op("vector",
                                 lambda e: e.tensor_scalar(out=s2row, in0=PB[64:65, 0:512],
                                                           scalar1=1.0 / D, scalar2=None,
                                                           op0=ALU.mult))
                m2 = em.op("scalar", lambda e: e.activation(out=msq, in_=mrow, func=AF.Square),
                           waits=[("vector", m_lnfree)])
                m3 = em.op("vector",
                           lambda e: e.tensor_tensor(out=varr, in0=s2row, in1=msq,
                                                     op=ALU.subtract),
                           waits=[("scalar", m2)])
                m4 = em.op("scalar",
                           lambda e: e.activation(out=msq, in_=varr, func=AF.Ln,
                                                  bias=cst[0:1, 1:2]),
                           waits=[("vector", m3)])
                m5 = em.op("scalar",
                           lambda e: e.activation(out=rstd_t, in_=msq, func=AF.Exp,
                                                  scale=-0.5))
                m6 = em.op("vector",
                           lambda e: e.tensor_tensor(out=mrs_t, in0=mrow, in1=rstd_t,
                                                     op=ALU.mult),
                           waits=[("scalar", m5)])
                m7 = em.op("sync", lambda e: e.dma_start(out=s_hbm[0, :], in_=rstd_t[0:1, :]),
                           waits=[("vector", m6)], dma="lnb")
                m7 = em.op("sync", lambda e: e.dma_start(out=s_hbm[1, :], in_=mrs_t[0:1, :]),
                           dma="lnb")
                m8 = em.op("sync",
                           lambda e: e.dma_start(out=lnrep[:, 0, :],
                                                 in_=s_hbm[0:1, :].broadcast_to([128, 512])),
                           waits=[("q_lnb", m7)], dma="lnb")
                m8 = em.op("sync",
                           lambda e: e.dma_start(out=lnrep[:, 1, :],
                                                 in_=s_hbm[1:2, :].broadcast_to([128, 512])),
                           dma="lnb")
                for oc2 in range(8):
                    em.op("vector",
                          lambda e, qc=qc, oc2=oc2: e.tensor_tensor(
                              out=resid[:, oc2, qc * 512:(qc + 1) * 512],
                              in0=resid[:, oc2, qc * 512:(qc + 1) * 512],
                              in1=lnrep[:, 0, :], op=ALU.mult),
                          waits=[("q_lnb", m8)])
                    m9 = em.op("vector",
                               lambda e, qc=qc, oc2=oc2: e.tensor_tensor(
                                   out=resid[:, oc2, qc * 512:(qc + 1) * 512],
                                   in0=resid[:, oc2, qc * 512:(qc + 1) * 512],
                                   in1=lnrep[:, 1, :], op=ALU.subtract))
                    if not fused:
                        m_out = em.op("sync",
                                      lambda e, qc=qc, oc2=oc2: e.dma_start(
                                          out=dout.rearrange("(c p) t -> p c t", p=128)[
                                              :, oc2, qc * 512:(qc + 1) * 512],
                                          in_=resid[:, oc2, qc * 512:(qc + 1) * 512]),
                                      waits=[("vector", m9)], dma="out")

        if not fused:
            em.wait("sync", "q_out", m_out)
        if fused:
            emit_A_x1tm(nc, em, ar, P, DR)


def emit_A_x1tm(nc, em, ar, P, DR):
    resid = ar.view(16704, [128, NCH, TOK], F32)
    idf = ar.view(44912, [128, 128], F32)
    stage = ar.view(45040, [128, NCH, 128], F32)
    PA = P["PA"]
    em.barrier()
    m_idf = em.op("sync", lambda e: e.dma_start(out=idf, in_=DR["identf"]), dma="atm")
    m_evd = {}
    m_st = 0
    for tt in range(16):
        for oc in range(NCH):
            g_ = tt * NCH + oc
            m_tp = em.op("tensor",
                         lambda e, tt=tt, oc=oc: e.transpose(
                             PA[:, (oc % 2) * 512:(oc % 2) * 512 + 128],
                             resid[:, oc, tt * 128:(tt + 1) * 128], idf),
                         waits=[("q_atm", m_idf if (tt == 0 and oc == 0) else 0),
                                ("vector", m_evd.get(g_ - 2, 0))])
            m_evd[g_] = em.op("vector",
                              lambda e, oc=oc: e.tensor_copy(
                                  out=stage[:, oc, :], in_=PA[:, (oc % 2) * 512:(oc % 2) * 512 + 128]),
                              waits=[("tensor", m_tp), ("q_atm", m_st if oc == 0 else 0)])
        m_ev = m_evd[tt * NCH + NCH - 1]
        m_st = em.op("sync",
                     lambda e, tt=tt: e.dma_start(
                         out=DR["x1_tm"][tt * 128:(tt + 1) * 128, :],
                         in_=stage.rearrange("p c t -> p (c t)")),
                     waits=[("vector", m_ev)], dma="atm")
    em.wait("sync", "q_atm", m_st)


def build_launch_A():
    nc = bass.Bass("TRN2", target_bir_lowering=False)
    DR = mk_dr_A(nc)
    es = contextlib.ExitStack()
    with es:
        ar = es.enter_context(Arena(nc))
        P = {"PA": es.enter_context(nc.psum_tensor("PA", [128, 2048], F32))[:, :],
             "PB": es.enter_context(nc.psum_tensor("PB", [65, 2048], F32))[:, :]}
        em = Emitter(nc, es)
        em.prefix = "A"
        blk = es.enter_context(nc.Block())
        emit_A(nc, em, ar, P, DR)
        em.build(blk)
    return nc


def host_launch_A(inputs):
    x = np.asarray(inputs["L0_attn_x"])
    stpat = np.zeros((128, 130), np.float32)
    stpat[:, 0] = 1.0
    stpat[:, 129] = 1.0
    consts = np.zeros((128, 68), np.float32)
    consts[:, 0] = 1.0
    consts[:, 1] = EPS
    consts[:, 4:68] = 1.0
    in_maps = []
    for c in range(8):
        b, hf = divmod(c, 2)
        xb = _fm(x[b])
        in_maps.append({
            "x_fm": xb,
            "xq_fm": np.ascontiguousarray(xb[:, hf * TOK:(hf + 1) * TOK]),
            "Wq": np.asarray(inputs["L0_attn_Wq"]),
            "Wk": np.asarray(inputs["L0_attn_Wk"]),
            "Wv": np.asarray(inputs["L0_attn_Wv"]),
            "Wo": np.asarray(inputs["L0_attn_Wo"]),
            "bo": np.ascontiguousarray(np.asarray(inputs["L0_attn_bo"]).reshape(NCH, 128).T),
            "stpat": stpat,
            "consts": consts,
        })
    nc = build_launch_A()
    res = run_bass_kernel_spmd(nc, in_maps, core_ids=list(range(8)))
    x1 = np.empty((B, S, D), np.float32)
    for c in range(8):
        b, hf = divmod(c, 2)
        x1[b, hf * TOK:(hf + 1) * TOK, :] = res.results[c]["x1_fm"].T
    return x1


C23 = float(2 ** 23)
CR = float(3 * 2 ** 22)


def emit_fp4_quant(em, in_ap, out_ap, P_, W_, scr, first_waits=()):
    ve = "vector"
    nb = W_ // 16
    em.op(ve, lambda e: e.tensor_reduce(
        out=scr["bm"], in_=in_ap.rearrange("p (g k) -> p g k", k=16),
        axis=AX.X, op=ALU.max, apply_absolute_value=True), waits=list(first_waits))
    em.op(ve, lambda e: e.tensor_single_scalar(out=scr["bm"], in_=scr["bm"],
                                               scalar=6e-12, op=ALU.max))
    em.op(ve, lambda e: e.reciprocal(out=scr["rc"], in_=scr["bm"]))
    em.op(ve, lambda e: e.tensor_single_scalar(out=scr["rc"], in_=scr["rc"],
                                               scalar=6.0, op=ALU.mult))
    em.op(ve, lambda e: e.tensor_single_scalar(out=scr["sg"], in_=in_ap,
                                               scalar=0.0, op=ALU.is_gt))
    em.op(ve, lambda e: e.tensor_scalar(out=scr["sg"], in0=scr["sg"], scalar1=2.0,
                                        scalar2=-1.0, op0=ALU.mult, op1=ALU.add))
    em.op(ve, lambda e: e.tensor_tensor(
        out=scr["a"].rearrange("p (g k) -> p g k", k=16),
        in0=in_ap.rearrange("p (g k) -> p g k", k=16),
        in1=scr["rc"].unsqueeze(2).broadcast_to([P_, nb, 16]),
        op=ALU.mult))
    em.op(ve, lambda e: e.tensor_tensor(out=scr["a"], in0=scr["a"], in1=scr["sg"],
                                        op=ALU.mult))
    em.op(ve, lambda e: e.tensor_single_scalar(out=scr["r1"], in_=scr["a"],
                                               scalar=2.0, op=ALU.min))
    em.op(ve, lambda e: e.tensor_scalar(out=scr["r1"], in0=scr["r1"], scalar1=2.0,
                                        scalar2=C23, op0=ALU.mult, op1=ALU.add))
    em.op(ve, lambda e: e.tensor_scalar(out=scr["r1"], in0=scr["r1"], scalar1=C23,
                                        scalar2=0.5, op0=ALU.subtract, op1=ALU.mult))
    em.op(ve, lambda e: e.tensor_scalar(out=scr["q"], in0=scr["a"], scalar1=2.0,
                                        scalar2=4.0, op0=ALU.max, op1=ALU.min))
    em.op(ve, lambda e: e.tensor_single_scalar(out=scr["q"], in_=scr["q"],
                                               scalar=C23, op=ALU.add))
    em.op(ve, lambda e: e.tensor_single_scalar(out=scr["q"], in_=scr["q"],
                                               scalar=C23 + 6.0, op=ALU.subtract))
    em.op(ve, lambda e: e.tensor_tensor(out=scr["q"], in0=scr["q"], in1=scr["r1"],
                                        op=ALU.add))
    em.op(ve, lambda e: e.tensor_scalar(out=scr["r1"], in0=scr["a"], scalar1=4.0,
                                        scalar2=6.0, op0=ALU.max, op1=ALU.min))
    em.op(ve, lambda e: e.tensor_scalar(out=scr["r1"], in0=scr["r1"], scalar1=0.5,
                                        scalar2=C23, op0=ALU.mult, op1=ALU.add))
    em.op(ve, lambda e: e.tensor_scalar(out=scr["r1"], in0=scr["r1"], scalar1=C23,
                                        scalar2=2.0, op0=ALU.subtract, op1=ALU.mult))
    em.op(ve, lambda e: e.tensor_tensor(out=scr["q"], in0=scr["q"], in1=scr["r1"],
                                        op=ALU.add))
    em.op(ve, lambda e: e.tensor_tensor(out=scr["q"], in0=scr["q"], in1=scr["sg"],
                                        op=ALU.mult))
    em.op(ve, lambda e: e.tensor_single_scalar(out=scr["q"], in_=scr["q"],
                                               scalar=1.0 / 6.0, op=ALU.mult))
    return em.op(ve, lambda e: e.tensor_tensor(
        out=out_ap.rearrange("p (g k) -> p g k", k=16),
        in0=scr["q"].rearrange("p (g k) -> p g k", k=16),
        in1=scr["bm"].unsqueeze(2).broadcast_to([P_, nb, 16]),
        op=ALU.mult))


def mk_dr_W(nc, fused=False):
    ki = "ExternalInput"
    ko = "Internal" if fused else "ExternalOutput"
    return {
        "wg": nc.dram_tensor("wg", [384, D], F32, kind=ki).ap(),
        "wu": nc.dram_tensor("wu", [384, D], F32, kind=ki).ap(),
        "wd": nc.dram_tensor("wd", [128, 2816], F32, kind=ki).ap(),
        "wgq": nc.dram_tensor("wgq", [384, D], BF16, kind=ko).ap(),
        "wuq": nc.dram_tensor("wuq", [384, D], BF16, kind=ko).ap(),
        "wdq": nc.dram_tensor("wdq", [128, 2816], BF16, kind=ko).ap(),
    }


def emit_W(nc, em, ar, P, DR, fused=False):
    dg, du, dd = DR["wg"], DR["wu"], DR["wd"]
    if not fused:
        og, ou, od = DR["wgq"], DR["wuq"], DR["wdq"]
    if True:
        xin = ar.view(0, [128, 3, D], F32)
        xout = ar.view(3072, [128, 3, D], BF16)
        scr = {
            "bm": ar.view(4608, [128, 3 * D // 16], F32),
            "rc": ar.view(4800, [128, 3 * D // 16], F32),
            "a": ar.view(4992, [128, 3, D], F32),
            "r1": ar.view(8064, [128, 3, D], F32),
            "q": ar.view(11136, [128, 3, D], F32),
            "sg": ar.view(14208, [128, 3, D], F32),
        }
        din = ar.view(17280, [128, 2816], F32)
        dosb = ar.view(20096, [128, 2816], BF16)
        scrd = {
            "bm": ar.view(21504, [128, 176], F32),
            "rc": ar.view(21680, [128, 176], F32),
            "a": ar.view(21856, [128, 2816], F32),
            "r1": ar.view(24672, [128, 2816], F32),
            "q": ar.view(27488, [128, 2816], F32),
            "sg": ar.view(30304, [128, 2816], F32),
        }

        if fused:
            xout2 = ar.view(33120, [128, 3, D], BF16)
            idb = ar.view(34656, [128, 128], BF16)
            wgT = ar.view(34720, [128, 8, 384], BF16)
            wuT = ar.view(36256, [128, 8, 384], BF16)
            wdT = ar.view(37792, [128, 22, 128], BF16)
            PT = P["WT"].bitcast(BF16)
            cc = DR["cc_src"]
            m_idb = em.op("sync", lambda e: e.dma_start(out=idb, in_=DR["identb"]),
                          dma="wld")

        m_q = 0
        m_store = 0
        m_tr_last = 0
        mats = ((dg, True, 0), (du, True, 1), (dd, False, 2))
        for (src_d, w3, mi) in mats:
            xo = xout if (not fused or mi == 0) else (xout2 if mi == 1 else dosb)
            if w3:
                m_ld = em.op("sync", lambda e, s=src_d: e.dma_start(
                    out=xin, in_=s.rearrange("(c p) d -> p c d", p=128)),
                    waits=[("vector", m_q)], dma="wld")
                flat_in = xin.rearrange("p c d -> p (c d)")
                flat_out = xo.rearrange("p c d -> p (c d)")
                scr_use = {k: (v.rearrange("p c d -> p (c d)") if k in ("a", "r1", "q", "sg") else v)
                           for k, v in scr.items()}
                m_q = emit_fp4_quant(em, flat_in, flat_out, 128, 3 * D, scr_use,
                                     first_waits=[("q_wld", m_ld),
                                                  ("q_wst", m_store)])
            else:
                m_ld = em.op("sync", lambda e, s=src_d: e.dma_start(out=din, in_=s),
                             waits=[("vector", m_q)], dma="wld")
                m_q = emit_fp4_quant(em, din, dosb, 128, 2816, scrd,
                                     first_waits=[("q_wld", m_ld),
                                                  ("q_wst", m_store)])
            if not fused:
                out_d = (DR["wgq"], DR["wuq"], DR["wdq"])[mi]
                if w3:
                    m_store = em.op("sync", lambda e, o=out_d: e.dma_start(
                        out=o.rearrange("(c p) d -> p c d", p=128), in_=xout),
                        waits=[("vector", m_q)], dma="wst")
                else:
                    m_store = em.op("sync", lambda e, o=out_d: e.dma_start(out=o, in_=dosb),
                                    waits=[("vector", m_q)], dma="wst")
                continue
            wT = (wgT, wuT, wdT)[mi]
            m_ev = {}
            if w3:
                tiles = [(c, k) for c in range(3) for k in range(8)]
            else:
                tiles = [(ic,) for ic in range(22)]
            for ti, t in enumerate(tiles):
                sl = ti % 2

                def mk_tp(e, t=t, sl=sl, xo=xo):
                    if len(t) == 2:
                        src = xo[:, t[0], t[1] * 128:(t[1] + 1) * 128]
                    else:
                        src = xo[:, t[0] * 128:(t[0] + 1) * 128]
                    return e.transpose(PT[:, sl * 1024:sl * 1024 + 128], src, idb)
                tw = [("vector", m_q if ti == 0 else 0),
                      ("vector", m_ev.get(ti - 2, 0)),
                      ("vector", m_tr_last if ti < 2 else 0)]
                if mi == 0 and ti == 0:
                    tw.append(("q_wld", m_idb))
                m_t = em.op("tensor", mk_tp, waits=tw)

                def mk_ev(e, t=t, sl=sl, wT=wT):
                    if len(t) == 2:
                        dst = wT[:, t[1], t[0] * 128:(t[0] + 1) * 128]
                    else:
                        dst = wT[:, t[0], :]
                    return e.tensor_copy(out=dst, in_=PT[:, sl * 1024:sl * 1024 + 128])
                m_ev[ti] = em.op("vector", mk_ev, waits=[("tensor", m_t)])
            m_tr_last = m_ev[len(tiles) - 1]
            if w3:
                ccd = cc[mi * 393216:(mi + 1) * 393216].rearrange(
                    "(dk p rw) -> p dk rw", dk=8, p=128, rw=384)
                srcv = wT.rearrange("p dk rw -> p dk rw")
            else:
                ccd = cc[786432:786432 + 360448].rearrange(
                    "(ic p dd) -> p ic dd", ic=22, p=128, dd=128)
                srcv = wT
            m_store = em.op("sync",
                            lambda e, ccd=ccd, srcv=srcv: e.dma_start(out=ccd, in_=srcv),
                            waits=[("vector", m_tr_last)], dma="wst")
        em.wait("sync", "q_wst", em.qcnt("wst"))


def build_launch_W():
    nc = bass.Bass("TRN2", target_bir_lowering=False)
    DR = mk_dr_W(nc)
    es = contextlib.ExitStack()
    with es:
        ar = es.enter_context(Arena(nc, cols=40000))
        em = Emitter(nc, es)
        em.prefix = "W"
        blk = es.enter_context(nc.Block())
        emit_W(nc, em, ar, {}, DR)
        em.build(blk)
    return nc


def host_launch_W(inputs):
    wg = np.asarray(inputs["L0_ffn_w_gate"])
    wu = np.asarray(inputs["L0_ffn_w_up"])
    wd = np.asarray(inputs["L0_ffn_w_down"])
    wgp = np.zeros((3072, D), np.float32); wgp[:2816] = wg
    wup = np.zeros((3072, D), np.float32); wup[:2816] = wu
    in_maps = []
    for c in range(8):
        in_maps.append({
            "wg": np.ascontiguousarray(wgp[c * 384:(c + 1) * 384]),
            "wu": np.ascontiguousarray(wup[c * 384:(c + 1) * 384]),
            "wd": np.ascontiguousarray(wd[c * 128:(c + 1) * 128]),
        })
    nc = build_launch_W()
    res = run_bass_kernel_spmd(nc, in_maps, core_ids=list(range(8)))
    wgq = np.concatenate([res.results[c]["wgq"] for c in range(8)])[:2816]
    wuq = np.concatenate([res.results[c]["wuq"] for c in range(8)])[:2816]
    wdq = np.concatenate([res.results[c]["wdq"] for c in range(8)])
    return wgq, wuq, wdq


def emit_fp4_quant_signed(em, in_ap, out_ap, P_, W_, scr, first_waits=()):
    ve = "vector"
    nb = W_ // 16
    em.op(ve, lambda e: e.tensor_reduce(
        out=scr["bm"], in_=in_ap.rearrange("p (g k) -> p g k", k=16),
        axis=AX.X, op=ALU.max, apply_absolute_value=True), waits=list(first_waits))
    em.op(ve, lambda e: e.tensor_single_scalar(out=scr["bm"], in_=scr["bm"],
                                               scalar=6e-12, op=ALU.max))
    em.op(ve, lambda e: e.reciprocal(out=scr["rc"], in_=scr["bm"]))
    em.op(ve, lambda e: e.tensor_single_scalar(out=scr["rc"], in_=scr["rc"],
                                               scalar=6.0, op=ALU.mult))
    em.op(ve, lambda e: e.tensor_tensor(
        out=scr["a"].rearrange("p (g k) -> p g k", k=16),
        in0=in_ap.rearrange("p (g k) -> p g k", k=16),
        in1=scr["rc"].unsqueeze(2).broadcast_to([P_, nb, 16]), op=ALU.mult))
    a, r1, q = scr["a"], scr["r1"], scr["q"]
    em.op(ve, lambda e: e.tensor_scalar(out=q, in0=a, scalar1=-2.0, scalar2=2.0,
                                        op0=ALU.max, op1=ALU.min))
    em.op(ve, lambda e: e.tensor_scalar(out=q, in0=q, scalar1=2.0, scalar2=CR,
                                        op0=ALU.mult, op1=ALU.add))
    em.op(ve, lambda e: e.tensor_scalar(out=q, in0=q, scalar1=CR, scalar2=0.5,
                                        op0=ALU.subtract, op1=ALU.mult))
    em.op(ve, lambda e: e.tensor_scalar(out=r1, in0=a, scalar1=2.0, scalar2=4.0,
                                        op0=ALU.max, op1=ALU.min))
    em.op(ve, lambda e: e.tensor_single_scalar(out=r1, in_=r1, scalar=CR, op=ALU.add))
    em.op(ve, lambda e: e.tensor_single_scalar(out=r1, in_=r1, scalar=CR + 2.0,
                                               op=ALU.subtract))
    em.op(ve, lambda e: e.tensor_tensor(out=q, in0=q, in1=r1, op=ALU.add))
    em.op(ve, lambda e: e.tensor_scalar(out=r1, in0=a, scalar1=-4.0, scalar2=-2.0,
                                        op0=ALU.max, op1=ALU.min))
    em.op(ve, lambda e: e.tensor_scalar(out=r1, in0=r1, scalar1=CR, scalar2=CR - 2.0,
                                        op0=ALU.add, op1=ALU.subtract))
    em.op(ve, lambda e: e.tensor_tensor(out=q, in0=q, in1=r1, op=ALU.add))
    em.op(ve, lambda e: e.tensor_scalar(out=r1, in0=a, scalar1=4.0, scalar2=6.0,
                                        op0=ALU.max, op1=ALU.min))
    em.op(ve, lambda e: e.tensor_scalar(out=r1, in0=r1, scalar1=0.5, scalar2=CR,
                                        op0=ALU.mult, op1=ALU.add))
    em.op(ve, lambda e: e.tensor_scalar(out=r1, in0=r1, scalar1=CR, scalar2=2.0,
                                        op0=ALU.subtract, op1=ALU.mult))
    em.op(ve, lambda e: e.tensor_single_scalar(out=q, in_=q, scalar=4.0, op=ALU.subtract))
    em.op(ve, lambda e: e.tensor_tensor(out=q, in0=q, in1=r1, op=ALU.add))
    em.op(ve, lambda e: e.tensor_scalar(out=r1, in0=a, scalar1=-6.0, scalar2=-4.0,
                                        op0=ALU.max, op1=ALU.min))
    em.op(ve, lambda e: e.tensor_scalar(out=r1, in0=r1, scalar1=0.5, scalar2=CR,
                                        op0=ALU.mult, op1=ALU.add))
    em.op(ve, lambda e: e.tensor_scalar(out=r1, in0=r1, scalar1=CR, scalar2=2.0,
                                        op0=ALU.subtract, op1=ALU.mult))
    em.op(ve, lambda e: e.tensor_single_scalar(out=q, in_=q, scalar=-4.0, op=ALU.subtract))
    em.op(ve, lambda e: e.tensor_tensor(out=q, in0=q, in1=r1, op=ALU.add))
    em.op(ve, lambda e: e.tensor_single_scalar(out=q, in_=q, scalar=1.0 / 6.0,
                                               op=ALU.mult))
    return em.op(ve, lambda e: e.tensor_tensor(
        out=out_ap.rearrange("p (g k) -> p g k", k=16),
        in0=q.rearrange("p (g k) -> p g k", k=16),
        in1=scr["bm"].unsqueeze(2).broadcast_to([P_, nb, 16]), op=ALU.mult))


IFF = 2816
NIC = [(0, 512), (512, 512), (1024, 512), (1536, 512), (2048, 512), (2560, 256)]


def mk_dr_B(nc, fused=False):
    ki = "ExternalInput"
    ko = "Internal" if fused else "ExternalOutput"
    kw = "Internal" if fused else ki
    d = {}
    d["x1_tm"] = nc.dram_tensor("x1_tm", [TOK, D], F32, kind=kw).ap()
    d["wgq_T"] = nc.dram_tensor("wgq_T", [D, IFF], BF16, kind=kw).ap()
    d["wuq_T"] = nc.dram_tensor("wuq_T", [D, IFF], BF16, kind=kw).ap()
    d["wdq_T"] = nc.dram_tensor("wdq_T", [IFF, D], BF16, kind=kw).ap()
    d["identb"] = nc.dram_tensor("identb", [128, 128], BF16, kind=ki).ap()
    d["x2_tm"] = nc.dram_tensor("x2_tm", [TOK, D], F32, kind=ko).ap()
    return d


def emit_B(nc, em, ar, P, DR, ntiles=16, phases=9, fused=False):
    dx = DR["x1_tm"]
    dib, dcst = DR["identb"], DR["consts"]
    if not fused:
        dwg, dwu, dwd = DR["wgq_T"], DR["wuq_T"], DR["wdq_T"]
        dout = DR["x2_tm"]
    if True:
        wg_sb = ar.view(0, [128, NCH, IFF], BF16)
        wu_sb = ar.view(11264, [128, NCH, IFF], BF16)
        wd_sb = ar.view(22528, [128, 22, D], BF16)
        cst = ar.view(33792, [128, 68], F32)
        idb = ar.view(33860, [128, 128], BF16)
        x1t = ar.view(33924, [128, 2, D], F32)
        xq_tm = ar.view(35972, [128, D], BF16)
        xq_fm = ar.view(36484, [128, NCH, 128], BF16)
        hbuf = ar.view(36996, [128, IFF], F32)
        hq_tm = ar.view(39812, [128, IFF], BF16)
        hq_fm = ar.view(41220, [128, 22, 128], BF16)
        sg_sil = ar.view(42628, [128, 512], F32)
        resid = ar.view(43140, [128, D], F32)
        sqb = ar.view(44164, [128, D], F32)
        mcol = ar.view(45188, [128, 1], F32)
        vcol = ar.view(45189, [128, 1], F32)
        rcol = ar.view(45190, [128, 1], F32)
        scr_r1 = ar.view(45191, [128, 1408], F32)
        scr_q = ar.view(46599, [128, 1408], F32)
        scr_bm = ar.view(48007, [128, 88], F32)
        scr_rc = ar.view(48095, [128, 88], F32)

        PG, PU, PD, PT = P["PG"], P["PU"], P["PD"], P["PT"]

        em.op("sync", lambda e: e.dma_start(out=cst, in_=dcst),
              waits=[("gpsimd", DR.get("m_wag", 0))] if fused else (), dma="bld")
        em.op("sync", lambda e: e.dma_start(out=idb, in_=dib), dma="bld")
        if fused:
            idf = ar.view(48183, [128, 128], F32)
            x2stg = ar.view(48311, [128, 2, 128], F32)
            x2stgb = ar.view(48567, [128, 2, 128], BF16)
            em.op("sync", lambda e: e.dma_start(out=idf, in_=DR["identf"]), dma="bld")
            ccw = DR["cc_w"]
            ccw_g = ccw[:, 0:393216].rearrange("r (dk p rw) -> r dk p rw",
                                               dk=8, p=128, rw=384)
            ccw_u = ccw[:, 393216:786432].rearrange("r (dk p rw) -> r dk p rw",
                                                    dk=8, p=128, rw=384)
            ccw_d = ccw[:, 786432:1146880].rearrange("r (ic p dd) -> r ic p dd",
                                                     ic=22, p=128, dd=128)
            for (srcv, dstsb) in ((ccw_g, wg_sb), (ccw_u, wu_sb)):
                for dk in range(8):
                    em.op("sync", lambda e, srcv=srcv, dstsb=dstsb, dk=dk: e.dma_start(
                        out=dstsb[:, dk, 0:2688].rearrange("p (r rw) -> p r rw", r=7),
                        in_=srcv[0:7, dk].rearrange("r p rw -> p r rw")), dma="bld")
                    em.op("sync", lambda e, srcv=srcv, dstsb=dstsb, dk=dk: e.dma_start(
                        out=dstsb[:, dk, 2688:2816],
                        in_=srcv[7, dk, :, 0:128]), dma="bld")
            for ic in range(22):
                em.op("sync", lambda e, ic=ic: e.dma_start(
                    out=wd_sb[:, ic, :].rearrange("p (r dd) -> p r dd", r=8),
                    in_=ccw_d[:, ic].rearrange("r p dd -> p r dd")), dma="bld")
            m_wld = em.qcnt("bld")
            dx2o = DR["x2_own"].rearrange("(c p) t -> p c t", p=128)
            dx2s = DR["x2s"].rearrange("(c p) t -> p c t", p=128)
        else:
            em.op("sync", lambda e: e.dma_start(
                out=wg_sb, in_=dwg.rearrange("(c p) i -> p c i", p=128)), dma="bld")
            em.op("sync", lambda e: e.dma_start(
                out=wu_sb, in_=dwu.rearrange("(c p) i -> p c i", p=128)), dma="bld")
            m_wld = em.op("sync", lambda e: e.dma_start(
                out=wd_sb, in_=dwd.rearrange("(c p) d -> p c d", p=128)), dma="bld")

        m_x1 = {}
        m_xq = {}
        m_xqf = {}
        m_gu = {}
        m_h = {}
        m_hq = {}
        m_hqf = {}
        m_down = {}
        m_res = {}
        m_out = 0
        m_sil = {}
        m_x2f = {}
        m_x2d = {}
        m_x2tp = {}
        for tt in range(ntiles):
            par = tt % 2
            m_x1[tt] = em.op("sync",
                             lambda e, tt=tt, par=par: e.dma_start(
                                 out=x1t[:, par, :], in_=dx[tt * 128:(tt + 1) * 128, :]),
                             waits=[("vector", m_res.get(tt - 2, 0)),
                                    ("q_bld", m_wld if tt == 0 else 0)],
                             dma="bx%d" % (tt % 2))
            scrx = {"bm": scr_bm[:, 0:64], "rc": scr_rc[:, 0:64],
                    "a": hbuf[:, 0:1024], "r1": scr_r1[:, 0:1024],
                    "q": scr_q[:, 0:1024]}
            m_xq[tt] = emit_fp4_quant_signed(
                em, x1t[:, par, :], xq_tm, 128, D, scrx,
                first_waits=[("q_bx%d" % (tt % 2), m_x1[tt]),
                             ("tensor", (lambda t=tt: m_gu[t - 1]) if tt >= 1 else 0),
                             ("scalar", (lambda t=tt: m_sil[(t - 1, 5)]) if tt >= 1 else 0)])
            if phases < 2:
                m_out = em.op("gpsimd",
                              lambda e, tt=tt: e.dma_start(
                                  out=dout[tt * 128:(tt + 1) * 128, 0:512],
                                  in_=xq_tm.bitcast(F32)),
                              waits=[("vector", m_xq[tt])], dma="out")
                continue
            for c in range(8):
                m_tp = em.op("tensor",
                             lambda e, c=c: e.transpose(
                                 PT[:, c % 2, 0:128], xq_tm[:, c * 128:(c + 1) * 128], idb),
                             waits=[("vector", m_xq[tt] if c == 0 else 0),
                                    ("vector", (lambda t=tt, c=c: m_xqf[(t, c - 2)]) if c >= 2 else 0),
                                    ("tensor", (lambda t=tt: m_gu[t - 1]) if (c == 0 and tt >= 1) else 0)])
                m_xqf[(tt, c)] = em.op("vector",
                                       lambda e, c=c: e.tensor_copy(
                                           out=xq_fm[:, c, :], in_=PT[:, c % 2, 0:128]),
                                       waits=[("tensor", m_tp)])
            if phases < 3:
                m_out = em.op("gpsimd",
                              lambda e, tt=tt: e.dma_start(
                                  out=dout[tt * 128:(tt + 1) * 128, 0:512],
                                  in_=xq_fm.rearrange("p c t -> p (c t)").bitcast(F32)[:, 0:512]),
                              waits=[("vector", m_xqf[(tt, 7)])], dma="out")
                continue
            for ici, (i0, iw) in enumerate(NIC):
                gp = ici % 2

                def mk_g(e, i0=i0, iw=iw, gp=gp):
                    inst = None
                    for c in range(8):
                        inst = e.matmul(PG[:, gp, 0:iw], xq_fm[:, c, :],
                                        wg_sb[:, c, i0:i0 + iw],
                                        start=(c == 0), stop=(c == 7))
                    return inst

                def mk_u(e, i0=i0, iw=iw, gp=gp):
                    inst = None
                    for c in range(8):
                        inst = e.matmul(PU[:, gp, 0:iw], xq_fm[:, c, :],
                                        wu_sb[:, c, i0:i0 + iw],
                                        start=(c == 0), stop=(c == 7))
                    return inst
                gw = [("vector", m_xqf[(tt, 7)])]
                if (tt, ici - 2) in m_sil:
                    gw.append(("scalar", (lambda k=(tt, ici - 2): m_sil[k])))
                    gw.append(("vector", (lambda k=(tt, ici - 2): m_h[k])))
                elif tt >= 1 and ici < 2:
                    gw.append(("scalar", (lambda k=(tt - 1, 4 + ici): m_sil[k])))
                    gw.append(("vector", (lambda k=(tt - 1, 4 + ici): m_h[k])))
                em.op("tensor", mk_g, waits=gw)
                m_gu[tt] = em.op("tensor", mk_u)
                m_sil[(tt, ici)] = em.op(
                    "scalar",
                    lambda e, i0=i0, iw=iw, gp=gp: e.activation(
                        out=sg_sil[:, 0:iw], in_=PG[:, gp, 0:iw], func=AF.Silu),
                    waits=[("tensor", m_gu[tt]),
                           ("vector", (lambda k=(tt, ici - 1): m_h[k]) if ici >= 1 else
                            ((lambda k=(tt - 1, 5): m_h[k]) if tt >= 1 else 0))])
                m_h[(tt, ici)] = em.op(
                    "vector",
                    lambda e, i0=i0, iw=iw, gp=gp: e.tensor_tensor(
                        out=hbuf[:, i0:i0 + iw], in0=sg_sil[:, 0:iw],
                        in1=PU[:, gp, 0:iw], op=ALU.mult),
                    waits=[("scalar", m_sil[(tt, ici)])])
            if phases < 4:
                m_out = em.op("gpsimd",
                              lambda e, tt=tt: e.dma_start(
                                  out=dout[tt * 128:(tt + 1) * 128, :],
                                  in_=hbuf[:, 0:1024]),
                              waits=[("vector", m_h[(tt, 5)])], dma="out")
                continue
            for hh in range(2):
                o0 = hh * 1408
                scrh = {"bm": scr_bm, "rc": scr_rc, "a": hbuf[:, o0:o0 + 1408],
                        "r1": scr_r1, "q": scr_q}
                m_hq[(tt, hh)] = emit_fp4_quant_signed(
                    em, hbuf[:, o0:o0 + 1408], hq_tm[:, o0:o0 + 1408], 128, 1408, scrh,
                    first_waits=[("vector", m_h[(tt, 5)])])
            if phases < 5:
                m_out = em.op("gpsimd",
                              lambda e, tt=tt: e.dma_start(
                                  out=dout[tt * 128:(tt + 1) * 128, 0:704],
                                  in_=hq_tm.bitcast(F32)),
                              waits=[("vector", m_hq[(tt, 1)])], dma="out")
                continue
            for c in range(22):
                m_tp = em.op("tensor",
                             lambda e, c=c: e.transpose(
                                 PT[:, c % 2, 0:128], hq_tm[:, c * 128:(c + 1) * 128], idb),
                             waits=[("vector", m_hq[(tt, 1)] if c == 0 else 0),
                                    ("vector", (lambda t=tt, c=c: m_hqf[(t, c - 2)]) if c >= 2 else 0),
                                    ("tensor", (lambda t=tt: m_down[t - 1]) if (c == 0 and tt >= 1) else 0)])
                m_hqf[(tt, c)] = em.op("vector",
                                       lambda e, c=c: e.tensor_copy(
                                           out=hq_fm[:, c, :], in_=PT[:, c % 2, 0:128]),
                                       waits=[("tensor", m_tp)])
            if phases < 6:
                m_out = em.op("gpsimd",
                              lambda e, tt=tt: e.dma_start(
                                  out=dout[tt * 128:(tt + 1) * 128, 0:704],
                                  in_=hq_fm.rearrange("p c t -> p (c t)").bitcast(F32)[:, 0:704]),
                              waits=[("vector", m_hqf[(tt, 21)])], dma="out")
                continue
            def mk_d(e):
                inst = None
                for c in range(22):
                    for half in range(2):
                        inst = e.matmul(PD[:, half * 512:(half + 1) * 512],
                                        hq_fm[:, c, :],
                                        wd_sb[:, c, half * 512:(half + 1) * 512],
                                        start=(c == 0), stop=(c == 21))
                return inst
            m_down[tt] = em.op("tensor", mk_d,
                               waits=[("vector", m_hqf[(tt, 21)]),
                                      ("vector", (m_x2f.get((tt - 1, 7), 0) if fused
                                                  else m_res.get(tt - 1, 0)))])
            m_r0 = em.op("vector",
                         lambda e, par=par: e.tensor_tensor(
                             out=resid, in0=x1t[:, par, :], in1=PD[:, :], op=ALU.add),
                         waits=[("tensor", m_down[tt])])
            em.op("vector", lambda e: e.tensor_reduce(out=mcol, in_=resid, axis=AX.X,
                                                      op=ALU.add))
            m_sq2 = em.op("scalar", lambda e: e.activation(out=sqb, in_=resid,
                                                           func=AF.Square),
                          waits=[("vector", m_r0), ("q_out", m_out),
                                 ("tensor", m_x2tp.get(tt - 1, 0) if fused else 0)])
            m_v1 = em.op("vector", lambda e: e.tensor_reduce(out=vcol, in_=sqb, axis=AX.X,
                                                             op=ALU.add),
                         waits=[("scalar", m_sq2)])
            em.op("vector", lambda e: e.tensor_single_scalar(out=mcol, in_=mcol,
                                                             scalar=1.0 / D, op=ALU.mult))
            m_v2 = em.op("vector", lambda e: e.tensor_single_scalar(out=vcol, in_=vcol,
                                                                    scalar=1.0 / D, op=ALU.mult))
            m_msq = em.op("scalar", lambda e: e.activation(out=rcol, in_=mcol,
                                                           func=AF.Square),
                          waits=[("vector", m_v2)])
            m_v3 = em.op("vector", lambda e: e.tensor_tensor(out=vcol, in0=vcol, in1=rcol,
                                                             op=ALU.subtract),
                         waits=[("scalar", m_msq)])
            m_ln = em.op("scalar", lambda e: e.activation(out=rcol, in_=vcol, func=AF.Ln,
                                                          bias=cst[:, 1:2]),
                         waits=[("vector", m_v3)])
            m_ln = em.op("scalar", lambda e: e.activation(out=rcol, in_=rcol, func=AF.Exp,
                                                          scale=-0.5))
            m_res[tt] = em.op("vector",
                              lambda e: e.tensor_scalar(
                                  out=sqb, in0=resid, scalar1=mcol[:, 0:1],
                                  scalar2=rcol[:, 0:1], op0=ALU.subtract, op1=ALU.mult),
                              waits=[("scalar", m_ln)])
            if not fused:
                m_out = em.op("gpsimd",
                              lambda e, tt=tt: e.dma_start(
                                  out=dout[tt * 128:(tt + 1) * 128, :], in_=sqb),
                              waits=[("vector", m_res[tt])], dma="out")
                continue
            PX2 = P["PD"]
            for oc in range(8):
                sl = oc % 2
                m_tp2 = em.op("tensor",
                              lambda e, oc=oc, sl=sl: e.transpose(
                                  PX2[:, sl * 512:sl * 512 + 128],
                                  sqb[:, oc * 128:(oc + 1) * 128], idf),
                              waits=[("vector", m_res[tt] if oc == 0 else 0),
                                     ("vector", m_x2f.get((tt, oc - 2), 0))])
                m_f = em.op("vector",
                            lambda e, sl=sl: e.tensor_copy(out=x2stg[:, sl, :],
                                                           in_=PX2[:, sl * 512:sl * 512 + 128]),
                            waits=[("tensor", m_tp2),
                                   ("q_bxo%d" % sl, m_x2d.get((tt, oc - 2), 0))])
                m_x2f[(tt, oc)] = em.op(
                    "vector",
                    lambda e, sl=sl: e.tensor_copy(out=x2stgb[:, sl, :],
                                                   in_=PX2[:, sl * 512:sl * 512 + 128]))
                em.op("sync",
                      lambda e, oc=oc, sl=sl, tt=tt: e.dma_start(
                          out=dx2o[:, oc, tt * 128:(tt + 1) * 128], in_=x2stg[:, sl, :]),
                      waits=[("vector", m_x2f[(tt, oc)])], dma="bxo%d" % sl)
                m_x2d[(tt, oc)] = em.op(
                    "sync",
                    lambda e, oc=oc, sl=sl, tt=tt: e.dma_start(
                        out=dx2s[:, oc, tt * 128:(tt + 1) * 128], in_=x2stgb[:, sl, :]),
                    dma="bxo%d" % sl)
            m_x2tp[tt] = em.cnt["tensor"]
        if fused:
            em.wait("sync", "q_bxo0", em.qcnt("bxo0"))
            em.wait("sync", "q_bxo1", em.qcnt("bxo1"))
        else:
            em.wait("gpsimd", "q_out", m_out)


def mk_P_B(PS1, PS2):
    return {
        "PG": PS1[:, 0:1024].rearrange("p (a b) -> p a b", b=512),
        "PU": PS1[:, 1024:2048].rearrange("p (a b) -> p a b", b=512),
        "PD": PS2[:, 0:1024],
        "PT": PS2[:, 1024:2048].bitcast(BF16).rearrange("p (a b) -> p a b", b=1024),
    }


def build_launch_B(ntiles=16, phases=9):
    nc = bass.Bass("TRN2", target_bir_lowering=False)
    DR = mk_dr_B(nc)
    DR["consts"] = nc.dram_tensor("consts", [128, 68], F32, kind="ExternalInput").ap()
    es = contextlib.ExitStack()
    with es:
        ar = es.enter_context(Arena(nc, cols=48850))
        PS1 = es.enter_context(nc.psum_tensor("PS1", [128, 2048], F32))[:, :]
        PS2 = es.enter_context(nc.psum_tensor("PS2", [128, 2048], F32))[:, :]
        P = mk_P_B(PS1, PS2)
        em = Emitter(nc, es)
        em.prefix = "B"
        blk = es.enter_context(nc.Block())
        emit_B(nc, em, ar, P, DR, ntiles=ntiles, phases=phases)
        em.build(blk)
    return nc


def host_launch_B(x1, inputs, wq, ntiles=16, phases=9):
    wgq, wuq, wdq = wq
    consts = np.zeros((128, 68), np.float32)
    consts[:, 0] = 1.0
    consts[:, 1] = EPS
    consts[:, 4:68] = 1.0
    identb = np.eye(128).astype(BF)
    wgq_T = np.ascontiguousarray(wgq.T)
    wuq_T = np.ascontiguousarray(wuq.T)
    wdq_T = np.ascontiguousarray(wdq.T)
    in_maps = []
    for c in range(8):
        b, hf = divmod(c, 2)
        in_maps.append({
            "x1_tm": np.ascontiguousarray(x1[b, hf * TOK:(hf + 1) * TOK, :]),
            "wgq_T": wgq_T, "wuq_T": wuq_T, "wdq_T": wdq_T,
            "identb": identb, "consts": consts,
        })
    nc = build_launch_B(ntiles=ntiles, phases=phases)
    res = run_bass_kernel_spmd(nc, in_maps, core_ids=list(range(8)))
    x2 = np.empty((B, S, D), np.float32)
    for c in range(8):
        b, hf = divmod(c, 2)
        x2[b, hf * TOK:(hf + 1) * TOK, :] = res.results[c]["x2_tm"]
    return x2


DIN = 2048
NST = 16


def mk_dr_C(nc, fused=False):
    ki = "ExternalInput"
    ko = "Internal" if fused else "ExternalOutput"
    d = {}
    if not fused:
        d["x2_fm"] = nc.dram_tensor("x2_fm", [D, S], F32, kind=ki).ap()
    for nm, shape, dt_ in (("winx", [128, NCH, DIN], BF16), ("winz", [128, NCH, 1024], BF16),
                           ("wxproj", [128, 16, 96], BF16), ("wdt", [64, 8, 128], BF16),
                           ("wconv", [128, 16, 4], F32), ("bconv", [128, 16], F32),
                           ("bdt", [128, 8], F32), ("Dcol", [128, 8], F32),
                           ("acomp", [128, 8, 16], F32), ("rep8", [128, 16, 128], BF16),
                           ("sel8", [128, 16, 128], BF16), ("bcpat", [32, 2, 128], BF16)):
        d[nm] = nc.dram_tensor(nm, shape, dt_, kind=ki).ap()
    d["yz_fm"] = nc.dram_tensor("yz_fm", [1024, S], BF16, kind=ko).ap()
    return d


def emit_C(nc, em, ar, P, DR, fused=False):
    if fused:
        dx2g = DR["x2_both"]
        m_xag = DR.get("m_xag", 0)
    else:
        dx2 = DR["x2_fm"]
        dwix, dwiz, dwxp, dwdt = DR["winx"], DR["winz"], DR["wxproj"], DR["wdt"]
    dwcv, dbcv, dbdt, dDc = DR["wconv"], DR["bconv"], DR["bdt"], DR["Dcol"]
    dac, drep, dsel, dbcp = DR["acomp"], DR["rep8"], DR["sel8"], DR["bcpat"]
    dcst = DR["consts"]
    dout = DR["yz_fm"]
    if True:
        winx = ar.view(0, [128, NCH, DIN], BF16)
        winz = ar.view(8192, [128, NCH, 1024], BF16)
        wxp = ar.view(12288, [128, 16, 96], BF16)
        wdt = ar.view(13056, [64, 8, 128], BF16)
        wcv = ar.view(13568, [128, 16, 4], F32)
        bcv = ar.view(13632, [128, 16], F32)
        bdt = ar.view(13648, [128, 8], F32)
        Dc = ar.view(13656, [128, 8], F32)
        aneg = ar.view(13664, [128, 8, 16], F32)
        rep8 = ar.view(13792, [128, 16, 128], BF16)
        sel8 = ar.view(14816, [128, 16, 128], BF16)
        bcp = ar.view(15840, [32, 2, 128], BF16)
        cst = ar.view(15968, [128, 68], F32)
        x2st = ar.view(16036, [128, NCH, 512], F32)
        x2bf = ar.view(20132, [128, 2, NCH, 512], BF16)
        xzb = ar.view(24228, [128, 16, 515], F32)
        cvb = ar.view(32468, [128, 2, 512], F32)
        xinbf = ar.view(33492, [128, 16, 512], BF16)
        xinhf = ar.view(37588, [128, 8, 512], F32)
        dbc = ar.view(41684, [96, 512], F32)
        dbcbf = ar.view(42196, [96, 512], BF16)
        bcbf = ar.view(42324, [32, 512], BF16)
        vbuf = ar.view(42452 + 128, [128, 512], F32)
        spb = ar.view(42964 + 128, [128, 512], F32)
        dtbf = ar.view(43476 + 128, [128, 8, 512], BF16)
        upbf = ar.view(45524 + 128, [128, 8, 512], BF16)
        brep = ar.view(47572 + 128, [128, 512], BF16)
        crep = ar.view(47828 + 128, [128, 512], BF16)
        dtas = ar.view(48084 + 128, [128, 512], F32)
        dab = ar.view(24228, [128, 2, 512], F32)
        ubb = ar.view(25252, [128, 2, 512], F32)
        sresb = ar.view(30548, [128, 4, 512], F32)
        mbfb = ar.view(26276, [128, 4, 512], BF16)
        state = ar.view(27812, [128, 128], F32)
        ybuf = ar.view(27940, [128, 512], F32)
        zsil = ar.view(28452, [128, 512], F32)
        yzbf = ar.view(28964, [128, 2, 512], BF16)
        halo = ar.view(29476, [128, 16, 3], F32)
        dtasb = ar.view(29524, [128, 2, 512], F32)
        PXZ, PB96, PDT = P["PXZ"], P["PB96"], P["PDT"]
        PDA, PUPa, PUPb, PY = P["PDA"], P["PUPa"], P["PUPb"], P["PY"]
        if fused:
            x2g0 = dx2g.rearrange("h (c p) t -> h p c t", p=128)

            def x2slice(tc):
                return x2g0[tc // 4][:, :, (tc % 4) * 512:(tc % 4) * 512 + 512]
        else:
            x2re = dx2.rearrange("(c p) t -> p c t", p=128) if dx2.ndim == 2 else dx2

            def x2slice(tc):
                if x2re.ndim == 3:
                    return x2re[:, :, tc * 512:(tc + 1) * 512]
                return x2re[:, :, tc // 4, (tc % 4) * 512:(tc % 4) * 512 + 512]

        ld_pairs = [(wcv, dwcv), (bcv, dbcv), (bdt, dbdt), (Dc, dDc),
                    (rep8, drep), (sel8, dsel), (bcp, dbcp), (cst, dcst)]
        if not fused:
            ld_pairs = [(winx, dwix), (winz, dwiz), (wxp, dwxp), (wdt, dwdt)] + ld_pairs
        for (dst_, src_) in ld_pairs:
            m_ld = em.op("sync", lambda e, d_=dst_, s_=src_: e.dma_start(out=d_, in_=s_),
                         dma="cld")
        m_wc = 0
        if fused:
            wstg = ar.view(16036, [128, 2, 2048], F32)
            wstg1 = ar.view(16036, [128, 2, 1024], F32)
            wstg2 = ar.view(16036, [128, 16, 96], F32)
            wstg3 = ar.view(16036, [64, 8, 128], F32)
            for k in range(4):
                m_l = em.op("sync", lambda e, k=k: e.dma_start(
                    out=wstg, in_=DR["winx_f"][:, 2 * k:2 * k + 2, :]),
                    waits=[("vector", m_wc)], dma="cwl")
                m_wc = em.op("vector", lambda e, k=k: e.tensor_copy(
                    out=winx[:, 2 * k:2 * k + 2, :], in_=wstg), waits=[("q_cwl", m_l)])
            for k in range(4):
                m_l = em.op("sync", lambda e, k=k: e.dma_start(
                    out=wstg1, in_=DR["winz_f"][:, 2 * k:2 * k + 2, :]),
                    waits=[("vector", m_wc)], dma="cwl")
                m_wc = em.op("vector", lambda e, k=k: e.tensor_copy(
                    out=winz[:, 2 * k:2 * k + 2, :], in_=wstg1), waits=[("q_cwl", m_l)])
            m_l = em.op("sync", lambda e: e.dma_start(out=wstg2, in_=DR["wxp_f"]),
                        waits=[("vector", m_wc)], dma="cwl")
            m_wc = em.op("vector", lambda e: e.tensor_copy(out=wxp, in_=wstg2),
                         waits=[("q_cwl", m_l)])
            m_l = em.op("sync", lambda e: e.dma_start(out=wstg3, in_=DR["wdt_f"]),
                        waits=[("vector", m_wc)], dma="cwl")
            m_wc = em.op("vector", lambda e: e.tensor_copy(out=wdt, in_=wstg3),
                         waits=[("q_cwl", m_l)])
        m_ac = em.op("sync", lambda e: e.dma_start(out=aneg, in_=dac), dma="cld")
        m_ae = em.op("scalar", lambda e: e.activation(
            out=aneg.rearrange("p a b -> p (a b)"), in_=aneg.rearrange("p a b -> p (a b)"),
            func=AF.Exp), waits=[("q_cld", m_ac)])
        m_an = em.op("vector", lambda e: e.tensor_single_scalar(
            out=aneg.rearrange("p a b -> p (a b)"), in_=aneg.rearrange("p a b -> p (a b)"),
            scalar=-1.0, op=ALU.mult), waits=[("scalar", m_ae)])
        em.op("vector", lambda e: e.memset(state, 0.0))
        m_hz = em.op("vector", lambda e: e.memset(halo, 0.0))

        m_x2 = {}
        m_cast = {}
        m_xz = {}
        m_ev = {}
        m_cv = {}
        m_sil = {}
        m_dbc = 0
        m_dbce = 0
        m_dt = {}
        m_sp = {}
        m_bc = 0
        m_brep = 0
        m_scan = {}
        m_y = {}
        m_yz = {}
        m_out = 0
        m_halo_save = m_hz
        m_dtamm = {}
        m_upmm = {}
        m_dtts = {}
        m_ub = {}
        m_cm = {}
        m_exp = {}
        m_ymm = {}
        m_zm = {}
        m_zs = {}
        for tc in range(8):
            par = tc % 2
            if fused:
                m_x2[tc] = em.op("sync",
                                 lambda e, tc=tc, par=par: e.dma_start(
                                     out=x2bf[:, par], in_=x2slice(tc)),
                                 waits=[("tensor", m_zm.get(tc - 2, 0)),
                                        ("q_cld", em.qcnt("cld") if tc == 0 else 0),
                                        ("gpsimd", m_xag if tc == 0 else 0)],
                                 dma="cld")
                m_cast[tc] = m_wc
            else:
                m_x2[tc] = em.op("sync",
                                 lambda e, tc=tc: e.dma_start(
                                     out=x2st, in_=x2slice(tc)),
                                 waits=[("vector", m_cast.get(tc - 1, 0)),
                                        ("q_cld", em.qcnt("cld") if tc == 0 else 0)], dma="cld")
                m_cast[tc] = em.op("vector",
                                   lambda e, par=par: e.tensor_copy(out=x2bf[:, par], in_=x2st),
                                   waits=[("q_cld", m_x2[tc]),
                                          ("tensor", m_zm.get(tc - 2, 0))])
            for oc in range(16):
                def mk_xz(e, oc=oc, par=par):
                    inst = None
                    for kc in range(8):
                        inst = e.matmul(PXZ[:, :], winx[:, kc, oc * 128:(oc + 1) * 128],
                                        x2bf[:, par, kc, :], start=(kc == 0), stop=(kc == 7))
                    return inst
                xzw = [("vector", m_cast[tc]),
                       ("q_cld", m_x2[tc] if fused else 0),
                       ("vector", m_ev.get((tc, oc - 1),
                                           m_ev.get((tc - 1, 15), 0) if tc > 0 else 0))]
                if oc == 0 and tc > 0:
                    xzw.append(("scalar", m_zs.get((tc - 1, 7), 0)))
                m_xz[(tc, oc)] = em.op("tensor", mk_xz, waits=xzw)
                m_ev[(tc, oc)] = em.op("vector",
                                       lambda e, oc=oc: e.tensor_copy(
                                           out=xzb[:, oc, 3:515], in_=PXZ[:, :]),
                                       waits=[("tensor", m_xz[(tc, oc)]),
                                              ("vector", m_halo_save if oc == 0 else 0),
                                              ("vector", m_scan.get((tc - 1, 7, 15), 0) if oc == 0 else 0)])
                m_ev[(tc, oc)] = em.op("vector",
                                       lambda e, oc=oc: e.tensor_copy(
                                           out=xzb[:, oc, 0:3], in_=halo[:, oc, :]))
            for oc in range(16):
                cpar = oc % 2

                def mk_cv0(e, oc=oc, cpar=cpar):
                    return e.tensor_scalar(out=cvb[:, cpar, :], in0=xzb[:, oc, 0:512],
                                           scalar1=wcv[:, oc:oc + 1, 0],
                                           scalar2=bcv[:, oc:oc + 1],
                                           op0=ALU.mult, op1=ALU.add)
                em.op("vector", mk_cv0, waits=[("scalar", m_sil.get((tc, oc - 2), 0))])
                for k in range(1, 4):
                    em.op("vector",
                          lambda e, oc=oc, k=k, cpar=cpar: e.scalar_tensor_tensor(
                              out=cvb[:, cpar, :], in0=xzb[:, oc, k:k + 512],
                              scalar=wcv[:, oc:oc + 1, k], in1=cvb[:, cpar, :],
                              op0=ALU.mult, op1=ALU.add))
                m_cv[(tc, oc)] = em.cnt["vector"]
                m_sil[(tc, oc)] = em.op("scalar",
                                        lambda e, oc=oc, cpar=cpar: e.activation(
                                            out=xinbf[:, oc, :], in_=cvb[:, cpar, :],
                                            func=AF.Silu),
                                        waits=[("vector", m_cv[(tc, oc)])])
                if oc < 8:
                    m_sil[(tc, oc)] = em.op("scalar",
                                            lambda e, oc=oc, cpar=cpar: e.activation(
                                                out=xinhf[:, oc, :], in_=cvb[:, cpar, :],
                                                func=AF.Silu))
            m_halo_save = em.op("vector",
                                lambda e: e.tensor_copy(out=halo, in_=xzb[:, :, 512:515]),
                                waits=[("scalar", m_sil[(tc, 15)])])
            def mk_dbc(e):
                inst = None
                for oc in range(16):
                    inst = e.matmul(PB96[:, :], wxp[:, oc, :], xinbf[:, oc, :],
                                    start=(oc == 0), stop=(oc == 15))
                return inst
            m_dbcm = em.op("tensor", mk_dbc,
                           waits=[("scalar", m_sil[(tc, 15)]),
                                  ("vector", m_dbce if tc > 0 else 0)])
            m_dbce0 = em.op("vector", lambda e: e.tensor_copy(out=dbc, in_=PB96[:, :]),
                            waits=[("tensor", m_dbcm)])
            em.op("vector", lambda e: e.tensor_copy(out=dbcbf, in_=dbc))
            m_dbce = em.op("vector", lambda e: e.tensor_copy(out=bcbf, in_=dbc[64:96, :]))
            for oc in range(8):
                def mk_dt(e, oc=oc):
                    return e.matmul(PDT[:, :], wdt[:, oc, :], dbcbf[0:64, :],
                                    start=True, stop=True)
                m_dtm = em.op("tensor", mk_dt,
                              waits=[("vector", m_dbce),
                                     ("vector", m_dtts.get((tc, oc - 1, 0), 0)),
                                     ("vector", m_sp.get((tc, oc - 1), 0))])
                em.op("vector",
                      lambda e, oc=oc: e.tensor_scalar(out=vbuf, in0=PDT[:, :],
                                                       scalar1=bdt[:, oc:oc + 1],
                                                       scalar2=None, op0=ALU.add),
                      waits=[("tensor", m_dtm)])
                m_e1 = em.op("scalar", lambda e: e.activation(out=spb, in_=vbuf,
                                                              func=AF.Exp, scale=-1.0),
                             waits=[("vector", em.cnt["vector"])])
                m_l1 = em.op("scalar", lambda e: e.activation(out=spb, in_=spb,
                                                              func=AF.Ln, bias=cst[:, 0:1]))
                em.op("vector", lambda e: e.tensor_single_scalar(out=vbuf, in_=vbuf,
                                                                 scalar=0.0, op=ALU.max),
                      waits=[("scalar", m_l1)])
                m_sp[(tc, oc)] = em.op("vector",
                                       lambda e, oc=oc: e.tensor_tensor(
                                           out=dtbf[:, oc, :], in0=vbuf, in1=spb,
                                           op=ALU.add))
                m_sp[(tc, oc)] = em.op("vector",
                                       lambda e, oc=oc: e.tensor_tensor(
                                           out=upbf[:, oc, :], in0=xinhf[:, oc, :],
                                           in1=dtbf[:, oc, :], op=ALU.mult))
            def mk_b(e):
                return e.matmul(PDA[:, 0, :], bcp[:, 0, :], bcbf,
                                start=True, stop=True)

            def mk_c(e):
                return e.matmul(PUPa, bcp[:, 1, :], bcbf,
                                start=True, stop=True)
            m_bm = em.op("tensor", mk_b, waits=[("vector", m_dbce),
                                                ("vector", m_ub.get((tc - 1, 7, 15), 0)),
                                                ("scalar", m_dtts.get((tc - 1, 7, 15), 0))])
            m_cmm = em.op("tensor", mk_c)
            m_brep = em.op("vector", lambda e: e.tensor_copy(out=brep, in_=PDA[:, 0, :]),
                           waits=[("tensor", m_bm),
                                  ("gpsimd", m_cm.get((tc - 1, 7, 15), 0))])
            m_crep = em.op("vector", lambda e: e.tensor_copy(out=crep, in_=PUPa),
                           waits=[("tensor", m_cmm)])
            def pget(d, tc_, oc_, o_, k):
                g_ = (tc_ * 8 + oc_) * 16 + o_ - k
                if g_ < 0:
                    return 0
                return d.get((g_ // 128, (g_ // 16) % 8, g_ % 16), 0)

            for oc in range(8):
                for o in range(16):
                    sl2 = o % 2
                    sl4 = o % 4

                    def mk_da(e, oc=oc, o=o, sl2=sl2):
                        return e.matmul(PDA[:, sl2, :], rep8[:, o, :], dtbf[:, oc, :],
                                        start=True, stop=True, skip_group_check=True)
                    m_dtamm[(tc, oc, o)] = em.op(
                        "tensor", mk_da,
                        waits=[("vector", m_brep if (oc == 0 and o == 0) else 0),
                               ("scalar", pget(m_dtts, tc, oc, o, 2))])

                    def mk_up(e, oc=oc, o=o, sl2=sl2):
                        return e.matmul(PUPa if sl2 == 0 else PUPb,
                                        rep8[:, o, :], upbf[:, oc, :],
                                        start=True, stop=True, skip_group_check=True)
                    m_upmm[(tc, oc, o)] = em.op(
                        "tensor", mk_up,
                        waits=[("vector", m_crep if (oc == 0 and o == 0) else 0),
                               ("vector", pget(m_ub, tc, oc, o, 2))])
                    m_dtts[(tc, oc, o)] = em.op(
                        "scalar",
                        lambda e, oc=oc, o=o, sl2=sl2: e.activation(
                            out=dtasb[:, sl2, :], in_=PDA[:, sl2, :], func=AF.Relu,
                            scale=aneg[:, oc, o:o + 1], bias=cst[:, 2:3]),
                        waits=[("tensor", m_dtamm[(tc, oc, o)])])
                    m_exp[(tc, oc, o)] = em.op(
                        "scalar",
                        lambda e, sl2=sl2: e.activation(
                            out=dab[:, sl2, :], in_=dtasb[:, sl2, :], func=AF.Exp,
                            bias=cst[:, 3:4]),
                        waits=[("vector", pget(m_scan, tc, oc, o, 2))])
                    m_ub[(tc, oc, o)] = em.op(
                        "vector",
                        lambda e, sl2=sl2: e.tensor_tensor(
                            out=ubb[:, sl2, :], in0=(PUPa if sl2 == 0 else PUPb),
                            in1=brep, op=ALU.mult),
                        waits=[("tensor", m_upmm[(tc, oc, o)])])
                    g = oc * 16 + o
                    m_scan[(tc, oc, o)] = em.op(
                        "vector",
                        lambda e, g=g, sl2=sl2, sl4=sl4: e.tensor_tensor_scan(
                            out=sresb[:, sl4, :], data0=dab[:, sl2, :],
                            data1=ubb[:, sl2, :],
                            initial=state[:, g:g + 1], op0=ALU.mult, op1=ALU.add),
                        waits=[("scalar", m_exp[(tc, oc, o)]),
                               ("gpsimd", pget(m_cm, tc, oc, o, 4))])
                    em.op("vector",
                          lambda e, g=g, sl4=sl4: e.tensor_copy(
                              out=state[:, g:g + 1], in_=sresb[:, sl4, 511:512]))
                    m_cm[(tc, oc, o)] = em.op(
                        "gpsimd",
                        lambda e, sl4=sl4: e.tensor_tensor(
                            out=mbfb[:, sl4, :], in0=sresb[:, sl4, :], in1=crep,
                            op=ALU.mult),
                        waits=[("vector", m_scan[(tc, oc, o)]),
                               ("tensor", pget(m_ymm, tc, oc, o, 4))])

                    def mk_y(e, oc=oc, o=o, sl4=sl4):
                        return e.matmul(PY[:, :], sel8[:, o, :], mbfb[:, sl4, :],
                                        start=(o == 0), stop=(o == 15),
                                        skip_group_check=True)
                    m_ymm[(tc, oc, o)] = em.op("tensor", mk_y,
                                               waits=[("gpsimd", m_cm[(tc, oc, o)]),
                                                      ("vector", m_y.get((tc, oc - 1), 0) if o == 0 else 0)])
                m_y[(tc, oc)] = em.op(
                    "vector",
                    lambda e, oc=oc: e.scalar_tensor_tensor(
                        out=ybuf, in0=xinhf[:, oc, :], scalar=Dc[:, oc:oc + 1],
                        in1=PY[:, :], op0=ALU.mult, op1=ALU.add),
                    waits=[("tensor", m_ymm[(tc, oc, 15)])])
                def mk_z(e, oc=oc, par=par):
                    inst = None
                    for kc in range(8):
                        inst = e.matmul(PXZ[:, :], winz[:, kc, oc * 128:(oc + 1) * 128],
                                        x2bf[:, par, kc, :], start=(kc == 0), stop=(kc == 7))
                    return inst
                m_zm[tc] = em.op("tensor", mk_z,
                                 waits=[("scalar", m_zs.get((tc, oc - 1),
                                                            m_zs.get((tc - 1, 7), 0))),
                                        ("vector", m_ev[(tc, 15)] if oc < 2 else 0)])
                m_zs[(tc, oc)] = em.op("scalar",
                                       lambda e, oc=oc: e.activation(
                                           out=zsil, in_=PXZ[:, :], func=AF.Silu),
                                       waits=[("tensor", m_zm[tc]),
                                              ("vector", m_yz.get((tc, oc - 1), 0))])
                m_yz[(tc, oc)] = em.op("vector",
                                       lambda e, oc=oc: e.tensor_tensor(
                                           out=yzbf[:, oc % 2, :], in0=ybuf, in1=zsil,
                                           op=ALU.mult),
                                       waits=[("scalar", m_zs[(tc, oc)]),
                                              ("q_out", m_out)])
                m_out = em.op("gpsimd",
                              lambda e, oc=oc, tc=tc: e.dma_start(
                                  out=dout[oc * 128:(oc + 1) * 128, tc * 512:(tc + 1) * 512],
                                  in_=yzbf[:, oc % 2, :]),
                              waits=[("vector", m_yz[(tc, oc)])], dma="out")
        em.wait("gpsimd", "q_out", m_out)


def mk_P_C(PS1, PS2):
    return {
        "PXZ": PS1[:, 0:512],
        "PDA": PS1[:, 512:1536].rearrange("p (a b) -> p a b", b=512),
        "PUPa": PS1[:, 1536:2048],
        "PUPb": PS2[:, 1536:2048],
        "PB96": PS2[0:96, 0:512],
        "PDT": PS2[:, 512:1024],
        "PY": PS2[:, 1024:1536],
    }


def build_launch_C():
    nc = bass.Bass("TRN2", target_bir_lowering=False)
    DR = mk_dr_C(nc)
    DR["consts"] = nc.dram_tensor("consts", [128, 68], F32, kind="ExternalInput").ap()
    es = contextlib.ExitStack()
    with es:
        ar = es.enter_context(Arena(nc, cols=48850))
        PS1 = es.enter_context(nc.psum_tensor("PS1", [128, 2048], F32))[:, :]
        PS2 = es.enter_context(nc.psum_tensor("PS2", [128, 2048], F32))[:, :]
        P = mk_P_C(PS1, PS2)
        em = Emitter(nc, es)
        em.prefix = "C"
        blk = es.enter_context(nc.Block())
        emit_C(nc, em, ar, P, DR)
        em.build(blk)
    return nc


def host_launch_C(x2, inputs):
    W_in = np.asarray(inputs["L1_attn_W_in"])
    W_conv = np.asarray(inputs["L1_attn_W_conv"])
    b_conv = np.asarray(inputs["L1_attn_b_conv"])
    W_xp = np.asarray(inputs["L1_attn_W_xproj"])
    W_dt = np.asarray(inputs["L1_attn_W_dt"])
    b_dt = np.asarray(inputs["L1_attn_b_dt"])
    A_log = np.asarray(inputs["L1_attn_A_log"])
    Dv = np.asarray(inputs["L1_attn_D"])
    consts = np.zeros((128, 68), np.float32)
    consts[:, 0] = 1.0
    consts[:, 1] = EPS
    consts[:, 4:68] = 1.0
    rep8 = np.zeros((128, 16, 128), BF)
    sel8 = np.zeros((128, 16, 128), BF)
    for o in range(16):
        for p in range(128):
            dd = 8 * o + (p % 8)
            rep8[dd, o, p] = 1
            sel8[p, o, dd] = 1
    bcp = np.zeros((32, 2, 128), BF)
    for p in range(128):
        bcp[p // 8, 0, p] = 1
        bcp[16 + p // 8, 1, p] = 1
    in_maps = []
    for c in range(8):
        b, hf = divmod(c, 2)
        own = slice(hf * 1024, (hf + 1) * 1024)
        oth = slice((1 - hf) * 1024, (2 - hf) * 1024)
        perm = np.r_[np.arange(hf * 1024, (hf + 1) * 1024),
                     np.arange((1 - hf) * 1024, (2 - hf) * 1024)]
        winx = np.concatenate([W_in[:, :2048][:, perm]], axis=1)
        winz = W_in[:, 2048:][:, own]
        wxp_p = W_xp[perm, :]
        wdt_h = W_dt[:, own]
        acomp = np.zeros((128, 8, 16), np.float32)
        for p in range(128):
            n, dj = p // 8, p % 8
            for ch in range(8):
                for o in range(16):
                    acomp[p, ch, o] = A_log[hf * 1024 + ch * 128 + 8 * o + dj, n]
        in_maps.append({
            "x2_fm": _fm(x2[b]),
            "winx": np.ascontiguousarray(
                winx.reshape(NCH, 128, 2048).transpose(1, 0, 2)).astype(BF),
            "winz": np.ascontiguousarray(
                winz.reshape(NCH, 128, 1024).transpose(1, 0, 2)).astype(BF),
            "wxproj": np.ascontiguousarray(
                wxp_p.reshape(16, 128, 96).transpose(1, 0, 2)).astype(BF),
            "wdt": np.ascontiguousarray(
                wdt_h.reshape(64, 8, 128)).astype(BF),
            "wconv": np.ascontiguousarray(
                W_conv[:, :2048][:, perm].T.reshape(16, 128, 4).transpose(1, 0, 2)),
            "bconv": np.ascontiguousarray(
                b_conv[perm].reshape(16, 128).T),
            "bdt": np.ascontiguousarray(b_dt[own].reshape(8, 128).T),
            "Dcol": np.ascontiguousarray(Dv[own].reshape(8, 128).T),
            "acomp": acomp,
            "rep8": rep8, "sel8": sel8, "bcpat": bcp, "consts": consts,
        })
    nc = build_launch_C()
    res = run_bass_kernel_spmd(nc, in_maps, core_ids=list(range(8)))
    yz = np.empty((B, DIN, S), np.float32)
    for c in range(8):
        b, hf = divmod(c, 2)
        yz[b, hf * 1024:(hf + 1) * 1024, :] = res.results[c]["yz_fm"].astype(np.float32)
    return yz


def mk_dr_D(nc, fused=False):
    ki = "ExternalInput"
    ko = "Internal" if fused else "ExternalOutput"
    d = {}
    if not fused:
        d["yz_in"] = nc.dram_tensor("yz_in", [DIN, TOK], BF16, kind=ki).ap()
        d["x2h_fm"] = nc.dram_tensor("x2h_fm", [D, TOK], F32, kind=ki).ap()
    d["s_hbm"] = nc.dram_tensor("s_hbmD", [2, 512], F32, kind="Internal").ap()
    d["W_out"] = nc.dram_tensor("W_out", [DIN, D], F32, kind=ki).ap()
    d["x3_fm"] = nc.dram_tensor("x3_fm", [D, TOK], F32, kind=ko).ap()
    return d


def emit_D(nc, em, ar, P, DR):
    dyz = DR["yz_in"]
    dwo = DR["W_out"]
    dx2 = DR["x2h_fm"]
    dst, dcst = DR["stpat"], DR["consts"]
    dout = DR["x3_fm"]
    s_hbm = DR["s_hbm"]
    if True:
        wout = ar.view(0, [128, 16, D], BF16)
        yzsb = ar.view(8192, [128, 16, TOK], BF16)
        resid = ar.view(24576, [128, NCH, TOK], F32)
        wstage = ar.view(40960, [128, 4, D], F32)
        x2st = ar.view(45056, [128, 2, 512], F32)
        stp = ar.view(46080, [128, 130], F32)
        cst = ar.view(46210, [128, 68], F32)
        sqt = ar.view(46278, [128, 512], F32)
        mrow = ar.view(46790, [1, 512], F32)
        s2row = ar.view(47302, [1, 512], F32)
        msq = ar.view(47303 + 511, [1, 512], F32)
        varr = ar.view(48326, [1, 512], F32)
        PA, PB = P["PA"], P["PB"]
        mrow = ar.view(46790, [1, 512], F32)
        s2row = ar.view(46791, [1, 512], F32, base=1) if False else ar.view(47302, [1, 512], F32)
        msq = ar.view(47814, [1, 512], F32)
        varr = ar.view(40960, [1, 512], F32)
        rstd_t = ar.view(41472, [1, 512], F32)
        mrs_t = ar.view(41984, [1, 512], F32)
        lnrep = ar.view(42496, [128, 2, 512], F32)

        x2re = dx2.rearrange("(c p) t -> p c t", p=128) if dx2.ndim == 2 else dx2
        outre = dout.rearrange("(c p) t -> p c t", p=128)

        em.op("sync", lambda e: e.dma_start(out=stp, in_=dst), dma="dld")
        em.op("sync", lambda e: e.dma_start(out=cst, in_=dcst), dma="dld")
        m_yzl = em.op("sync", lambda e: e.dma_start(
            out=yzsb, in_=dyz.rearrange("(c p) t -> p c t", p=128)
            if dyz.ndim == 2 else dyz), dma="dld")
        m_cast = 0
        for hf in range(4):
            m_ld = em.op("sync",
                         lambda e, hf=hf: e.dma_start(
                             out=wstage,
                             in_=dwo.rearrange("(c p) o -> p c o", p=128)[:, hf * 4:hf * 4 + 4, :]),
                         waits=[("vector", m_cast)], dma="dld")
            m_cast = em.op("vector",
                           lambda e, hf=hf: e.tensor_copy(
                               out=wout[:, hf * 4:hf * 4 + 4, :], in_=wstage),
                           waits=[("q_dld", m_ld)])
        m_wot = {}
        m_res = {}
        m_sq = {}
        m_stat = {}
        m_lnfree = 0
        m_xq = {}
        m_out = 0
        m_g8 = 0
        for j in range(32):
            qc, oc = divmod(j, 8)
            m_xq[j] = em.op("sync",
                            lambda e, qc=qc, oc=oc, j=j: e.dma_start(
                                out=x2st[:, j % 2, :], in_=x2re[:, oc, qc * 512:(qc + 1) * 512]),
                            waits=[("vector", m_res.get(j - 2, 0))],
                            dma="dxq%d" % (j % 2))
            m_wot[j] = em.op("tensor",
                             lambda e, qc=qc, oc=oc, j=j: [
                                 e.matmul(PA[:, (j % 2) * 512:(j % 2) * 512 + 512],
                                          wout[:, kc, oc * 128:(oc + 1) * 128],
                                          yzsb[:, kc, qc * 512:(qc + 1) * 512],
                                          start=(kc == 0), stop=(kc == 15))
                                 for kc in range(16)][-1],
                             waits=[("vector", m_cast), ("q_dld", em.qcnt("dld")),
                                    ("vector", m_res.get(j - 2, 0))])
            m_res[j] = em.op("vector",
                             lambda e, qc=qc, oc=oc, j=j: e.tensor_tensor(
                                 out=resid[:, oc, qc * 512:(qc + 1) * 512],
                                 in0=PA[:, (j % 2) * 512:(j % 2) * 512 + 512],
                                 in1=x2st[:, j % 2, :], op=ALU.add),
                             waits=[("tensor", m_wot[j]), ("q_dxq%d" % (j % 2), m_xq[j])])
            m_sq[j] = em.op("scalar",
                            lambda e, qc=qc, oc=oc: e.activation(
                                out=sqt, in_=resid[:, oc, qc * 512:(qc + 1) * 512],
                                func=AF.Square),
                            waits=[("vector", m_res[j]),
                                   ("tensor", m_stat.get(j - 1, 0))])

            def mk_stat(e, qc=qc, oc=oc):
                e.matmul(PB[:, 0:512], stp[:, 0:65],
                         resid[:, oc, qc * 512:(qc + 1) * 512],
                         start=(oc == 0), stop=False, skip_group_check=True)
                return e.matmul(PB[:, 0:512], stp[:, 65:130], sqt,
                                start=False, stop=(oc == 7), skip_group_check=True)
            m_stat[j] = em.op("tensor", mk_stat,
                              waits=[("scalar", m_sq[j]),
                                     ("vector", m_lnfree if oc == 0 else 0)])
            if oc == 7:
                m1 = em.op("vector",
                           lambda e: e.tensor_scalar(out=mrow, in0=PB[0:1, 0:512],
                                                     scalar1=1.0 / D, scalar2=None,
                                                     op0=ALU.mult),
                           waits=[("tensor", m_stat[j])])
                m_lnfree = em.op("vector",
                                 lambda e: e.tensor_scalar(out=s2row, in0=PB[64:65, 0:512],
                                                           scalar1=1.0 / D, scalar2=None,
                                                           op0=ALU.mult))
                m2 = em.op("scalar", lambda e: e.activation(out=msq, in_=mrow, func=AF.Square),
                           waits=[("vector", m_lnfree)])
                m3 = em.op("vector",
                           lambda e: e.tensor_tensor(out=varr, in0=s2row, in1=msq,
                                                     op=ALU.subtract),
                           waits=[("scalar", m2), ("vector", m_cast)])
                m4 = em.op("scalar",
                           lambda e: e.activation(out=msq, in_=varr, func=AF.Ln,
                                                  bias=cst[0:1, 1:2]),
                           waits=[("vector", m3)])
                m5 = em.op("scalar",
                           lambda e: e.activation(out=rstd_t, in_=msq, func=AF.Exp,
                                                  scale=-0.5))
                m6 = em.op("vector",
                           lambda e: e.tensor_tensor(out=mrs_t, in0=mrow, in1=rstd_t,
                                                     op=ALU.mult),
                           waits=[("scalar", m5)])
                m7 = em.op("gpsimd", lambda e: e.dma_start(out=s_hbm[0, :], in_=rstd_t[0:1, :]),
                           waits=[("vector", m6)], dma="lnb")
                m7 = em.op("gpsimd", lambda e: e.dma_start(out=s_hbm[1, :], in_=mrs_t[0:1, :]),
                           dma="lnb")
                m8 = em.op("gpsimd",
                           lambda e: e.dma_start(out=lnrep[:, 0, :],
                                                 in_=s_hbm[0:1, :].broadcast_to([128, 512])),
                           waits=[("q_lnb", m7)], dma="lnb")
                m8 = em.op("gpsimd",
                           lambda e: e.dma_start(out=lnrep[:, 1, :],
                                                 in_=s_hbm[1:2, :].broadcast_to([128, 512])),
                           dma="lnb")
                m_g8 = m8
                for oc2 in range(8):
                    em.op("vector",
                          lambda e, qc=qc, oc2=oc2: e.tensor_tensor(
                              out=resid[:, oc2, qc * 512:(qc + 1) * 512],
                              in0=resid[:, oc2, qc * 512:(qc + 1) * 512],
                              in1=lnrep[:, 0, :], op=ALU.mult),
                          waits=[("q_lnb", m8)])
                    m9 = em.op("vector",
                               lambda e, qc=qc, oc2=oc2: e.tensor_tensor(
                                   out=resid[:, oc2, qc * 512:(qc + 1) * 512],
                                   in0=resid[:, oc2, qc * 512:(qc + 1) * 512],
                                   in1=lnrep[:, 1, :], op=ALU.subtract))
                    m_out = em.op("gpsimd",
                                  lambda e, qc=qc, oc2=oc2: e.dma_start(
                                      out=outre[:, oc2, qc * 512:(qc + 1) * 512],
                                      in_=resid[:, oc2, qc * 512:(qc + 1) * 512]),
                                  waits=[("vector", m9)], dma="out")
        em.wait("gpsimd", "q_out", m_out)


def build_launch_D():
    nc = bass.Bass("TRN2", target_bir_lowering=False)
    DR = mk_dr_D(nc)
    DR["stpat"] = nc.dram_tensor("stpat", [128, 130], F32, kind="ExternalInput").ap()
    DR["consts"] = nc.dram_tensor("consts", [128, 68], F32, kind="ExternalInput").ap()
    es = contextlib.ExitStack()
    with es:
        ar = es.enter_context(Arena(nc, cols=48850))
        P = {"PA": es.enter_context(nc.psum_tensor("PA", [128, 2048], F32))[:, :],
             "PB": es.enter_context(nc.psum_tensor("PB", [65, 2048], F32))[:, :]}
        em = Emitter(nc, es)
        em.prefix = "Dp"
        blk = es.enter_context(nc.Block())
        emit_D(nc, em, ar, P, DR)
        em.build(blk)
    return nc


def host_launch_D(yz, x2, inputs):
    stpat = np.zeros((128, 130), np.float32)
    stpat[:, 0] = 1.0
    stpat[:, 129] = 1.0
    consts = np.zeros((128, 68), np.float32)
    consts[:, 0] = 1.0
    consts[:, 1] = EPS
    consts[:, 4:68] = 1.0
    yzbf = yz if yz.dtype == BF else yz
    in_maps = []
    for c in range(8):
        b, hf = divmod(c, 2)
        in_maps.append({
            "yz_in": np.ascontiguousarray(yz[b][:, hf * TOK:(hf + 1) * TOK]).astype(BF),
            "W_out": np.asarray(inputs["L1_attn_W_out"]),
            "x2h_fm": np.ascontiguousarray(_fm(x2[b])[:, hf * TOK:(hf + 1) * TOK]),
            "stpat": stpat, "consts": consts,
        })
    nc = build_launch_D()
    res = run_bass_kernel_spmd(nc, in_maps, core_ids=list(range(8)))
    x3 = np.empty((B, S, D), np.float32)
    for c in range(8):
        b, hf = divmod(c, 2)
        x3[b, hf * TOK:(hf + 1) * TOK, :] = res.results[c]["x3_fm"].T
    return x3


def mk_dr_E(nc, fused=False):
    ki = "ExternalInput"
    ko = "ExternalOutput"
    d = {}
    if not fused:
        d["x3_fm"] = nc.dram_tensor("x3_fm", [D, TOK], F32, kind=ki).ap()
    for nm, shape in (("gw", [128, 8, 16]), ("be1", [128, 16, 4]), ("be2", [16, D]),
                      ("be3", [128, 16, 4]), ("ws1", [D, D]), ("ws2", [D, D]),
                      ("ws3", [D, D]), ("bs1", [128, 8]), ("bs2", [128, 8]),
                      ("bs3", [128, 8]), ("identf", [128, 128])):
        d[nm] = nc.dram_tensor(nm, shape, F32, kind=ki).ap()
    d["gb"] = nc.dram_tensor("gb", [16], F32, kind=ki).ap()
    d["We1"] = nc.dram_tensor("We1", [16, D, 512], F32, kind=ki).ap()
    d["We2"] = nc.dram_tensor("We2", [16, 512, D], F32, kind=ki).ap()
    d["We3"] = nc.dram_tensor("We3", [16, D, 512], F32, kind=ki).ap()
    d["x4_fm"] = nc.dram_tensor("x4_fm", [D, TOK], F32, kind=ko).ap()
    d["s_hbmE"] = nc.dram_tensor("s_hbmE", [2, 512], F32, kind="Internal").ap()
    d["g_hbm"] = nc.dram_tensor("g_hbm", [16], F32, kind="Internal").ap()
    d["c_hbm"] = nc.dram_tensor("c_hbm", [16, 1024], F32, kind="Internal").ap()
    return d


def emit_E(nc, em, ar, P, DR):
    dx3 = DR["x3_fm"]
    dgw, dgb = DR["gw"], DR["gb"]
    dwe1, dwe2, dwe3 = DR["We1"], DR["We2"], DR["We3"]
    dbe1, dbe2, dbe3 = DR["be1"], DR["be2"], DR["be3"]
    dws1, dws2, dws3 = DR["ws1"], DR["ws2"], DR["ws3"]
    dbs1, dbs2, dbs3 = DR["bs1"], DR["bs2"], DR["bs3"]
    dst, dcst, didf = DR["stpat"], DR["consts"], DR["identf"]
    dout = DR["x4_fm"]
    s_hbm, g_hbm, c_hbm = DR["s_hbmE"], DR["g_hbm"], DR["c_hbm"]

    HTOK = 1024
    if True:
        x3h = ar.view(0, [128, NCH, HTOK], F32)
        x3bf = ar.view(8192, [128, NCH, HTOK], BF16)
        yacc = ar.view(12288, [128, NCH, HTOK], F32)
        webf = ar.view(20480, [128, 24576], BF16)
        wstage = ar.view(32768, [128, 4096], F32)
        chbuf = ar.view(36864, [128, 8, 512], BF16)
        ssil = ar.view(38912, [128, 512], F32)
        chf = ar.view(39424, [128, 512], F32)
        crep = ar.view(39936, [128, 2, 512], F32)
        gw = ar.view(40960, [128, 8, 16], F32)
        gbb = ar.view(41088, [128, 16], F32)
        lgs = ar.view(41104, [16, HTOK], F32)
        ltm = ar.view(42128, [128, 8, 16], F32)
        gsc1 = ar.view(42256, [128, 8, 16], F32)
        gsc2 = ar.view(42384, [128, 8, 16], F32)
        gred = ar.view(42512, [128, 8, 4], F32)
        gred2 = ar.view(42544, [128, 8, 4], F32)
        grow = ar.view(42576, [128, 8], F32)
        grow2 = ar.view(42584, [128, 8], F32)
        comb = ar.view(42592, [128, 8, 16], F32)
        combf_bf = ar.view(42128, [16, 1024], BF16)
        combf = ar.view(42720, [16, HTOK], F32)
        be1c = ar.view(43744, [128, 16, 4], F32)
        be3c = ar.view(43808, [128, 16, 4], F32)
        be2r = ar.view(43872, [16, D], BF16)
        bs1c = ar.view(44384, [128, 8], F32)
        bs2c = ar.view(44392, [128, 8], F32)
        bs3c = ar.view(44400, [128, 8], F32)
        stp = ar.view(44408, [128, 130], F32)
        cst = ar.view(44538, [128, 132], F32)
        idf = ar.view(44670, [128, 128], F32)
        resid = ar.view(44798, [128, 512], F32)
        sqt = ar.view(45310, [128, 512], F32)
        mrow = ar.view(45822, [1, 512], F32)
        s2row = ar.view(46334, [1, 512], F32)
        msq = ar.view(46846, [1, 512], F32)
        varr = ar.view(47358, [1, 512], F32)
        rstd_t = ar.view(47870, [1, 512], F32)
        mrs_t = ar.view(41104, [1, 512], F32)
        lnrep = ar.view(36864, [128, 2, 512], F32)

        PH1, PH3, PYO, PST = P["PH1"], P["PH3"], P["PYO"], P["PST"]
        x3re = dx3.rearrange("(c p) t -> p c t", p=128)
        outre = dout.rearrange("(c p) t -> p c t", p=128)

        for (d_, s_) in ((gw, dgw), (be1c, dbe1), (be3c, dbe3),
                         (bs1c, dbs1), (bs2c, dbs2), (bs3c, dbs3),
                         (stp, dst), (cst, dcst), (idf, didf)):
            em.op("sync", lambda e, d_=d_, s_=s_: e.dma_start(out=d_, in_=s_), dma="eld")
        m_gw = em.op("sync", lambda e: e.dma_start(out=g_hbm, in_=dgb), dma="eld")
        m_gbl = em.op("sync", lambda e: e.dma_start(
            out=gbb, in_=g_hbm.unsqueeze(0).broadcast_to([128, 16])),
            waits=[("q_eld", m_gw)], dma="eld")
        m_b2l = em.op("sync", lambda e: e.dma_start(out=combf, in_=dbe2), dma="eld")
        ELD_ALL = em.qcnt("eld")
        m_b2c = em.op("vector", lambda e: e.tensor_copy(out=be2r, in_=combf),
                      waits=[("q_eld", ELD_ALL)])

        ws1v = webf[:, 0:8192].rearrange("p (a b) -> p a b", b=1024)
        ws3v = webf[:, 8192:16384].rearrange("p (a b) -> p a b", b=1024)
        ws2v = webf[:, 16384:24576].rearrange("p (a b) -> p a b", b=1024)

        def we_views(sl):
            base = sl * 12288
            return (webf[:, base:base + 4096].rearrange("p (a b) -> p a b", b=512),
                    webf[:, base + 4096:base + 8192].rearrange("p (a b) -> p a b", b=512),
                    webf[:, base + 8192:base + 12288].rearrange("p (a b) -> p a b", b=1024))

        m_x3 = {}
        m_cast = {}
        m_tpe = {}
        m_gate = {}
        m_sil = {}
        m_chf = {}
        m_ch = {}
        m_yom = {}
        m_ya = {}
        m_out = 0
        m_lnfree = 0
        m_res = {}
        m_wc = {}
        m_crep = {}
        m_chbm = {}
        wslot = [0]
        lastchf = [0]
        lastya = [0]

        wsl = [0]

        def _stage_piece(srcv, dstv, extra_waits=()):
            s_ = wsl[0] & 1
            wsl[0] += 1
            ncols = 1
            for d_ in srcv.shape[1:]:
                ncols *= d_
            stg = wstage.rearrange("p (h n) -> p h n", h=2)[:, s_, 0:ncols]
            if len(srcv.shape) == 3:
                stg = stg.rearrange("p (a b) -> p a b", b=srcv.shape[2])
            wl = [("vector", m_wc.get(s_, 0))] + list(extra_waits)
            m_l = em.op("sync",
                        lambda e, stg=stg, srcv=srcv: e.dma_start(out=stg, in_=srcv),
                        waits=wl, dma="ew%d" % s_)
            m_wc[s_] = em.op("vector",
                             lambda e, stg=stg, dstv=dstv: e.tensor_copy(out=dstv, in_=stg),
                             waits=[("q_ew%d" % s_, m_l)])
            return m_wc[s_]

        def load_shared(hp):
            pieces = [(dws1, ws1v), (dws3, ws3v), (dws2, ws2v)]
            m_last = 0
            for pi, (dsrc, dstv) in enumerate(pieces):
                srcre = dsrc.rearrange("(c p) i -> p c i", p=128)
                for hf in range(4):
                    ew = []
                    if pi == 0 and hf == 0:
                        ew.append(("tensor", (lambda h=hp: m_yom.get((h - 1, 15, 7), 0))))
                    m_last = _stage_piece(srcre[:, hf * 2:hf * 2 + 2, :],
                                          dstv[:, hf * 2:hf * 2 + 2, :], ew)
            return m_last

        def load_expert(hp, ei):
            sl = wslot[0]
            wslot[0] ^= 1
            w1v, w3v, w2v = we_views(sl)
            pieces = [(dwe1[ei].rearrange("(c p) i -> p c i", p=128), w1v),
                      (dwe3[ei].rearrange("(c p) i -> p c i", p=128), w3v),
                      (dwe2[ei].rearrange("(c p) i -> p c i", p=128), w2v)]
            m_last = 0
            for pi, (srcre, dstv) in enumerate(pieces):
                nch_ = dstv.shape[1]
                for hf in range(2):
                    ew = []
                    if pi == 0 and hf == 0:
                        ew.append(("tensor", (lambda h=hp, i=ei: m_yom.get((h, i - 2, 7),
                                              m_yom.get((h - 1, 15, 7), 0) if i - 2 < 0 else 0))))
                    h0 = hf * (nch_ // 2)
                    h1 = (hf + 1) * (nch_ // 2)
                    m_last = _stage_piece(srcre[:, h0:h1, :], dstv[:, h0:h1, :], ew)
            return sl, m_last

        V = "vector"
        for hp in range(2):
            hcol = hp * HTOK
            m_x3[hp] = em.op("sync",
                             lambda e, hcol=hcol: e.dma_start(
                                 out=x3h, in_=x3re[:, :, hcol:hcol + HTOK]),
                             waits=[("q_out", m_out),
                                    ("tensor", (lambda h=hp: m_res.get((h - 1, "laststat"), 0)))],
                             dma="ex3")
            m_cast[hp] = em.op(V, lambda e: e.tensor_copy(out=x3bf, in_=x3h),
                               waits=[("q_ex3", m_x3[hp])])
            def mk_lg(e):
                inst = None
                for t2 in range(2):
                    for kc in range(8):
                        inst = e.matmul(PH1[0:16, t2, :], gw[:, kc, :],
                                        x3h[:, kc, t2 * 512:(t2 + 1) * 512],
                                        start=(kc == 0), stop=(kc == 7))
                return inst
            m_lgm = em.op("tensor", mk_lg,
                          waits=[("q_ex3", m_x3[hp]), ("q_eld", ELD_ALL if hp == 0 else 0),
                                 ("vector", (lambda h=hp: m_res.get((h - 1, "lastln"), 0))),
                                 ("scalar", (lambda h=hp: m_res.get((h - 1, "lastsq"), 0)))])
            m_lge = em.op(V, lambda e: e.tensor_copy(
                out=lgs.rearrange("p (a b) -> p a b", a=2), in_=PH1[0:16, :, :]),
                waits=[("tensor", m_lgm), ("vector", m_b2c if hp == 0 else 0)])
            for tl in range(8):
                m_tp = em.op("tensor",
                             lambda e, tl=tl: e.transpose(
                                 PH3[:, 0, 0:16], lgs[:, tl * 128:(tl + 1) * 128],
                                 idf[0:16, 0:16]),
                             waits=[("vector", m_lge if tl == 0 else 0),
                                    ("vector", m_tpe.get((hp, tl - 1), 0))])
                m_tpe[(hp, tl)] = em.op(V,
                                        lambda e, tl=tl: e.tensor_copy(
                                            out=ltm[:, tl, :], in_=PH3[:, 0, 0:16]),
                                        waits=[("tensor", m_tp)])
            g4 = gsc2.rearrange("p a (g f) -> p (a g) f", g=4)
            gredf = gred.rearrange("p a b -> p (a b)")
            gred2f = gred2.rearrange("p a b -> p (a b)")
            comb4 = comb.rearrange("p a (g f) -> p (a g) f", g=4)
            em.op(V, lambda e: e.tensor_reduce(out=grow, in_=ltm, axis=AX.X, op=ALU.max),
                  waits=[("vector", m_tpe[(hp, 7)])])
            em.op(V, lambda e: e.tensor_tensor(
                out=gsc1, in0=ltm, in1=grow.unsqueeze(2).broadcast_to([128, 8, 16]),
                op=ALU.subtract))
            m_ex = em.op("scalar", lambda e: e.activation(
                out=gsc1.rearrange("p a b -> p (a b)"),
                in_=gsc1.rearrange("p a b -> p (a b)"), func=AF.Exp),
                waits=[("vector", em.cnt["vector"])])
            em.op(V, lambda e: e.tensor_reduce(out=grow, in_=gsc1, axis=AX.X, op=ALU.add),
                  waits=[("scalar", m_ex)])
            em.op(V, lambda e: e.reciprocal(out=grow2, in_=grow))
            em.op(V, lambda e: e.tensor_tensor(
                out=gsc1, in0=gsc1, in1=grow2.unsqueeze(2).broadcast_to([128, 8, 16]),
                op=ALU.mult))
            em.op(V, lambda e: e.tensor_tensor(
                out=gsc2, in0=gsc1, in1=gbb.unsqueeze(1).broadcast_to([128, 8, 16]),
                op=ALU.add))
            em.op(V, lambda e: e.tensor_reduce(out=gredf, in_=g4, axis=AX.X, op=ALU.max))
            em.op(V, lambda e: e.tensor_tensor(
                out=comb4, in0=g4, in1=gredf.unsqueeze(2).broadcast_to([128, 32, 4]),
                op=ALU.is_ge))
            em.op(V, lambda e: e.tensor_scalar(
                out=comb.rearrange("p a b -> p (a b)"), in0=comb.rearrange("p a b -> p (a b)"),
                scalar1=-1e30, scalar2=None, op0=ALU.mult))
            em.op(V, lambda e: e.tensor_tensor(out=comb4, in0=comb4, in1=g4, op=ALU.add))
            em.op(V, lambda e: e.tensor_reduce(out=gred2f, in_=comb4, axis=AX.X, op=ALU.max))
            em.op(V, lambda e: e.tensor_tensor(out=gredf, in0=gredf, in1=gred2f, op=ALU.add))
            em.op(V, lambda e: e.tensor_reduce(out=grow, in_=gred, axis=AX.X, op=ALU.max))
            em.op(V, lambda e: e.tensor_tensor(
                out=gred2, in0=gred, in1=grow.unsqueeze(2).broadcast_to([128, 8, 4]),
                op=ALU.is_ge))
            em.op(V, lambda e: e.tensor_scalar(out=gred2f, in0=gred2f, scalar1=-1e30,
                                               scalar2=None, op0=ALU.mult))
            em.op(V, lambda e: e.tensor_tensor(out=gred2, in0=gred2, in1=gred, op=ALU.add))
            em.op(V, lambda e: e.tensor_reduce(out=grow2, in_=gred2, axis=AX.X, op=ALU.max))
            em.op(V, lambda e: e.tensor_tensor(
                out=gred2, in0=gred, in1=grow2.unsqueeze(2).broadcast_to([128, 8, 4]),
                op=ALU.is_ge))
            em.op(V, lambda e: e.tensor_scalar(out=gred2f, in0=gred2f, scalar1=1e30,
                                               scalar2=-1e30, op0=ALU.mult, op1=ALU.add))
            em.op(V, lambda e: e.tensor_tensor(
                out=g4, in0=g4, in1=gred2f.unsqueeze(2).broadcast_to([128, 32, 4]),
                op=ALU.add))
            em.op(V, lambda e: e.tensor_reduce(out=grow, in_=gsc2, axis=AX.X, op=ALU.max))
            em.op(V, lambda e: e.tensor_tensor(
                out=comb, in0=gsc2, in1=grow.unsqueeze(2).broadcast_to([128, 8, 16]),
                op=ALU.is_ge))
            em.op(V, lambda e: e.tensor_scalar(
                out=comb.rearrange("p a b -> p (a b)"), in0=comb.rearrange("p a b -> p (a b)"),
                scalar1=-1e30, scalar2=None, op0=ALU.mult))
            em.op(V, lambda e: e.tensor_tensor(out=comb, in0=comb, in1=gsc2, op=ALU.add))
            em.op(V, lambda e: e.tensor_reduce(out=grow2, in_=comb, axis=AX.X, op=ALU.max))
            em.op(V, lambda e: e.tensor_tensor(
                out=comb, in0=gsc2, in1=grow2.unsqueeze(2).broadcast_to([128, 8, 16]),
                op=ALU.is_ge))
            m_cmb = em.op(V, lambda e: e.tensor_tensor(out=comb, in0=comb, in1=gsc1,
                                                       op=ALU.mult))
            for tl in range(8):
                m_tp = em.op("tensor",
                             lambda e, tl=tl: e.transpose(
                                 PH3[0:16, 0, 0:128], comb[:, tl, :], idf),
                             waits=[("vector", m_cmb if tl == 0 else 0),
                                    ("vector", m_gate.get((hp, tl - 1), 0))])
                m_gate[(hp, tl)] = em.op(V,
                                         lambda e, tl=tl: e.tensor_copy(
                                             out=combf[:, tl * 128:(tl + 1) * 128],
                                             in_=PH3[0:16, 0, 0:128]),
                                         waits=[("tensor", m_tp)])
            m_chbm[hp] = em.op("gpsimd", lambda e: e.dma_start(out=c_hbm, in_=combf),
                               waits=[("vector", m_gate[(hp, 7)])], dma="cmb")
            m_cbf = em.op(V, lambda e: e.tensor_copy(out=combf_bf, in_=combf))
            for ei in range(-1, 16):
                if ei < 0:
                    m_w = load_shared(hp)
                    nic, wid = 8, 1024
                else:
                    sl, m_w = load_expert(hp, ei)
                    w1v, w3v, w2v = we_views(sl)
                    nic, wid = 4, 512
                    for tc in range(2):
                        m_crep[(hp, ei, tc)] = em.op(
                            "gpsimd",
                            lambda e, ei=ei, tc=tc: e.dma_start(
                                out=crep[:, tc, :],
                                in_=c_hbm[ei, tc * 512:(tc + 1) * 512]
                                .unsqueeze(0).broadcast_to([128, 512])),
                            waits=[("q_cmb", m_chbm[hp]),
                                   ("vector", (lambda h=hp, i=ei, t=tc: m_ch.get((h, i - 1, t, 3), 0)))],
                            dma="crep%d" % tc)
                for tc in range(2):
                    t0 = tc * 512
                    for ic in range(nic):
                        icp = ic % 2
                        w1 = ws1v if ei < 0 else w1v
                        w3 = ws3v if ei < 0 else w3v

                        def mk1(e, w1=w1, ic=ic, icp=icp, t0=t0):
                            inst = None
                            for kc in range(8):
                                inst = e.matmul(PH1[:, icp, :],
                                                w1[:, kc, ic * 128:(ic + 1) * 128]
                                                if w1 is not ws1v else w1[:, kc, ic * 128:(ic + 1) * 128],
                                                x3bf[:, kc, t0:t0 + 512],
                                                start=(kc == 0), stop=(kc == 7))
                            return inst
                        h1w = [("vector", m_w if (tc == 0 and ic == 0) else 0),
                               ("vector", m_cast[hp] if (ei < 0 and tc == 0 and ic == 0) else 0)]
                        if ic >= 2:
                            h1w.append(("scalar", m_sil[(hp, ei, tc, ic - 2)]))
                        else:
                            h1w.append(("vector", lastchf[0]))
                        m_h1 = em.op("tensor", mk1, waits=h1w)

                        def mk3(e, w3=w3, ic=ic, icp=icp, t0=t0):
                            inst = None
                            for kc in range(8):
                                inst = e.matmul(PH3[:, icp, :],
                                                w3[:, kc, ic * 128:(ic + 1) * 128],
                                                x3bf[:, kc, t0:t0 + 512],
                                                start=(kc == 0), stop=(kc == 7))
                            return inst
                        m_h3 = em.op("tensor", mk3,
                                     waits=[("vector", m_chf.get((hp, ei, tc, ic - 2), lastchf[0]))])
                        bcol = (bs1c[:, hp * 0:1] if False else bs1c[:, 0:1]) if ei < 0 else be1c[:, ei, ic:ic + 1]
                        bcol1 = bs1c[:, ic // 1:ic // 1 + 1] if ei < 0 else be1c[:, ei, ic:ic + 1]
                        m_sil[(hp, ei, tc, ic)] = em.op(
                            "scalar",
                            lambda e, ei=ei, ic=ic, icp=icp: e.activation(
                                out=ssil, in_=PH1[:, icp, :], func=AF.Silu,
                                bias=(bs1c[:, ic:ic + 1] if ei < 0 else be1c[:, ei, ic:ic + 1])),
                            waits=[("tensor", m_h1),
                                   ("vector", m_ch.get((hp, ei, tc, ic - 1), lastchf[0]))])
                        m_chf[(hp, ei, tc, ic)] = em.op(
                            V,
                            lambda e, ei=ei, ic=ic, icp=icp: e.scalar_tensor_tensor(
                                out=chf, in0=PH3[:, icp, :],
                                scalar=(bs3c[:, ic:ic + 1] if ei < 0 else be3c[:, ei, ic:ic + 1]),
                                in1=ssil, op0=ALU.add, op1=ALU.mult),
                            waits=[("tensor", m_h3),
                                   ("scalar", m_sil[(hp, ei, tc, ic)])])
                        if ei < 0:
                            m_ch[(hp, ei, tc, ic)] = em.op(
                                V, lambda e, ic=ic: e.tensor_copy(out=chbuf[:, ic, :], in_=chf))
                        else:
                            m_ch[(hp, ei, tc, ic)] = em.op(
                                V, lambda e, ic=ic, tc=tc: e.tensor_tensor(
                                    out=chbuf[:, ic, :], in0=chf, in1=crep[:, tc, :],
                                    op=ALU.mult),
                                waits=[("q_crep%d" % tc, m_crep[(hp, ei, tc)])])
                    lastchf[0] = m_ch[(hp, ei, tc, nic - 1)]
                    for oc in range(8):
                        ocp = oc % 2
                        w2 = ws2v if ei < 0 else w2v

                        def mk2(e, w2=w2, oc=oc, ocp=ocp, nic=nic, ei=ei, tc=tc):
                            inst = None
                            for ic in range(nic):
                                inst = e.matmul(PYO[:, ocp, :],
                                                w2[:, ic, oc * 128:(oc + 1) * 128],
                                                chbuf[:, ic, :],
                                                start=(ic == 0), stop=(ei < 0 or ic < nic - 1) and (ic == nic - 1),
                                                skip_group_check=True)
                            if ei >= 0:
                                inst = e.matmul(PYO[:, ocp, :], be2r[:, oc * 128:(oc + 1) * 128],
                                                combf_bf[:, tc * 512:(tc + 1) * 512],
                                                start=False, stop=True, skip_group_check=True)
                            return inst
                        m_yom[(hp, ei, oc)] = em.op(
                            "tensor", mk2,
                            waits=[("vector", m_ch[(hp, ei, tc, nic - 1)]),
                                   ("vector", m_ya.get((hp, ei, tc, oc - 2), lastya[0]))])
                        if ei < 0:
                            m_ya[(hp, ei, tc, oc)] = em.op(
                                V, lambda e, oc=oc, ocp=ocp, t0=t0: e.tensor_scalar(
                                    out=yacc[:, oc, t0:t0 + 512], in0=PYO[:, ocp, :],
                                    scalar1=bs2c[:, oc:oc + 1], scalar2=None, op0=ALU.add),
                                waits=[("tensor", m_yom[(hp, ei, oc)])])
                        else:
                            m_ya[(hp, ei, tc, oc)] = em.op(
                                V, lambda e, oc=oc, ocp=ocp, t0=t0: e.tensor_tensor(
                                    out=yacc[:, oc, t0:t0 + 512],
                                    in0=yacc[:, oc, t0:t0 + 512],
                                    in1=PYO[:, ocp, :], op=ALU.add),
                                waits=[("tensor", m_yom[(hp, ei, oc)])])
                        if oc == 7:
                            lastya[0] = m_ya[(hp, ei, tc, 7)]
            for tc in range(2):
                t0 = tc * 512
                m_stat_l = 0
                for oc in range(8):
                    m_r0 = em.op(V,
                                 lambda e, oc=oc, t0=t0: e.tensor_tensor(
                                     out=resid, in0=x3h[:, oc, t0:t0 + 512],
                                     in1=yacc[:, oc, t0:t0 + 512], op=ALU.add),
                                 waits=[("vector", m_ya[(hp, 15, tc, 7)]),
                                        ("tensor", m_stat_l)])
                    m_sq = em.op("scalar",
                                 lambda e: e.activation(out=sqt, in_=resid, func=AF.Square),
                                 waits=[("vector", m_r0)])
                    def mk_stat(e, oc=oc):
                        e.matmul(PST[:, :], stp[:, 0:65], resid,
                                 start=(oc == 0), stop=False, skip_group_check=True)
                        return e.matmul(PST[:, :], stp[:, 65:130], sqt,
                                        start=False, stop=(oc == 7), skip_group_check=True)
                    m_stat_l = em.op("tensor", mk_stat,
                                     waits=[("scalar", m_sq),
                                            ("vector", m_lnfree if oc == 0 else 0)])
                m_res[(hp, "laststat")] = m_stat_l
                m1 = em.op(V, lambda e: e.tensor_scalar(out=mrow, in0=PST[0:1, :],
                                                        scalar1=1.0 / D, scalar2=None,
                                                        op0=ALU.mult),
                           waits=[("tensor", m_stat_l)])
                m_lnfree = em.op(V, lambda e: e.tensor_scalar(out=s2row, in0=PST[64:65, :],
                                                              scalar1=1.0 / D, scalar2=None,
                                                              op0=ALU.mult))
                m2 = em.op("scalar", lambda e: e.activation(out=msq, in_=mrow, func=AF.Square),
                           waits=[("vector", m_lnfree)])
                m3 = em.op(V, lambda e: e.tensor_tensor(out=varr, in0=s2row, in1=msq,
                                                        op=ALU.subtract),
                           waits=[("scalar", m2)])
                m4 = em.op("scalar", lambda e: e.activation(out=msq, in_=varr, func=AF.Ln,
                                                            bias=cst[0:1, 1:2]),
                           waits=[("vector", m3)])
                m5 = em.op("scalar", lambda e: e.activation(out=rstd_t, in_=msq, func=AF.Exp,
                                                            scale=-0.5))
                m6 = em.op(V, lambda e: e.tensor_tensor(out=mrs_t, in0=mrow, in1=rstd_t,
                                                        op=ALU.mult),
                           waits=[("scalar", m5)])
                m7 = em.op("gpsimd", lambda e: e.dma_start(out=s_hbm[0, :], in_=rstd_t[0:1, :]),
                           waits=[("vector", m6)], dma="lnb")
                m7 = em.op("gpsimd", lambda e: e.dma_start(out=s_hbm[1, :], in_=mrs_t[0:1, :]),
                           dma="lnb")
                m8 = em.op("gpsimd",
                           lambda e: e.dma_start(out=lnrep[:, 0, :],
                                                 in_=s_hbm[0:1, :].broadcast_to([128, 512])),
                           waits=[("q_lnb", m7),
                                  ("vector", m_ch[(hp, 15, 1, 3)])], dma="lnb")
                m8 = em.op("gpsimd",
                           lambda e: e.dma_start(out=lnrep[:, 1, :],
                                                 in_=s_hbm[1:2, :].broadcast_to([128, 512])),
                           dma="lnb")
                m_sq2 = 0
                for oc2 in range(8):
                    m_r1 = em.op(V,
                                 lambda e, oc2=oc2, t0=t0: e.tensor_tensor(
                                     out=yacc[:, oc2, t0:t0 + 512],
                                     in0=x3h[:, oc2, t0:t0 + 512],
                                     in1=yacc[:, oc2, t0:t0 + 512], op=ALU.add),
                                 waits=[("q_lnb", m8)])
                    m_r1 = em.op(V,
                                 lambda e, oc2=oc2, t0=t0: e.tensor_tensor(
                                     out=yacc[:, oc2, t0:t0 + 512],
                                     in0=yacc[:, oc2, t0:t0 + 512],
                                     in1=lnrep[:, 0, :], op=ALU.mult))
                    m_r1 = em.op(V,
                                 lambda e, oc2=oc2, t0=t0: e.tensor_tensor(
                                     out=yacc[:, oc2, t0:t0 + 512],
                                     in0=yacc[:, oc2, t0:t0 + 512],
                                     in1=lnrep[:, 1, :], op=ALU.subtract))
                    m_out = em.op("gpsimd",
                                  lambda e, oc2=oc2, t0=t0, hcol=hcol: e.dma_start(
                                      out=outre[:, oc2, hcol + t0:hcol + t0 + 512],
                                      in_=yacc[:, oc2, t0:t0 + 512]),
                                  waits=[("vector", m_r1)], dma="out")
                m_res[(hp, "lastln")] = m_r1
                m_res[(hp, "lastsq")] = em.cnt["scalar"]
        em.wait("gpsimd", "q_out", m_out)


def mk_P_E(PS1, PS2):
    return {
        "PH1": PS1[:, 0:1024].rearrange("p (a b) -> p a b", b=512),
        "PH3": PS1[:, 1024:2048].rearrange("p (a b) -> p a b", b=512),
        "PYO": PS2[:, 0:1024].rearrange("p (a b) -> p a b", b=512),
        "PST": PS2[0:65, 1024:1536],
    }


def build_launch_E():
    nc = bass.Bass("TRN2", target_bir_lowering=False)
    DR = mk_dr_E(nc)
    DR["stpat"] = nc.dram_tensor("stpat", [128, 130], F32, kind="ExternalInput").ap()
    DR["consts"] = nc.dram_tensor("consts", [128, 132], F32, kind="ExternalInput").ap()
    es = contextlib.ExitStack()
    with es:
        ar = es.enter_context(Arena(nc, cols=48850))
        PS1 = es.enter_context(nc.psum_tensor("PS1", [128, 2048], F32))[:, :]
        PS2 = es.enter_context(nc.psum_tensor("PS2", [128, 2048], F32))[:, :]
        P = mk_P_E(PS1, PS2)
        em = Emitter(nc, es)
        em.prefix = "E"
        blk = es.enter_context(nc.Block())
        emit_E(nc, em, ar, P, DR)
        em.build(blk)
    return nc


def host_launch_E(x3, inputs):
    stpat = np.zeros((128, 130), np.float32)
    stpat[:, 0] = 1.0
    stpat[:, 129] = 1.0
    consts = np.zeros((128, 132), np.float32)
    consts[:, 0] = 1.0
    consts[:, 1] = EPS
    consts[:, 4:132] = 1.0
    identf = np.eye(128, dtype=np.float32)
    gwl = np.ascontiguousarray(
        np.asarray(inputs["L1_ffn_gate_w"]).T.reshape(8, 128, 16).transpose(1, 0, 2))
    be1l = np.ascontiguousarray(
        np.asarray(inputs["L1_ffn_be1"]).reshape(16, 4, 128).transpose(2, 0, 1))
    be3l = np.ascontiguousarray(
        np.asarray(inputs["L1_ffn_be3"]).reshape(16, 4, 128).transpose(2, 0, 1))
    bs1l = np.ascontiguousarray(np.asarray(inputs["L1_ffn_bs1"]).reshape(8, 128).T)
    bs2l = np.ascontiguousarray(np.asarray(inputs["L1_ffn_bs2"]).reshape(8, 128).T)
    bs3l = np.ascontiguousarray(np.asarray(inputs["L1_ffn_bs3"]).reshape(8, 128).T)
    in_maps = []
    for c in range(8):
        b, hf = divmod(c, 2)
        in_maps.append({
            "x3_fm": np.ascontiguousarray(_fm(x3[b])[:, hf * TOK:(hf + 1) * TOK]),
            "gw": gwl, "gb": np.asarray(inputs["L1_ffn_gate_b"]),
            "We1": np.asarray(inputs["L1_ffn_We1"]),
            "We2": np.asarray(inputs["L1_ffn_We2"]),
            "We3": np.asarray(inputs["L1_ffn_We3"]),
            "be1": be1l, "be2": np.asarray(inputs["L1_ffn_be2"]), "be3": be3l,
            "ws1": np.asarray(inputs["L1_ffn_ws1"]),
            "ws2": np.asarray(inputs["L1_ffn_ws2"]),
            "ws3": np.asarray(inputs["L1_ffn_ws3"]),
            "bs1": bs1l, "bs2": bs2l, "bs3": bs3l,
            "stpat": stpat, "consts": consts, "identf": identf,
        })
    nc = build_launch_E()
    res = run_bass_kernel_spmd(nc, in_maps, core_ids=list(range(8)))
    x4 = np.empty((B, S, D), np.float32)
    for c in range(8):
        b, hf = divmod(c, 2)
        x4[b, hf * TOK:(hf + 1) * TOK, :] = res.results[c]["x4_fm"].T
    return x4


RG_PAIRS = [[0, 1], [2, 3], [4, 5], [6, 7]]
RG_ALL = [[0, 1, 2, 3, 4, 5, 6, 7]]


def emit_D2(nc, em, ar, P, DR):
    dyz = DR["yz_fm"]
    dwo = DR["W_outh"]
    dP = DR["P_mix"]
    dmix = DR["mix_own"]
    dx2 = DR["x2_own"]
    dst_, dcst = DR["stpat"], DR["consts"]
    dout = DR["x3_fm"]
    s_hbm = DR["s_hbmD"]
    if True:
        wout = ar.view(0, [128, 8, D], BF16)
        yzsb = ar.view(4096, [128, 8, S], BF16)
        wstage = ar.view(20480, [128, 4, D], F32)
        pstg = ar.view(24576, [128, 2, 512], F32)
        xstg = ar.view(25600, [128, 2, 512], F32)
        resid = ar.view(26624, [128, NCH, TOK], F32)
        sqt = ar.view(43008, [128, 512], F32)
        stp = ar.view(43520, [128, 130], F32)
        cst = ar.view(43650, [128, 68], F32)
        mrow = ar.view(43718, [1, 512], F32)
        s2row = ar.view(44230, [1, 512], F32)
        msq = ar.view(44742, [1, 512], F32)
        varr = ar.view(45254, [1, 512], F32)
        rstd_t = ar.view(45766, [1, 512], F32)
        mrs_t = ar.view(46278, [1, 512], F32)
        lnrep = ar.view(46790, [128, 2, 512], F32)
        PA, PB = P["PA"], P["PB"]

        yzre = dyz.rearrange("(c p) t -> p c t", p=128)
        wore = dwo.rearrange("(c p) o -> p c o", p=128)
        mixre = dmix.rearrange("(c p) t -> p c t", p=128)
        x2re = dx2.rearrange("(c p) t -> p c t", p=128)
        outre = dout.rearrange("(c p) t -> p c t", p=128)

        em.op("sync", lambda e: e.dma_start(out=stp, in_=dst_), dma="dld")
        em.op("sync", lambda e: e.dma_start(out=cst, in_=dcst), dma="dld")
        m_yzl = em.op("sync", lambda e: e.dma_start(out=yzsb, in_=yzre), dma="dld")
        m_cast = 0
        for hf in range(2):
            m_ld = em.op("sync",
                         lambda e, hf=hf: e.dma_start(
                             out=wstage, in_=wore[:, hf * 4:hf * 4 + 4, :]),
                         waits=[("vector", m_cast)], dma="dld")
            m_cast = em.op("vector",
                           lambda e, hf=hf: e.tensor_copy(
                               out=wout[:, hf * 4:hf * 4 + 4, :], in_=wstage),
                           waits=[("q_dld", m_ld)])
        m_mm = {}
        m_pe = {}
        m_ps = {}
        for j in range(64):
            hp, r_ = divmod(j, 32)
            qc, oc = divmod(r_, 8)
            m_mm[j] = em.op("tensor",
                            lambda e, hp=hp, qc=qc, oc=oc, j=j: [
                                e.matmul(PA[:, (j % 2) * 512:(j % 2) * 512 + 512],
                                         wout[:, kc, oc * 128:(oc + 1) * 128],
                                         yzsb[:, kc,
                                              hp * 2048 + qc * 512:hp * 2048 + qc * 512 + 512],
                                         start=(kc == 0), stop=(kc == 7))
                                for kc in range(8)][-1],
                            waits=[("vector", m_cast),
                                   ("q_dld", em.qcnt("dld") if j == 0 else 0),
                                   ("vector", m_pe.get(j - 2, 0))])
            m_pe[j] = em.op("vector",
                            lambda e, j=j: e.tensor_copy(
                                out=pstg[:, j % 2, :],
                                in_=PA[:, (j % 2) * 512:(j % 2) * 512 + 512]),
                            waits=[("tensor", m_mm[j]),
                                   ("q_dp%d" % (j % 2), m_ps.get(j - 2, 0))])
            m_ps[j] = em.op("sync",
                            lambda e, hp=hp, qc=qc, oc=oc, j=j: e.dma_start(
                                out=dP[hp].rearrange("(c p) t -> p c t", p=128)[
                                    :, oc, qc * 512:(qc + 1) * 512],
                                in_=pstg[:, j % 2, :]),
                            waits=[("vector", m_pe[j])], dma="dp%d" % (j % 2))
        m_rs = em.op("gpsimd",
                     lambda e: e.collective_compute(
                         "ReduceScatter", ALU.add, replica_groups=RG_PAIRS,
                         ins=[dP[:, :, :]], outs=[dmix[:, :]]),
                     waits=[("q_dp0", em.qcnt("dp0")), ("q_dp1", em.qcnt("dp1"))])
        m_mx = {}
        m_res = {}
        m_sq = {}
        m_stat = {}
        m_lnfree = 0
        m_out = 0
        for j in range(32):
            qc, oc = divmod(j, 8)
            m_mx[j] = em.op("sync",
                            lambda e, qc=qc, oc=oc: e.dma_start(
                                out=resid[:, oc, qc * 512:(qc + 1) * 512],
                                in_=mixre[:, oc, qc * 512:(qc + 1) * 512]),
                            waits=[("gpsimd", m_rs if j == 0 else 0)],
                            dma="dmx")
            m_xq = em.op("sync",
                         lambda e, qc=qc, oc=oc, j=j: e.dma_start(
                             out=xstg[:, j % 2, :],
                             in_=x2re[:, oc, qc * 512:(qc + 1) * 512]),
                         waits=[("vector", m_res.get(j - 2, 0))],
                         dma="dxq%d" % (j % 2))
            m_res[j] = em.op("vector",
                             lambda e, qc=qc, oc=oc, j=j: e.tensor_tensor(
                                 out=resid[:, oc, qc * 512:(qc + 1) * 512],
                                 in0=resid[:, oc, qc * 512:(qc + 1) * 512],
                                 in1=xstg[:, j % 2, :], op=ALU.add),
                             waits=[("q_dmx", m_mx[j]),
                                    ("q_dxq%d" % (j % 2), m_xq)])
            m_sq[j] = em.op("scalar",
                            lambda e, qc=qc, oc=oc: e.activation(
                                out=sqt, in_=resid[:, oc, qc * 512:(qc + 1) * 512],
                                func=AF.Square),
                            waits=[("vector", m_res[j]),
                                   ("tensor", m_stat.get(j - 1, 0))])

            def mk_stat(e, qc=qc, oc=oc):
                e.matmul(PB[:, 0:512], stp[:, 0:65],
                         resid[:, oc, qc * 512:(qc + 1) * 512],
                         start=(oc == 0), stop=False, skip_group_check=True)
                return e.matmul(PB[:, 0:512], stp[:, 65:130], sqt,
                                start=False, stop=(oc == 7), skip_group_check=True)
            m_stat[j] = em.op("tensor", mk_stat,
                              waits=[("scalar", m_sq[j]),
                                     ("vector", m_lnfree if oc == 0 else 0)])
            if oc == 7:
                m1 = em.op("vector",
                           lambda e: e.tensor_scalar(out=mrow, in0=PB[0:1, 0:512],
                                                     scalar1=1.0 / D, scalar2=None,
                                                     op0=ALU.mult),
                           waits=[("tensor", m_stat[j])])
                m_lnfree = em.op("vector",
                                 lambda e: e.tensor_scalar(out=s2row, in0=PB[64:65, 0:512],
                                                           scalar1=1.0 / D, scalar2=None,
                                                           op0=ALU.mult))
                m2 = em.op("scalar", lambda e: e.activation(out=msq, in_=mrow,
                                                            func=AF.Square),
                           waits=[("vector", m_lnfree)])
                m3 = em.op("vector",
                           lambda e: e.tensor_tensor(out=varr, in0=s2row, in1=msq,
                                                     op=ALU.subtract),
                           waits=[("scalar", m2)])
                m4 = em.op("scalar",
                           lambda e: e.activation(out=msq, in_=varr, func=AF.Ln,
                                                  bias=cst[0:1, 1:2]),
                           waits=[("vector", m3)])
                m5 = em.op("scalar",
                           lambda e: e.activation(out=rstd_t, in_=msq, func=AF.Exp,
                                                  scale=-0.5))
                m6 = em.op("vector",
                           lambda e: e.tensor_tensor(out=mrs_t, in0=mrow, in1=rstd_t,
                                                     op=ALU.mult),
                           waits=[("scalar", m5)])
                m7 = em.op("gpsimd", lambda e: e.dma_start(out=s_hbm[0, :], in_=rstd_t[0:1, :]),
                           waits=[("vector", m6)], dma="lnb")
                m7 = em.op("gpsimd", lambda e: e.dma_start(out=s_hbm[1, :], in_=mrs_t[0:1, :]),
                           dma="lnb")
                m8 = em.op("gpsimd",
                           lambda e: e.dma_start(out=lnrep[:, 0, :],
                                                 in_=s_hbm[0:1, :].broadcast_to([128, 512])),
                           waits=[("q_lnb", m7)], dma="lnb")
                m8 = em.op("gpsimd",
                           lambda e: e.dma_start(out=lnrep[:, 1, :],
                                                 in_=s_hbm[1:2, :].broadcast_to([128, 512])),
                           dma="lnb")
                for oc2 in range(8):
                    em.op("vector",
                          lambda e, qc=qc, oc2=oc2: e.tensor_tensor(
                              out=resid[:, oc2, qc * 512:(qc + 1) * 512],
                              in0=resid[:, oc2, qc * 512:(qc + 1) * 512],
                              in1=lnrep[:, 0, :], op=ALU.mult),
                          waits=[("q_lnb", m8)])
                    m9 = em.op("vector",
                               lambda e, qc=qc, oc2=oc2: e.tensor_tensor(
                                   out=resid[:, oc2, qc * 512:(qc + 1) * 512],
                                   in0=resid[:, oc2, qc * 512:(qc + 1) * 512],
                                   in1=lnrep[:, 1, :], op=ALU.subtract))
                    m_out = em.op("gpsimd",
                                  lambda e, qc=qc, oc2=oc2: e.dma_start(
                                      out=outre[:, oc2, qc * 512:(qc + 1) * 512],
                                      in_=resid[:, oc2, qc * 512:(qc + 1) * 512]),
                                  waits=[("vector", m9)], dma="out")
        em.wait("gpsimd", "q_out", m_out)


def build_fused(upto=6, dbg=False):
    nc = bass.Bass("TRN2", target_bir_lowering=False)
    ki = "ExternalInput"
    DR = {}
    DBG = ("x1_tm", "x2_own", "yz_fm", "x3_fm")

    def dt(name, shape, dtype=F32, kind=ki):
        if dbg and kind == "Internal" and name in DBG:
            kind = "ExternalOutput"
        return nc.dram_tensor(name, shape, dtype, kind=kind).ap()

    DR["stpat"] = dt("stpat", [128, 130])
    consts = dt("consts", [128, 132])
    DR["identf"] = dt("identf", [128, 128])
    DR["identb"] = dt("identb", [128, 128], BF16)
    for nm in ("Wq", "Wk", "Wv", "Wo"):
        DR[nm] = dt(nm, [D, D])
    DR["x_fm"] = dt("x_fm", [D, S])
    DR["xq_fm"] = dt("xq_fm", [D, TOK])
    DR["bo"] = dt("bo", [128, NCH])
    DR["x1_fm"] = dt("x1_fm", [D, TOK], F32, "Internal")
    DR["k_hbm"] = dt("k_hbm", [D, S], BF16, "Internal")
    DR["v_hbm"] = dt("v_hbm", [S, D], BF16, "Internal")
    DR["s_hbm"] = dt("s_hbmA", [2, 512], F32, "Internal")
    DR["r_hbm"] = dt("r_hbm", [TOK], F32, "Internal")
    DR["x1_tm"] = dt("x1_tm", [TOK, D], F32, "Internal")
    DR["wg"] = dt("wg", [384, D])
    DR["wu"] = dt("wu", [384, D])
    DR["wd"] = dt("wd", [128, 2816])
    DR["cc_src"] = dt("cc_src", [1146880], BF16, "Internal")
    DR["cc_w"] = dt("cc_w", [8, 1146880], BF16, "Internal")
    DR["x2_own"] = dt("x2_own", [D, TOK], F32, "Internal")
    DR["x2s"] = dt("x2s", [D, TOK], BF16, "Internal")
    DR["x2_both"] = dt("x2_both", [2, D, TOK], BF16, "Internal")
    DR["winx_f"] = dt("winx_f", [128, NCH, DIN])
    DR["winz_f"] = dt("winz_f", [128, NCH, 1024])
    DR["wxp_f"] = dt("wxp_f", [128, 16, 96])
    DR["wdt_f"] = dt("wdt_f", [64, 8, 128])
    for nm, shape in (("wconv", [128, 16, 4]), ("bconv", [128, 16]),
                      ("bdt", [128, 8]), ("Dcol", [128, 8]), ("acomp", [128, 8, 16])):
        DR[nm] = dt(nm, shape)
    for nm, shape in (("rep8", [128, 16, 128]), ("sel8", [128, 16, 128]),
                      ("bcpat", [32, 2, 128])):
        DR[nm] = dt(nm, shape, BF16)
    DR["yz_fm"] = dt("yz_fm", [1024, S], BF16, "Internal")
    DR["W_outh"] = dt("W_outh", [1024, D])
    DR["P_mix"] = dt("P_mix", [2, D, TOK], F32, "Internal")
    DR["mix_own"] = dt("mix_own", [D, TOK], F32, "Internal")
    DR["s_hbmD"] = dt("s_hbmD", [2, 512], F32, "Internal")
    DR["x3_fm"] = dt("x3_fm", [D, TOK], F32, "Internal")
    for nm, shape in (("gw", [128, 8, 16]), ("be1", [128, 16, 4]), ("be2", [16, D]),
                      ("be3", [128, 16, 4]), ("ws1", [D, D]), ("ws2", [D, D]),
                      ("ws3", [D, D]), ("bs1", [128, 8]), ("bs2", [128, 8]),
                      ("bs3", [128, 8])):
        DR[nm] = dt(nm, shape)
    DR["gb"] = dt("gb", [16])
    DR["We1"] = dt("We1", [16, D, 512])
    DR["We2"] = dt("We2", [16, 512, D])
    DR["We3"] = dt("We3", [16, D, 512])
    DR["x4_fm"] = dt("x4_fm", [D, TOK], F32, "ExternalOutput")
    DR["s_hbmE"] = dt("s_hbmE", [2, 512], F32, "Internal")
    DR["g_hbm"] = dt("g_hbm", [16], F32, "Internal")
    DR["c_hbm"] = dt("c_hbm", [16, 1024], F32, "Internal")

    es = contextlib.ExitStack()
    with es:
        ar = es.enter_context(Arena(nc, cols=48850))
        PS1 = es.enter_context(nc.psum_tensor("PS1", [128, 2048], F32))[:, :]
        PS2 = es.enter_context(nc.psum_tensor("PS2", [128, 2048], F32))[:, :]
        em = Emitter(nc, es)
        em.prefix = ""
        blk = es.enter_context(nc.Block())
        emit_W(nc, em, ar, {"WT": PS1[:, 0:1024]}, DR, fused=True)
        em.barrier()
        m_wag = em.op("gpsimd",
                      lambda e: e.collective_compute(
                          "AllGather", ALU.bypass, replica_groups=RG_ALL,
                          ins=[DR["cc_src"][:]], outs=[DR["cc_w"][:, :]]),
                      waits=[("q_wst", em.qcnt("wst"))])
        if upto >= 2:
            DRA = dict(DR)
            DRA["consts"] = consts[:, 0:68]
            emit_A(nc, em, ar, {"PA": PS1, "PB": PS2[0:65, :]}, DRA, fused=True)
            em.barrier()
        if upto >= 3:
            DRB = dict(DR)
            DRB["consts"] = consts[:, 0:68]
            DRB["m_wag"] = m_wag
            emit_B(nc, em, ar, mk_P_B(PS1, PS2), DRB, fused=True)
            em.barrier()
            m_xag = em.op("gpsimd",
                          lambda e: e.collective_compute(
                              "AllGather", ALU.bypass, replica_groups=RG_PAIRS,
                              ins=[DR["x2s"][:, :]],
                              outs=[DR["x2_both"][:, :, :]]),
                          waits=[("q_bxo0", em.qcnt("bxo0")),
                                 ("q_bxo1", em.qcnt("bxo1"))])
        if upto >= 4:
            DRC = dict(DR)
            DRC["consts"] = consts[:, 0:68]
            DRC["m_xag"] = m_xag
            emit_C(nc, em, ar, mk_P_C(PS1, PS2), DRC, fused=True)
            em.barrier()
        if upto >= 5:
            DRD = dict(DR)
            DRD["consts"] = consts[:, 0:68]
            emit_D2(nc, em, ar, {"PA": PS1, "PB": PS2[0:65, :]}, DRD)
            em.barrier()
        if upto >= 6:
            DRE = dict(DR)
            DRE["consts"] = consts
            emit_E(nc, em, ar, mk_P_E(PS1, PS2), DRE)
        em.build(blk)
    return nc


def make_in_maps(inputs):
    stpat = np.zeros((128, 130), np.float32)
    stpat[:, 0] = 1.0
    stpat[:, 129] = 1.0
    consts = np.zeros((128, 132), np.float32)
    consts[:, 0] = 1.0
    consts[:, 1] = EPS
    consts[:, 2] = 10.0
    consts[:, 3] = -10.0
    consts[:, 4:132] = 1.0
    identf = np.eye(128, dtype=np.float32)
    identb = np.eye(128).astype(BF)
    x0 = np.asarray(inputs["L0_attn_x"])
    Wq = np.asarray(inputs["L0_attn_Wq"])
    Wk = np.asarray(inputs["L0_attn_Wk"])
    Wv = np.asarray(inputs["L0_attn_Wv"])
    Wo = np.asarray(inputs["L0_attn_Wo"])
    bo_l = np.ascontiguousarray(np.asarray(inputs["L0_attn_bo"]).reshape(NCH, 128).T)
    wgp = np.zeros((3072, D), np.float32)
    wgp[:2816] = np.asarray(inputs["L0_ffn_w_gate"])
    wup = np.zeros((3072, D), np.float32)
    wup[:2816] = np.asarray(inputs["L0_ffn_w_up"])
    wd_full = np.asarray(inputs["L0_ffn_w_down"])
    W_in = np.asarray(inputs["L1_attn_W_in"])
    W_conv = np.asarray(inputs["L1_attn_W_conv"])
    b_conv = np.asarray(inputs["L1_attn_b_conv"])
    W_xp = np.asarray(inputs["L1_attn_W_xproj"])
    W_dt = np.asarray(inputs["L1_attn_W_dt"])
    b_dt = np.asarray(inputs["L1_attn_b_dt"])
    A_log = np.asarray(inputs["L1_attn_A_log"])
    Dv = np.asarray(inputs["L1_attn_D"])
    W_out = np.asarray(inputs["L1_attn_W_out"])
    rep8 = np.zeros((128, 16, 128), BF)
    sel8 = np.zeros((128, 16, 128), BF)
    for o in range(16):
        for p in range(128):
            dd = 8 * o + (p % 8)
            rep8[dd, o, p] = 1
            sel8[p, o, dd] = 1
    bcp = np.zeros((32, 2, 128), BF)
    for p in range(128):
        bcp[p // 8, 0, p] = 1
        bcp[16 + p // 8, 1, p] = 1
    gwl = np.ascontiguousarray(
        np.asarray(inputs["L1_ffn_gate_w"]).T.reshape(8, 128, 16).transpose(1, 0, 2))
    be1l = np.ascontiguousarray(
        np.asarray(inputs["L1_ffn_be1"]).reshape(16, 4, 128).transpose(2, 0, 1))
    be3l = np.ascontiguousarray(
        np.asarray(inputs["L1_ffn_be3"]).reshape(16, 4, 128).transpose(2, 0, 1))
    bs1l = np.ascontiguousarray(np.asarray(inputs["L1_ffn_bs1"]).reshape(8, 128).T)
    bs2l = np.ascontiguousarray(np.asarray(inputs["L1_ffn_bs2"]).reshape(8, 128).T)
    bs3l = np.ascontiguousarray(np.asarray(inputs["L1_ffn_bs3"]).reshape(8, 128).T)

    in_maps = []
    for c in range(8):
        b, hf = divmod(c, 2)
        xb = _fm(x0[b])
        own = slice(hf * 1024, (hf + 1) * 1024)
        perm = np.r_[np.arange(hf * 1024, (hf + 1) * 1024),
                     np.arange((1 - hf) * 1024, (2 - hf) * 1024)]
        winx = W_in[:, :2048][:, perm]
        winz = W_in[:, 2048:][:, own]
        wxp_p = W_xp[perm, :]
        wdt_h = W_dt[:, own]
        acomp = np.zeros((128, 8, 16), np.float32)
        for p in range(128):
            n, dj = p // 8, p % 8
            for ch in range(8):
                for o in range(16):
                    acomp[p, ch, o] = A_log[hf * 1024 + ch * 128 + 8 * o + dj, n]
        in_maps.append({
            "x_fm": xb,
            "xq_fm": np.ascontiguousarray(xb[:, hf * TOK:(hf + 1) * TOK]),
            "Wq": Wq, "Wk": Wk, "Wv": Wv, "Wo": Wo, "bo": bo_l,
            "stpat": stpat, "consts": consts, "identf": identf, "identb": identb,
            "wg": np.ascontiguousarray(wgp[c * 384:(c + 1) * 384]),
            "wu": np.ascontiguousarray(wup[c * 384:(c + 1) * 384]),
            "wd": np.ascontiguousarray(wd_full[c * 128:(c + 1) * 128]),
            "winx_f": np.ascontiguousarray(winx.reshape(NCH, 128, DIN).transpose(1, 0, 2)),
            "winz_f": np.ascontiguousarray(winz.reshape(NCH, 128, 1024).transpose(1, 0, 2)),
            "wxp_f": np.ascontiguousarray(wxp_p.reshape(16, 128, 96).transpose(1, 0, 2)),
            "wdt_f": np.ascontiguousarray(wdt_h.reshape(64, 8, 128)),
            "wconv": np.ascontiguousarray(
                W_conv[:, :2048][:, perm].T.reshape(16, 128, 4).transpose(1, 0, 2)),
            "bconv": np.ascontiguousarray(b_conv[perm].reshape(16, 128).T),
            "bdt": np.ascontiguousarray(b_dt[own].reshape(8, 128).T),
            "Dcol": np.ascontiguousarray(Dv[own].reshape(8, 128).T),
            "acomp": acomp,
            "rep8": rep8, "sel8": sel8, "bcpat": bcp,
            "W_outh": np.ascontiguousarray(W_out[own, :]),
            "gw": gwl, "gb": np.asarray(inputs["L1_ffn_gate_b"]),
            "We1": np.asarray(inputs["L1_ffn_We1"]),
            "We2": np.asarray(inputs["L1_ffn_We2"]),
            "We3": np.asarray(inputs["L1_ffn_We3"]),
            "be1": be1l, "be2": np.asarray(inputs["L1_ffn_be2"]), "be3": be3l,
            "ws1": np.asarray(inputs["L1_ffn_ws1"]),
            "ws2": np.asarray(inputs["L1_ffn_ws2"]),
            "ws3": np.asarray(inputs["L1_ffn_ws3"]),
            "bs1": bs1l, "bs2": bs2l, "bs3": bs3l,
        })
    return in_maps


def kernel(**inputs):
    in_maps = make_in_maps(inputs)
    nc = build_fused()
    res = run_bass_kernel_spmd(nc, in_maps, core_ids=list(range(8)))
    x4 = np.empty((B, S, D), np.float32)
    for c in range(8):
        b, hf = divmod(c, 2)
        x4[b, hf * TOK:(hf + 1) * TOK, :] = res.results[c]["x4_fm"].T
    return x4


def kernel_unfused(**inputs):
    wq = host_launch_W(inputs)
    x1 = host_launch_A(inputs)
    x2 = host_launch_B(x1, inputs, wq)
    yz = host_launch_C(x2, inputs)
    x3 = host_launch_D(yz, x2, inputs)
    x4 = host_launch_E(x3, inputs)
    return x4


# version 16 (final)
# speedup vs baseline: 1.1906x; 1.1906x over previous
import contextlib
import numpy as np
import ml_dtypes
import concourse.bass as bass
import concourse.mybir as mybir
from concourse.bass_utils import run_bass_kernel_spmd

F32 = mybir.dt.float32
BF16 = mybir.dt.bfloat16
AF = mybir.ActivationFunctionType
ALU = mybir.AluOpType
AX = mybir.AxisListType

D = 1024
S = 4096
B = 4
H, DH = 16, 64
TOK = 2048
NCH = 8
BF = ml_dtypes.bfloat16
EPS = 1e-5


class Arena:
    def __init__(self, nc, name="arena", cols=47400):
        self.cm = nc.sbuf_tensor(name, [128, cols], F32)
        self.cols = cols

    def __enter__(self):
        self.h = self.cm.__enter__()
        return self

    def __exit__(self, *a):
        return self.cm.__exit__(*a)

    def view(self, off, shape, dtype=F32, base=0):
        parts = shape[0]
        free = 1
        for s_ in shape[1:]:
            free *= s_
        fw = free if dtype == F32 else (free + 1) // 2
        assert off + fw <= self.cols, f"arena overflow {off}+{fw}>{self.cols}"
        assert base + parts <= 128
        ap = self.h[base:base + parts, off:off + fw]
        if dtype != F32:
            ap = ap.bitcast(dtype)
        if len(shape) > 2:
            names = " ".join(f"d{i}" for i in range(1, len(shape)))
            ap = ap.rearrange(f"p ({names}) -> p {names}",
                              **{f"d{i}": shape[i] for i in range(1, len(shape))})
        return ap


class Emitter:

    ENGINES = ("sync", "vector", "tensor", "scalar", "gpsimd")
    SELF_SER = ("vector", "scalar", "gpsimd")

    def __init__(self, nc, es):
        self.nc = nc
        self.ops = {e: [] for e in self.ENGINES}
        self.cnt = {e: 0 for e in self.ENGINES}
        self.sems = {}
        self.es = es
        self.prefix = ""
        for e in self.ENGINES:
            self.sems[e] = es.enter_context(nc.semaphore("sem_" + e))

    def qname(self, name):
        return "q_" + self.prefix + name

    def qcnt(self, name):
        return self.cnt.get(self.qname(name), 0)

    def op(self, engine, build, waits=(), dma=False):
        sems = self.sems
        waits = [((("q_" + self.prefix + e_[2:]) if e_.startswith("q_") else e_), v)
                 for (e_, v) in waits if callable(v) or v > 0]
        if (engine in self.SELF_SER) and not dma and self.cnt[engine] > 0:
            waits.append((engine, self.cnt[engine]))
        if dma:
            ch = "q_" + self.prefix + (dma if isinstance(dma, str) else engine)
            if ch not in self.sems:
                self.sems[ch] = self.es.enter_context(self.nc.semaphore(ch))
                self.cnt[ch] = 0
            self.cnt[ch] += 16
            myinc = 16
            mysem = sems[ch]
            key = ch
        else:
            self.cnt[engine] += 1
            myinc = 1
            mysem = sems[engine]
            key = engine

        def f(eng):
            for e_, v in waits:
                v = v() if callable(v) else v
                if v > 0:
                    eng.wait_ge(sems[e_], v)
            inst = build(eng)
            if inst is not None:
                inst.then_inc(mysem, myinc)
        self.ops[engine].append(f)
        return self.cnt[key]

    def wait(self, engine, on, val):
        if on.startswith("q_"):
            on = "q_" + self.prefix + on[2:]
        sems = self.sems
        self.ops[engine].append(lambda eng: eng.wait_ge(sems[on], val))

    def barrier(self):
        snap = dict(self.cnt)
        for e in self.ENGINES:
            sems = self.sems
            def f(eng, snap=snap, e=e):
                for k, v in snap.items():
                    if v > 0 and k != e:
                        eng.wait_ge(sems[k], v)
            self.ops[e].append(f)

    def build(self, blk):
        for name in self.ENGINES:
            lst = self.ops[name]
            if not lst:
                continue

            def body(eng, lst=lst):
                for f in lst:
                    f(eng)
            getattr(blk, name)(body)


def _fm(x):
    return np.ascontiguousarray(np.asarray(x).T)


def mk_dr_A(nc, fused=False):
    ki = "ExternalInput"
    ko = "Internal" if fused else "ExternalOutput"
    d = {nm: nc.dram_tensor(nm, [D, D], F32, kind=ki).ap()
         for nm in ("Wq", "Wk", "Wv", "Wo")}
    d["x_fm"] = nc.dram_tensor("x_fm", [D, S], F32, kind=ki).ap()
    d["xq_fm"] = nc.dram_tensor("xq_fm", [D, TOK], F32, kind=ki).ap()
    d["bo"] = nc.dram_tensor("bo", [128, NCH], F32, kind=ki).ap()
    d["stpat"] = nc.dram_tensor("stpat", [128, 130], F32, kind=ki).ap()
    d["consts"] = nc.dram_tensor("consts", [128, 68 if not fused else 132], F32, kind=ki).ap()
    d["x1_fm"] = nc.dram_tensor("x1_fm", [D, TOK], F32, kind=ko).ap()
    d["k_hbm"] = nc.dram_tensor("k_hbm", [D, S], BF16, kind="Internal").ap()
    d["v_hbm"] = nc.dram_tensor("v_hbm", [S, D], BF16, kind="Internal").ap()
    d["r_hbm"] = nc.dram_tensor("r_hbm", [TOK], F32, kind="Internal").ap()
    d["s_hbm"] = nc.dram_tensor("s_hbm", [2, 512], F32, kind="Internal").ap()
    return d


def emit_A(nc, em, ar, P, DR, fused=False):
    dx, dxq = DR["x_fm"], DR["xq_fm"]
    dW = {nm: DR[nm] for nm in ("Wq", "Wk", "Wv", "Wo")}
    dbo, dst, dcst = DR["bo"], DR["stpat"], DR["consts"]
    dout = DR["x1_fm"]
    k_hbm, v_hbm, r_hbm, s_hbm = DR["k_hbm"], DR["v_hbm"], DR["r_hbm"], DR["s_hbm"]
    if True:
        q_sb = ar.view(0, [128, NCH, TOK], BF16)
        onorm = ar.view(8192, [128, NCH, TOK], BF16)
        bo_sb = ar.view(16384, [128, NCH], F32)
        stp = ar.view(16392, [128, 130], F32)
        cst = ar.view(16522, [128, 68], F32)
        SCR = 16704
        wq_sb = ar.view(SCR, [128, NCH, D], BF16)
        wk_sb = ar.view(SCR + 4096, [128, NCH, D], BF16)
        wv_sb = ar.view(SCR + 8192, [128, NCH, D], BF16)
        w_stage = ar.view(SCR + 12288, [128, 4, D], F32)
        x_stage = ar.view(SCR + 16384, [128, NCH, 512], F32)
        xbf = ar.view(SCR + 20480, [128, 2, NCH, 512], BF16)
        kv_out = ar.view(SCR + 24576, [128, 2, 512], BF16)
        kh = [ar.view(SCR, [64, S], BF16, base=0),
              ar.view(SCR, [64, S], BF16, base=64)]
        vt = ar.view(SCR + 2048, [128, 2, 32, 65], BF16)
        pt = ar.view(SCR + 4160, [128, 2, 1024], BF16)
        avt = ar.view(SCR + 5184, [65, 2, 1024], F32)
        rcp_t = ar.view(SCR + 7232, [1, 1024], F32)
        rcp_h = ar.view(SCR + 8256, [1, 1024], F32)
        rrep = ar.view(SCR + 9280, [64, 2, 1024], F32)
        resid = ar.view(SCR, [128, NCH, TOK], F32)
        P3 = 33088
        wo_sb = ar.view(P3, [128, NCH, D], BF16)
        wo_stage = ar.view(P3 + 4096, [128, 4, D], F32)
        xq_st = ar.view(P3 + 8192, [128, 2, 512], F32)
        sqt = ar.view(P3 + 9216, [128, 512], F32)
        mrow = ar.view(P3 + 9728, [1, 512], F32)
        s2row = ar.view(P3 + 10240, [1, 512], F32)
        msq = ar.view(P3 + 10752, [1, 512], F32)
        varr = ar.view(P3 + 11264, [1, 512], F32)
        rstd_t = ar.view(P3 + 11776, [1, 512], F32)
        mrs_t = ar.view(P3 + 12288, [1, 512], F32)
        lnrep = ar.view(P3 + 12800, [128, 2, 512], F32)

        PA, PB = P["PA"], P["PB"]
        xre = dx.rearrange("(c p) t -> p c t", p=128)
        xqre = dxq.rearrange("(c p) t -> p c t", p=128)
        vre = v_hbm.rearrange("(kc p) d -> p kc d", p=128)

        em.op("sync", lambda e: e.dma_start(out=stp, in_=dst), dma="ald")
        em.op("sync", lambda e: e.dma_start(out=cst, in_=dcst), dma="ald")
        m_init = em.op("sync", lambda e: e.dma_start(out=bo_sb, in_=dbo), dma="ald")

        W_SBS = {"Wq": wq_sb, "Wk": wk_sb, "Wv": wv_sb}
        m_cast = 0
        for i, (nm, hf) in enumerate([(n_, h_) for n_ in ("Wq", "Wk", "Wv") for h_ in range(2)]):
            m_ld = em.op("sync",
                         lambda e, nm=nm, hf=hf: e.dma_start(
                             out=w_stage,
                             in_=dW[nm].rearrange("(c p) o -> p c o", p=128)[:, hf * 4:hf * 4 + 4, :]),
                         waits=[("vector", m_cast)], dma="ald")
            m_cast = em.op("vector",
                           lambda e, nm=nm, hf=hf: e.tensor_copy(
                               out=W_SBS[nm][:, hf * 4:hf * 4 + 4, :], in_=w_stage),
                           waits=[("q_ald", m_ld)])

        g = 0
        m_tile = {}
        m_evac = {}
        m_store = {}
        chunk_start = []
        for phase, srcre, n_outer in (("KV", xre, 8), ("Q", xqre, 4)):
            for tc in range(n_outer):
                ci = len(chunk_start)
                chunk_start.append(g)
                m_ld = em.op("sync",
                             lambda e, srcre=srcre, tc=tc: e.dma_start(
                                 out=x_stage, in_=srcre[:, :, tc * 512:(tc + 1) * 512]),
                             waits=[("vector", m_cast)], dma="ald")
                par = ci % 2
                tw = m_tile.get(chunk_start[ci - 1] - 1 + 16, 0) if ci >= 2 else 0
                tw = m_tile.get(chunk_start[ci - 1] - 1, 0) if ci >= 2 else 0
                m_cast = em.op("vector",
                               lambda e, par=par: e.tensor_copy(out=xbf[:, par], in_=x_stage),
                               waits=[("q_ald", m_ld), ("tensor", tw)])
                tiles = ([("K", oc, 0) for oc in range(8)]
                         + [("V", t_, d_) for t_ in range(4) for d_ in range(2)]
                         ) if phase == "KV" else [("Q", oc, 0) for oc in range(8)]
                for (kind, a_, b_) in tiles:
                    bank = g % 2

                    def mk_mm(e, kind=kind, a_=a_, b_=b_, par=par, bank=bank):
                        inst = None
                        for kc in range(8):
                            if kind == "K":
                                lhsT = wk_sb[:, kc, a_ * 128:(a_ + 1) * 128]
                                rhs = xbf[:, par, kc, :]
                            elif kind == "V":
                                lhsT = xbf[:, par, kc, a_ * 128:(a_ + 1) * 128]
                                rhs = wv_sb[:, kc, b_ * 512:(b_ + 1) * 512]
                            else:
                                lhsT = wq_sb[:, kc, a_ * 128:(a_ + 1) * 128]
                                rhs = xbf[:, par, kc, :]
                            inst = e.matmul(PA[:, bank * 512:bank * 512 + 512], lhsT, rhs,
                                            start=(kc == 0), stop=(kc == 7))
                        return inst
                    m_tile[g] = em.op("tensor", mk_mm,
                                      waits=[("vector", m_cast),
                                             ("vector", m_evac.get(g - 2, 0))])
                    if kind == "Q":
                        m_evac[g] = em.op("vector",
                                          lambda e, a_=a_, tc=tc, bank=bank: e.tensor_copy(
                                              out=q_sb[:, a_, tc * 512:(tc + 1) * 512],
                                              in_=PA[:, bank * 512:bank * 512 + 512]),
                                          waits=[("tensor", m_tile[g])])
                    else:
                        m_evac[g] = em.op("vector",
                                          lambda e, bank=bank: e.tensor_copy(
                                              out=kv_out[:, bank],
                                              in_=PA[:, bank * 512:bank * 512 + 512]),
                                          waits=[("tensor", m_tile[g]),
                                                 ("q_akv%d" % (g % 2), m_store.get(g - 2, 0))])
                        if kind == "K":
                            def mk_st(e, a_=a_, tc=tc, bank=bank):
                                return e.dma_start(
                                    out=k_hbm[a_ * 128:(a_ + 1) * 128, tc * 512:(tc + 1) * 512],
                                    in_=kv_out[:, bank])
                        else:
                            def mk_st(e, a_=a_, b_=b_, tc=tc, bank=bank):
                                return e.dma_start(
                                    out=v_hbm[tc * 512 + a_ * 128: tc * 512 + (a_ + 1) * 128,
                                              b_ * 512:(b_ + 1) * 512],
                                    in_=kv_out[:, bank])
                        m_store[g] = em.op("sync", mk_st,
                                           waits=[("vector", m_evac[g])],
                                           dma="akv%d" % (g % 2))
                    g += 1
        NT = g
        m_p1_t = em.cnt["tensor"]
        m_p1_v = em.cnt["vector"]
        m_p1_g = em.cnt["gpsimd"]

        m_vtone = em.op("vector", lambda e: e.memset(vt[:, :, :, 64:65], 1.0),
                        waits=[("tensor", m_p1_t)])

        pend_rep = [None]

        def flush_rep():
            if pend_rep[0] is not None:
                fn, w_, key = pend_rep[0]
                m_rep[key] = em.op("tensor", fn, waits=w_)
                pend_rep[0] = None

        m_sc = {}
        m_exp = {}
        m_av = {}
        m_evA = {}
        m_rcp = {}
        m_rep = {}
        m_on = {}
        m_pref = {}
        for h in range(H):
            pw = [("tensor", m_sc.get((h - 2, 1, 31), 0)),
                  ("tensor", m_av.get((h - 2, 1, 31), 0))]
            if h < 2:
                pw += [("tensor", m_p1_t), ("vector", m_vtone),
                       ("q_akv0", em.qcnt("akv0")),
                       ("q_akv1", em.qcnt("akv1"))]
            em.op("sync",
                  lambda e, h=h: e.dma_start(out=kh[h % 2][:, :], in_=k_hbm[h * 64:(h + 1) * 64, :]),
                  waits=pw, dma="apf%d" % (h % 2))
            m_pref[h] = em.op("sync",
                              lambda e, h=h: e.dma_start(out=vt[:, h % 2, :, 0:64],
                                                         in_=vre[:, :, h * 64:(h + 1) * 64]),
                              dma="apf%d" % (h % 2))
            hb = (h % 2) * 64
            hc = h // 2
            for qh in range(2):
                for kc in range(32):
                    sidx = h * 64 + qh * 32 + kc
                    sb = sidx % 2
                    prev = (h * 64 + qh * 32 + kc - 2)
                    ph, pq, pk = prev // 64, (prev % 64) // 32, prev % 32
                    scw = [("scalar", m_exp.get((ph, pq, pk), 0))]
                    if kc == 0 and qh == 0:
                        scw.append(("q_apf%d" % (h % 2), m_pref[h]))

                    def mk_sc(e, h=h, qh=qh, kc=kc, sb=sb, hb=hb, hc=hc):
                        inst = None
                        for ha in range(2):
                            inst = e.matmul(
                                PA[:, sb * 1024 + ha * 512: sb * 1024 + (ha + 1) * 512],
                                kh[h % 2][:, kc * 128:(kc + 1) * 128],
                                q_sb[hb:hb + 64, hc, qh * 1024 + ha * 512: qh * 1024 + (ha + 1) * 512],
                                start=True, stop=True, skip_group_check=True)
                        return inst
                    m_sc[(h, qh, kc)] = em.op("tensor", mk_sc, waits=scw)
                    if kc == 2:
                        flush_rep()
                    p1 = (h * 64 + qh * 32 + kc - 1)
                    m_exp[(h, qh, kc)] = em.op(
                        "scalar",
                        lambda e, sb=sb: e.activation(
                            out=pt[:, sb, :], in_=PA[:, sb * 1024:(sb + 1) * 1024],
                            func=AF.Exp, scale=0.125),
                        waits=[("tensor", m_sc[(h, qh, kc)]),
                               ("tensor", m_av.get((p1 // 64, (p1 % 64) // 32, p1 % 32), 0))])
                    avw = [("scalar", m_exp[(h, qh, kc)])]
                    if kc == 0:
                        avw.append(("vector", m_on.get((h - 1, qh), 0)))

                    def mk_av(e, h=h, qh=qh, kc=kc, sb=sb):
                        inst = None
                        for ha in range(2):
                            inst = e.matmul(
                                PB[:, qh * 1024 + ha * 512: qh * 1024 + (ha + 1) * 512],
                                vt[:, h % 2, kc, :],
                                pt[:, sb, ha * 512:(ha + 1) * 512],
                                start=(kc == 0), stop=(kc == 31), skip_group_check=True)
                        return inst
                    m_av[(h, qh, kc)] = em.op("tensor", mk_av, waits=avw)
                m_evA[(h, qh)] = em.op(
                    "vector",
                    lambda e, qh=qh: e.tensor_copy(out=avt[:, qh, :],
                                                   in_=PB[:, qh * 1024:(qh + 1) * 1024]),
                    waits=[("tensor", m_av[(h, qh, 31)]),
                           ("scalar", m_rcp.get((h - 1, qh), 0))])
                em.op("scalar",
                      lambda e, qh=qh: e.activation(out=rcp_t[:, :], in_=avt[64:65, qh, :],
                                                    func=AF.Ln),
                      waits=[("vector", m_evA[(h, qh)]),
                             ("tensor", m_rep.get((h, qh - 1), m_rep.get((h - 1, 1), 0)))])
                m_rcp[(h, qh)] = em.op(
                    "scalar",
                    lambda e: e.activation(out=rcp_h[:, :], in_=rcp_t[:, :],
                                           func=AF.Exp, scale=-1.0))

                def mk_rep(e, qh=qh):
                    inst = None
                    for ha in range(2):
                        inst = e.matmul(
                            PB[0:64, qh * 1024 + ha * 512: qh * 1024 + (ha + 1) * 512],
                            cst[0:1, 4:68],
                            rcp_h[0:1, ha * 512:(ha + 1) * 512],
                            start=True, stop=True, skip_group_check=True)
                    return inst
                pend_rep[0] = (mk_rep, [("scalar", m_rcp[(h, qh)]),
                                        ("vector", m_evA[(h, qh)])], (h, qh))
                m_on[(h, qh)] = em.op(
                    "vector",
                    lambda e, qh=qh, hb=hb, hc=hc: e.tensor_tensor(
                        out=onorm[hb:hb + 64, hc, qh * 1024:(qh + 1) * 1024],
                        in0=avt[0:64, qh, :], in1=PB[0:64, qh * 1024:(qh + 1) * 1024],
                        op=ALU.mult),
                    waits=[("tensor", (lambda k=(h, qh): m_rep[k]))])
        flush_rep()

        m_p2_t = em.cnt["tensor"]
        m_p2_v = em.cnt["vector"]
        m_wo_cast = 0
        for hf in range(2):
            m_ld = em.op("sync",
                         lambda e, hf=hf: e.dma_start(
                             out=wo_stage,
                             in_=dW["Wo"].rearrange("(c p) o -> p c o", p=128)[:, hf * 4:hf * 4 + 4, :]),
                         waits=[("tensor", m_p2_t), ("vector", m_p2_v),
                                ("vector", m_wo_cast)], dma="ald")
            m_wo_cast = em.op("vector",
                              lambda e, hf=hf: e.tensor_copy(
                                  out=wo_sb[:, hf * 4:hf * 4 + 4, :], in_=wo_stage),
                              waits=[("q_ald", m_ld)])
        m_wot = {}
        m_res = {}
        m_sq = {}
        m_stat = {}
        m_lnfree = 0
        m_xq = {}
        m_out = 0
        for j in range(32):
            qc, oc = divmod(j, 8)
            m_xq[j] = em.op("sync",
                            lambda e, qc=qc, oc=oc, j=j: e.dma_start(
                                out=xq_st[:, j % 2, :], in_=xqre[:, oc, qc * 512:(qc + 1) * 512]),
                            waits=[("vector", m_res.get(j - 2, 0)),
                                   ("vector", m_wo_cast if j == 0 else 0)],
                            dma="axq%d" % (j % 2))
            m_wot[j] = em.op("tensor",
                             lambda e, qc=qc, oc=oc, j=j: [
                                 e.matmul(PA[:, (j % 2) * 512:(j % 2) * 512 + 512],
                                          wo_sb[:, kc, oc * 128:(oc + 1) * 128],
                                          onorm[:, kc, qc * 512:(qc + 1) * 512],
                                          start=(kc == 0), stop=(kc == 7))
                                 for kc in range(8)][-1],
                             waits=[("vector", m_wo_cast),
                                    ("vector", m_res.get(j - 2, 0))])
            em.op("vector",
                  lambda e, qc=qc, oc=oc, j=j: e.tensor_scalar(
                      out=resid[:, oc, qc * 512:(qc + 1) * 512],
                      in0=PA[:, (j % 2) * 512:(j % 2) * 512 + 512],
                      scalar1=bo_sb[:, oc:oc + 1], scalar2=None, op0=ALU.add),
                  waits=[("tensor", m_wot[j]), ("q_axq%d" % (j % 2), m_xq[j])])
            m_res[j] = em.op("vector",
                             lambda e, qc=qc, oc=oc, j=j: e.tensor_tensor(
                                 out=resid[:, oc, qc * 512:(qc + 1) * 512],
                                 in0=resid[:, oc, qc * 512:(qc + 1) * 512],
                                 in1=xq_st[:, j % 2, :], op=ALU.add))
            m_sq[j] = em.op("scalar",
                            lambda e, qc=qc, oc=oc: e.activation(
                                out=sqt, in_=resid[:, oc, qc * 512:(qc + 1) * 512],
                                func=AF.Square),
                            waits=[("vector", m_res[j]),
                                   ("tensor", m_stat.get(j - 1, 0))])

            def mk_stat(e, qc=qc, oc=oc):
                e.matmul(PB[:, 0:512], stp[:, 0:65],
                         resid[:, oc, qc * 512:(qc + 1) * 512],
                         start=(oc == 0), stop=False, skip_group_check=True)
                return e.matmul(PB[:, 0:512], stp[:, 65:130], sqt,
                                start=False, stop=(oc == 7), skip_group_check=True)
            m_stat[j] = em.op("tensor", mk_stat,
                              waits=[("scalar", m_sq[j]),
                                     ("vector", m_lnfree if oc == 0 else 0)])
            if oc == 7:
                m1 = em.op("vector",
                           lambda e: e.tensor_scalar(out=mrow, in0=PB[0:1, 0:512],
                                                     scalar1=1.0 / D, scalar2=None,
                                                     op0=ALU.mult),
                           waits=[("tensor", m_stat[j])])
                m_lnfree = em.op("vector",
                                 lambda e: e.tensor_scalar(out=s2row, in0=PB[64:65, 0:512],
                                                           scalar1=1.0 / D, scalar2=None,
                                                           op0=ALU.mult))
                m2 = em.op("scalar", lambda e: e.activation(out=msq, in_=mrow, func=AF.Square),
                           waits=[("vector", m_lnfree)])
                m3 = em.op("vector",
                           lambda e: e.tensor_tensor(out=varr, in0=s2row, in1=msq,
                                                     op=ALU.subtract),
                           waits=[("scalar", m2)])
                m4 = em.op("scalar",
                           lambda e: e.activation(out=msq, in_=varr, func=AF.Ln,
                                                  bias=cst[0:1, 1:2]),
                           waits=[("vector", m3)])
                m5 = em.op("scalar",
                           lambda e: e.activation(out=rstd_t, in_=msq, func=AF.Exp,
                                                  scale=-0.5))
                m6 = em.op("vector",
                           lambda e: e.tensor_tensor(out=mrs_t, in0=mrow, in1=rstd_t,
                                                     op=ALU.mult),
                           waits=[("scalar", m5)])
                m7 = em.op("sync", lambda e: e.dma_start(out=s_hbm[0, :], in_=rstd_t[0:1, :]),
                           waits=[("vector", m6)], dma="lnb")
                m7 = em.op("sync", lambda e: e.dma_start(out=s_hbm[1, :], in_=mrs_t[0:1, :]),
                           dma="lnb")
                m8 = em.op("sync",
                           lambda e: e.dma_start(out=lnrep[:, 0, :],
                                                 in_=s_hbm[0:1, :].broadcast_to([128, 512])),
                           waits=[("q_lnb", m7)], dma="lnb")
                m8 = em.op("sync",
                           lambda e: e.dma_start(out=lnrep[:, 1, :],
                                                 in_=s_hbm[1:2, :].broadcast_to([128, 512])),
                           dma="lnb")
                for oc2 in range(8):
                    em.op("vector",
                          lambda e, qc=qc, oc2=oc2: e.tensor_tensor(
                              out=resid[:, oc2, qc * 512:(qc + 1) * 512],
                              in0=resid[:, oc2, qc * 512:(qc + 1) * 512],
                              in1=lnrep[:, 0, :], op=ALU.mult),
                          waits=[("q_lnb", m8)])
                    m9 = em.op("vector",
                               lambda e, qc=qc, oc2=oc2: e.tensor_tensor(
                                   out=resid[:, oc2, qc * 512:(qc + 1) * 512],
                                   in0=resid[:, oc2, qc * 512:(qc + 1) * 512],
                                   in1=lnrep[:, 1, :], op=ALU.subtract))
                    if not fused:
                        m_out = em.op("sync",
                                      lambda e, qc=qc, oc2=oc2: e.dma_start(
                                          out=dout.rearrange("(c p) t -> p c t", p=128)[
                                              :, oc2, qc * 512:(qc + 1) * 512],
                                          in_=resid[:, oc2, qc * 512:(qc + 1) * 512]),
                                      waits=[("vector", m9)], dma="out")

        if not fused:
            em.wait("sync", "q_out", m_out)
        if fused:
            emit_A_x1tm(nc, em, ar, P, DR)


def emit_A_x1tm(nc, em, ar, P, DR):
    resid = ar.view(16704, [128, NCH, TOK], F32)
    idf = ar.view(44912, [128, 128], F32)
    stage = ar.view(45040, [128, NCH, 128], F32)
    PA = P["PA"]
    em.barrier()
    m_idf = em.op("sync", lambda e: e.dma_start(out=idf, in_=DR["identf"]), dma="atm")
    m_evd = {}
    m_st = 0
    for tt in range(16):
        for oc in range(NCH):
            g_ = tt * NCH + oc
            m_tp = em.op("tensor",
                         lambda e, tt=tt, oc=oc: e.transpose(
                             PA[:, (oc % 2) * 512:(oc % 2) * 512 + 128],
                             resid[:, oc, tt * 128:(tt + 1) * 128], idf),
                         waits=[("q_atm", m_idf if (tt == 0 and oc == 0) else 0),
                                ("vector", m_evd.get(g_ - 2, 0))])
            m_evd[g_] = em.op("vector",
                              lambda e, oc=oc: e.tensor_copy(
                                  out=stage[:, oc, :], in_=PA[:, (oc % 2) * 512:(oc % 2) * 512 + 128]),
                              waits=[("tensor", m_tp), ("q_atm", m_st if oc == 0 else 0)])
        m_ev = m_evd[tt * NCH + NCH - 1]
        m_st = em.op("sync",
                     lambda e, tt=tt: e.dma_start(
                         out=DR["x1_tm"][tt * 128:(tt + 1) * 128, :],
                         in_=stage.rearrange("p c t -> p (c t)")),
                     waits=[("vector", m_ev)], dma="atm")
    em.wait("sync", "q_atm", m_st)


def build_launch_A():
    nc = bass.Bass("TRN2", target_bir_lowering=False)
    DR = mk_dr_A(nc)
    es = contextlib.ExitStack()
    with es:
        ar = es.enter_context(Arena(nc))
        P = {"PA": es.enter_context(nc.psum_tensor("PA", [128, 2048], F32))[:, :],
             "PB": es.enter_context(nc.psum_tensor("PB", [65, 2048], F32))[:, :]}
        em = Emitter(nc, es)
        em.prefix = "A"
        blk = es.enter_context(nc.Block())
        emit_A(nc, em, ar, P, DR)
        em.build(blk)
    return nc


def host_launch_A(inputs):
    x = np.asarray(inputs["L0_attn_x"])
    stpat = np.zeros((128, 130), np.float32)
    stpat[:, 0] = 1.0
    stpat[:, 129] = 1.0
    consts = np.zeros((128, 68), np.float32)
    consts[:, 0] = 1.0
    consts[:, 1] = EPS
    consts[:, 4:68] = 1.0
    in_maps = []
    for c in range(8):
        b, hf = divmod(c, 2)
        xb = _fm(x[b])
        in_maps.append({
            "x_fm": xb,
            "xq_fm": np.ascontiguousarray(xb[:, hf * TOK:(hf + 1) * TOK]),
            "Wq": np.asarray(inputs["L0_attn_Wq"]),
            "Wk": np.asarray(inputs["L0_attn_Wk"]),
            "Wv": np.asarray(inputs["L0_attn_Wv"]),
            "Wo": np.asarray(inputs["L0_attn_Wo"]),
            "bo": np.ascontiguousarray(np.asarray(inputs["L0_attn_bo"]).reshape(NCH, 128).T),
            "stpat": stpat,
            "consts": consts,
        })
    nc = build_launch_A()
    res = run_bass_kernel_spmd(nc, in_maps, core_ids=list(range(8)))
    x1 = np.empty((B, S, D), np.float32)
    for c in range(8):
        b, hf = divmod(c, 2)
        x1[b, hf * TOK:(hf + 1) * TOK, :] = res.results[c]["x1_fm"].T
    return x1


C23 = float(2 ** 23)
CR = float(3 * 2 ** 22)


def emit_fp4_quant(em, in_ap, out_ap, P_, W_, scr, first_waits=()):
    ve = "vector"
    nb = W_ // 16
    em.op(ve, lambda e: e.tensor_reduce(
        out=scr["bm"], in_=in_ap.rearrange("p (g k) -> p g k", k=16),
        axis=AX.X, op=ALU.max, apply_absolute_value=True), waits=list(first_waits))
    em.op(ve, lambda e: e.tensor_single_scalar(out=scr["bm"], in_=scr["bm"],
                                               scalar=6e-12, op=ALU.max))
    em.op(ve, lambda e: e.reciprocal(out=scr["rc"], in_=scr["bm"]))
    em.op(ve, lambda e: e.tensor_single_scalar(out=scr["rc"], in_=scr["rc"],
                                               scalar=6.0, op=ALU.mult))
    em.op(ve, lambda e: e.tensor_single_scalar(out=scr["sg"], in_=in_ap,
                                               scalar=0.0, op=ALU.is_gt))
    em.op(ve, lambda e: e.tensor_scalar(out=scr["sg"], in0=scr["sg"], scalar1=2.0,
                                        scalar2=-1.0, op0=ALU.mult, op1=ALU.add))
    em.op(ve, lambda e: e.tensor_tensor(
        out=scr["a"].rearrange("p (g k) -> p g k", k=16),
        in0=in_ap.rearrange("p (g k) -> p g k", k=16),
        in1=scr["rc"].unsqueeze(2).broadcast_to([P_, nb, 16]),
        op=ALU.mult))
    em.op(ve, lambda e: e.tensor_tensor(out=scr["a"], in0=scr["a"], in1=scr["sg"],
                                        op=ALU.mult))
    em.op(ve, lambda e: e.tensor_single_scalar(out=scr["r1"], in_=scr["a"],
                                               scalar=2.0, op=ALU.min))
    em.op(ve, lambda e: e.tensor_scalar(out=scr["r1"], in0=scr["r1"], scalar1=2.0,
                                        scalar2=C23, op0=ALU.mult, op1=ALU.add))
    em.op(ve, lambda e: e.tensor_scalar(out=scr["r1"], in0=scr["r1"], scalar1=C23,
                                        scalar2=0.5, op0=ALU.subtract, op1=ALU.mult))
    em.op(ve, lambda e: e.tensor_scalar(out=scr["q"], in0=scr["a"], scalar1=2.0,
                                        scalar2=4.0, op0=ALU.max, op1=ALU.min))
    em.op(ve, lambda e: e.tensor_single_scalar(out=scr["q"], in_=scr["q"],
                                               scalar=C23, op=ALU.add))
    em.op(ve, lambda e: e.tensor_single_scalar(out=scr["q"], in_=scr["q"],
                                               scalar=C23 + 6.0, op=ALU.subtract))
    em.op(ve, lambda e: e.tensor_tensor(out=scr["q"], in0=scr["q"], in1=scr["r1"],
                                        op=ALU.add))
    em.op(ve, lambda e: e.tensor_scalar(out=scr["r1"], in0=scr["a"], scalar1=4.0,
                                        scalar2=6.0, op0=ALU.max, op1=ALU.min))
    em.op(ve, lambda e: e.tensor_scalar(out=scr["r1"], in0=scr["r1"], scalar1=0.5,
                                        scalar2=C23, op0=ALU.mult, op1=ALU.add))
    em.op(ve, lambda e: e.tensor_scalar(out=scr["r1"], in0=scr["r1"], scalar1=C23,
                                        scalar2=2.0, op0=ALU.subtract, op1=ALU.mult))
    em.op(ve, lambda e: e.tensor_tensor(out=scr["q"], in0=scr["q"], in1=scr["r1"],
                                        op=ALU.add))
    em.op(ve, lambda e: e.tensor_tensor(out=scr["q"], in0=scr["q"], in1=scr["sg"],
                                        op=ALU.mult))
    em.op(ve, lambda e: e.tensor_single_scalar(out=scr["q"], in_=scr["q"],
                                               scalar=1.0 / 6.0, op=ALU.mult))
    return em.op(ve, lambda e: e.tensor_tensor(
        out=out_ap.rearrange("p (g k) -> p g k", k=16),
        in0=scr["q"].rearrange("p (g k) -> p g k", k=16),
        in1=scr["bm"].unsqueeze(2).broadcast_to([P_, nb, 16]),
        op=ALU.mult))


def mk_dr_W(nc, fused=False):
    ki = "ExternalInput"
    ko = "Internal" if fused else "ExternalOutput"
    return {
        "wg": nc.dram_tensor("wg", [384, D], F32, kind=ki).ap(),
        "wu": nc.dram_tensor("wu", [384, D], F32, kind=ki).ap(),
        "wd": nc.dram_tensor("wd", [128, 2816], F32, kind=ki).ap(),
        "wgq": nc.dram_tensor("wgq", [384, D], BF16, kind=ko).ap(),
        "wuq": nc.dram_tensor("wuq", [384, D], BF16, kind=ko).ap(),
        "wdq": nc.dram_tensor("wdq", [128, 2816], BF16, kind=ko).ap(),
    }


def emit_W(nc, em, ar, P, DR, fused=False):
    dg, du, dd = DR["wg"], DR["wu"], DR["wd"]
    if not fused:
        og, ou, od = DR["wgq"], DR["wuq"], DR["wdq"]
    if True:
        xin = ar.view(0, [128, 3, D], F32)
        xout = ar.view(3072, [128, 3, D], BF16)
        scr = {
            "bm": ar.view(4608, [128, 3 * D // 16], F32),
            "rc": ar.view(4800, [128, 3 * D // 16], F32),
            "a": ar.view(4992, [128, 3, D], F32),
            "r1": ar.view(8064, [128, 3, D], F32),
            "q": ar.view(11136, [128, 3, D], F32),
            "sg": ar.view(14208, [128, 3, D], F32),
        }
        din = ar.view(17280, [128, 2816], F32)
        dosb = ar.view(20096, [128, 2816], BF16)
        scrd = {
            "bm": ar.view(21504, [128, 176], F32),
            "rc": ar.view(21680, [128, 176], F32),
            "a": ar.view(21856, [128, 2816], F32),
            "r1": ar.view(24672, [128, 2816], F32),
            "q": ar.view(27488, [128, 2816], F32),
            "sg": ar.view(30304, [128, 2816], F32),
        }

        if fused:
            xout2 = ar.view(33120, [128, 3, D], BF16)
            idb = ar.view(34656, [128, 128], BF16)
            wgT = ar.view(34720, [128, 8, 384], BF16)
            wuT = ar.view(36256, [128, 8, 384], BF16)
            wdT = ar.view(37792, [128, 22, 128], BF16)
            PT = P["WT"].bitcast(BF16)
            cc = DR["cc_src"]
            m_idb = em.op("sync", lambda e: e.dma_start(out=idb, in_=DR["identb"]),
                          dma="wld")

        m_q = 0
        m_store = 0
        m_tr_last = 0
        mats = ((dg, True, 0), (du, True, 1), (dd, False, 2))
        for (src_d, w3, mi) in mats:
            xo = xout if (not fused or mi == 0) else (xout2 if mi == 1 else dosb)
            if w3:
                m_ld = em.op("sync", lambda e, s=src_d: e.dma_start(
                    out=xin, in_=s.rearrange("(c p) d -> p c d", p=128)),
                    waits=[("vector", m_q)], dma="wld")
                flat_in = xin.rearrange("p c d -> p (c d)")
                flat_out = xo.rearrange("p c d -> p (c d)")
                scr_use = {k: (v.rearrange("p c d -> p (c d)") if k in ("a", "r1", "q", "sg") else v)
                           for k, v in scr.items()}
                m_q = emit_fp4_quant(em, flat_in, flat_out, 128, 3 * D, scr_use,
                                     first_waits=[("q_wld", m_ld),
                                                  ("q_wst", m_store)])
            else:
                m_ld = em.op("sync", lambda e, s=src_d: e.dma_start(out=din, in_=s),
                             waits=[("vector", m_q)], dma="wld")
                m_q = emit_fp4_quant(em, din, dosb, 128, 2816, scrd,
                                     first_waits=[("q_wld", m_ld),
                                                  ("q_wst", m_store)])
            if not fused:
                out_d = (DR["wgq"], DR["wuq"], DR["wdq"])[mi]
                if w3:
                    m_store = em.op("sync", lambda e, o=out_d: e.dma_start(
                        out=o.rearrange("(c p) d -> p c d", p=128), in_=xout),
                        waits=[("vector", m_q)], dma="wst")
                else:
                    m_store = em.op("sync", lambda e, o=out_d: e.dma_start(out=o, in_=dosb),
                                    waits=[("vector", m_q)], dma="wst")
                continue
            wT = (wgT, wuT, wdT)[mi]
            m_ev = {}
            if w3:
                tiles = [(c, k) for c in range(3) for k in range(8)]
            else:
                tiles = [(ic,) for ic in range(22)]
            for ti, t in enumerate(tiles):
                sl = ti % 2

                def mk_tp(e, t=t, sl=sl, xo=xo):
                    if len(t) == 2:
                        src = xo[:, t[0], t[1] * 128:(t[1] + 1) * 128]
                    else:
                        src = xo[:, t[0] * 128:(t[0] + 1) * 128]
                    return e.transpose(PT[:, sl * 1024:sl * 1024 + 128], src, idb)
                tw = [("vector", m_q if ti == 0 else 0),
                      ("vector", m_ev.get(ti - 2, 0)),
                      ("vector", m_tr_last if ti < 2 else 0)]
                if mi == 0 and ti == 0:
                    tw.append(("q_wld", m_idb))
                m_t = em.op("tensor", mk_tp, waits=tw)

                def mk_ev(e, t=t, sl=sl, wT=wT):
                    if len(t) == 2:
                        dst = wT[:, t[1], t[0] * 128:(t[0] + 1) * 128]
                    else:
                        dst = wT[:, t[0], :]
                    return e.tensor_copy(out=dst, in_=PT[:, sl * 1024:sl * 1024 + 128])
                m_ev[ti] = em.op("vector", mk_ev, waits=[("tensor", m_t)])
            m_tr_last = m_ev[len(tiles) - 1]
            if w3:
                ccd = cc[mi * 393216:(mi + 1) * 393216].rearrange(
                    "(dk p rw) -> p dk rw", dk=8, p=128, rw=384)
                srcv = wT.rearrange("p dk rw -> p dk rw")
            else:
                ccd = cc[786432:786432 + 360448].rearrange(
                    "(ic p dd) -> p ic dd", ic=22, p=128, dd=128)
                srcv = wT
            m_store = em.op("sync",
                            lambda e, ccd=ccd, srcv=srcv: e.dma_start(out=ccd, in_=srcv),
                            waits=[("vector", m_tr_last)], dma="wst")
        em.wait("sync", "q_wst", em.qcnt("wst"))


def build_launch_W():
    nc = bass.Bass("TRN2", target_bir_lowering=False)
    DR = mk_dr_W(nc)
    es = contextlib.ExitStack()
    with es:
        ar = es.enter_context(Arena(nc, cols=40000))
        em = Emitter(nc, es)
        em.prefix = "W"
        blk = es.enter_context(nc.Block())
        emit_W(nc, em, ar, {}, DR)
        em.build(blk)
    return nc


def host_launch_W(inputs):
    wg = np.asarray(inputs["L0_ffn_w_gate"])
    wu = np.asarray(inputs["L0_ffn_w_up"])
    wd = np.asarray(inputs["L0_ffn_w_down"])
    wgp = np.zeros((3072, D), np.float32); wgp[:2816] = wg
    wup = np.zeros((3072, D), np.float32); wup[:2816] = wu
    in_maps = []
    for c in range(8):
        in_maps.append({
            "wg": np.ascontiguousarray(wgp[c * 384:(c + 1) * 384]),
            "wu": np.ascontiguousarray(wup[c * 384:(c + 1) * 384]),
            "wd": np.ascontiguousarray(wd[c * 128:(c + 1) * 128]),
        })
    nc = build_launch_W()
    res = run_bass_kernel_spmd(nc, in_maps, core_ids=list(range(8)))
    wgq = np.concatenate([res.results[c]["wgq"] for c in range(8)])[:2816]
    wuq = np.concatenate([res.results[c]["wuq"] for c in range(8)])[:2816]
    wdq = np.concatenate([res.results[c]["wdq"] for c in range(8)])
    return wgq, wuq, wdq


def emit_fp4_quant_signed(em, in_ap, out_ap, P_, W_, scr, first_waits=()):
    ve = "vector"
    nb = W_ // 16
    em.op(ve, lambda e: e.tensor_reduce(
        out=scr["bm"], in_=in_ap.rearrange("p (g k) -> p g k", k=16),
        axis=AX.X, op=ALU.max, apply_absolute_value=True), waits=list(first_waits))
    em.op(ve, lambda e: e.tensor_single_scalar(out=scr["bm"], in_=scr["bm"],
                                               scalar=6e-12, op=ALU.max))
    em.op(ve, lambda e: e.reciprocal(out=scr["rc"], in_=scr["bm"]))
    em.op(ve, lambda e: e.tensor_single_scalar(out=scr["rc"], in_=scr["rc"],
                                               scalar=6.0, op=ALU.mult))
    em.op(ve, lambda e: e.tensor_tensor(
        out=scr["a"].rearrange("p (g k) -> p g k", k=16),
        in0=in_ap.rearrange("p (g k) -> p g k", k=16),
        in1=scr["rc"].unsqueeze(2).broadcast_to([P_, nb, 16]), op=ALU.mult))
    a, r1, q = scr["a"], scr["r1"], scr["q"]
    em.op(ve, lambda e: e.tensor_scalar(out=q, in0=a, scalar1=-2.0, scalar2=2.0,
                                        op0=ALU.max, op1=ALU.min))
    em.op(ve, lambda e: e.tensor_scalar(out=q, in0=q, scalar1=2.0, scalar2=CR,
                                        op0=ALU.mult, op1=ALU.add))
    em.op(ve, lambda e: e.tensor_scalar(out=q, in0=q, scalar1=CR, scalar2=0.5,
                                        op0=ALU.subtract, op1=ALU.mult))
    em.op(ve, lambda e: e.tensor_scalar(out=r1, in0=a, scalar1=2.0, scalar2=4.0,
                                        op0=ALU.max, op1=ALU.min))
    em.op(ve, lambda e: e.tensor_single_scalar(out=r1, in_=r1, scalar=CR, op=ALU.add))
    em.op(ve, lambda e: e.tensor_single_scalar(out=r1, in_=r1, scalar=CR + 2.0,
                                               op=ALU.subtract))
    em.op(ve, lambda e: e.tensor_tensor(out=q, in0=q, in1=r1, op=ALU.add))
    em.op(ve, lambda e: e.tensor_scalar(out=r1, in0=a, scalar1=-4.0, scalar2=-2.0,
                                        op0=ALU.max, op1=ALU.min))
    em.op(ve, lambda e: e.tensor_scalar(out=r1, in0=r1, scalar1=CR, scalar2=CR - 2.0,
                                        op0=ALU.add, op1=ALU.subtract))
    em.op(ve, lambda e: e.tensor_tensor(out=q, in0=q, in1=r1, op=ALU.add))
    em.op(ve, lambda e: e.tensor_scalar(out=r1, in0=a, scalar1=4.0, scalar2=6.0,
                                        op0=ALU.max, op1=ALU.min))
    em.op(ve, lambda e: e.tensor_scalar(out=r1, in0=r1, scalar1=0.5, scalar2=CR,
                                        op0=ALU.mult, op1=ALU.add))
    em.op(ve, lambda e: e.tensor_scalar(out=r1, in0=r1, scalar1=CR, scalar2=2.0,
                                        op0=ALU.subtract, op1=ALU.mult))
    em.op(ve, lambda e: e.tensor_single_scalar(out=q, in_=q, scalar=4.0, op=ALU.subtract))
    em.op(ve, lambda e: e.tensor_tensor(out=q, in0=q, in1=r1, op=ALU.add))
    em.op(ve, lambda e: e.tensor_scalar(out=r1, in0=a, scalar1=-6.0, scalar2=-4.0,
                                        op0=ALU.max, op1=ALU.min))
    em.op(ve, lambda e: e.tensor_scalar(out=r1, in0=r1, scalar1=0.5, scalar2=CR,
                                        op0=ALU.mult, op1=ALU.add))
    em.op(ve, lambda e: e.tensor_scalar(out=r1, in0=r1, scalar1=CR, scalar2=2.0,
                                        op0=ALU.subtract, op1=ALU.mult))
    em.op(ve, lambda e: e.tensor_single_scalar(out=q, in_=q, scalar=-4.0, op=ALU.subtract))
    em.op(ve, lambda e: e.tensor_tensor(out=q, in0=q, in1=r1, op=ALU.add))
    em.op(ve, lambda e: e.tensor_single_scalar(out=q, in_=q, scalar=1.0 / 6.0,
                                               op=ALU.mult))
    return em.op(ve, lambda e: e.tensor_tensor(
        out=out_ap.rearrange("p (g k) -> p g k", k=16),
        in0=q.rearrange("p (g k) -> p g k", k=16),
        in1=scr["bm"].unsqueeze(2).broadcast_to([P_, nb, 16]), op=ALU.mult))


IFF = 2816
NIC = [(0, 512), (512, 512), (1024, 512), (1536, 512), (2048, 512), (2560, 256)]


def mk_dr_B(nc, fused=False):
    ki = "ExternalInput"
    ko = "Internal" if fused else "ExternalOutput"
    kw = "Internal" if fused else ki
    d = {}
    d["x1_tm"] = nc.dram_tensor("x1_tm", [TOK, D], F32, kind=kw).ap()
    d["wgq_T"] = nc.dram_tensor("wgq_T", [D, IFF], BF16, kind=kw).ap()
    d["wuq_T"] = nc.dram_tensor("wuq_T", [D, IFF], BF16, kind=kw).ap()
    d["wdq_T"] = nc.dram_tensor("wdq_T", [IFF, D], BF16, kind=kw).ap()
    d["identb"] = nc.dram_tensor("identb", [128, 128], BF16, kind=ki).ap()
    d["x2_tm"] = nc.dram_tensor("x2_tm", [TOK, D], F32, kind=ko).ap()
    return d


def emit_B(nc, em, ar, P, DR, ntiles=16, phases=9, fused=False):
    dx = DR["x1_tm"]
    dib, dcst = DR["identb"], DR["consts"]
    if not fused:
        dwg, dwu, dwd = DR["wgq_T"], DR["wuq_T"], DR["wdq_T"]
        dout = DR["x2_tm"]
    if True:
        wg_sb = ar.view(0, [128, NCH, IFF], BF16)
        wu_sb = ar.view(11264, [128, NCH, IFF], BF16)
        wd_sb = ar.view(22528, [128, 22, D], BF16)
        cst = ar.view(33792, [128, 68], F32)
        idb = ar.view(33860, [128, 128], BF16)
        x1t = ar.view(33924, [128, 2, D], F32)
        xq_tm = ar.view(35972, [128, D], BF16)
        xq_fm = ar.view(36484, [128, NCH, 128], BF16)
        hbuf = ar.view(36996, [128, IFF], F32)
        hq_tm = ar.view(39812, [128, IFF], BF16)
        hq_fm = ar.view(41220, [128, 22, 128], BF16)
        sg_sil = ar.view(42628, [128, 512], F32)
        resid = ar.view(43140, [128, D], F32)
        sqb = ar.view(44164, [128, D], F32)
        mcol = ar.view(45188, [128, 1], F32)
        vcol = ar.view(45189, [128, 1], F32)
        rcol = ar.view(45190, [128, 1], F32)
        scr_r1 = ar.view(45191, [128, 1408], F32)
        scr_q = ar.view(46599, [128, 1408], F32)
        scr_bm = ar.view(48007, [128, 88], F32)
        scr_rc = ar.view(48095, [128, 88], F32)

        PG, PU, PD, PT = P["PG"], P["PU"], P["PD"], P["PT"]

        em.op("sync", lambda e: e.dma_start(out=cst, in_=dcst),
              waits=[("gpsimd", DR.get("m_wag", 0))] if fused else (), dma="bld")
        em.op("sync", lambda e: e.dma_start(out=idb, in_=dib), dma="bld")
        if fused:
            idf = ar.view(48183, [128, 128], F32)
            x2stg = ar.view(48311, [128, 2, 128], F32)
            x2stgb = ar.view(48567, [128, 2, 128], BF16)
            em.op("sync", lambda e: e.dma_start(out=idf, in_=DR["identf"]), dma="bld")
            ccw = DR["cc_w"]
            ccw_g = ccw[:, 0:393216].rearrange("r (dk p rw) -> r dk p rw",
                                               dk=8, p=128, rw=384)
            ccw_u = ccw[:, 393216:786432].rearrange("r (dk p rw) -> r dk p rw",
                                                    dk=8, p=128, rw=384)
            ccw_d = ccw[:, 786432:1146880].rearrange("r (ic p dd) -> r ic p dd",
                                                     ic=22, p=128, dd=128)
            for (srcv, dstsb) in ((ccw_g, wg_sb), (ccw_u, wu_sb)):
                for dk in range(8):
                    em.op("sync", lambda e, srcv=srcv, dstsb=dstsb, dk=dk: e.dma_start(
                        out=dstsb[:, dk, 0:2688].rearrange("p (r rw) -> p r rw", r=7),
                        in_=srcv[0:7, dk].rearrange("r p rw -> p r rw")), dma="bld")
                    em.op("sync", lambda e, srcv=srcv, dstsb=dstsb, dk=dk: e.dma_start(
                        out=dstsb[:, dk, 2688:2816],
                        in_=srcv[7, dk, :, 0:128]), dma="bld")
            for ic in range(22):
                em.op("sync", lambda e, ic=ic: e.dma_start(
                    out=wd_sb[:, ic, :].rearrange("p (r dd) -> p r dd", r=8),
                    in_=ccw_d[:, ic].rearrange("r p dd -> p r dd")), dma="bld")
            m_wld = em.qcnt("bld")
            dx2o = DR["x2_own"].rearrange("(c p) t -> p c t", p=128)
            dx2s = DR["x2s"].rearrange("(c p) t -> p c t", p=128)
        else:
            em.op("sync", lambda e: e.dma_start(
                out=wg_sb, in_=dwg.rearrange("(c p) i -> p c i", p=128)), dma="bld")
            em.op("sync", lambda e: e.dma_start(
                out=wu_sb, in_=dwu.rearrange("(c p) i -> p c i", p=128)), dma="bld")
            m_wld = em.op("sync", lambda e: e.dma_start(
                out=wd_sb, in_=dwd.rearrange("(c p) d -> p c d", p=128)), dma="bld")

        m_x1 = {}
        m_xq = {}
        m_xqf = {}
        m_gu = {}
        m_h = {}
        m_hq = {}
        m_hqf = {}
        m_down = {}
        m_res = {}
        m_out = 0
        m_sil = {}
        m_x2f = {}
        m_x2d = {}
        m_x2tp = {}
        for tt in range(ntiles):
            par = tt % 2
            m_x1[tt] = em.op("sync",
                             lambda e, tt=tt, par=par: e.dma_start(
                                 out=x1t[:, par, :], in_=dx[tt * 128:(tt + 1) * 128, :]),
                             waits=[("vector", m_res.get(tt - 2, 0)),
                                    ("q_bld", m_wld if tt == 0 else 0)],
                             dma="bx%d" % (tt % 2))
            scrx = {"bm": scr_bm[:, 0:64], "rc": scr_rc[:, 0:64],
                    "a": hbuf[:, 0:1024], "r1": scr_r1[:, 0:1024],
                    "q": scr_q[:, 0:1024]}
            m_xq[tt] = emit_fp4_quant_signed(
                em, x1t[:, par, :], xq_tm, 128, D, scrx,
                first_waits=[("q_bx%d" % (tt % 2), m_x1[tt]),
                             ("tensor", (lambda t=tt: m_gu[t - 1]) if tt >= 1 else 0),
                             ("scalar", (lambda t=tt: m_sil[(t - 1, 5)]) if tt >= 1 else 0)])
            if phases < 2:
                m_out = em.op("gpsimd",
                              lambda e, tt=tt: e.dma_start(
                                  out=dout[tt * 128:(tt + 1) * 128, 0:512],
                                  in_=xq_tm.bitcast(F32)),
                              waits=[("vector", m_xq[tt])], dma="out")
                continue
            for c in range(8):
                m_tp = em.op("tensor",
                             lambda e, c=c: e.transpose(
                                 PT[:, c % 2, 0:128], xq_tm[:, c * 128:(c + 1) * 128], idb),
                             waits=[("vector", m_xq[tt] if c == 0 else 0),
                                    ("vector", (lambda t=tt, c=c: m_xqf[(t, c - 2)]) if c >= 2 else 0),
                                    ("tensor", (lambda t=tt: m_gu[t - 1]) if (c == 0 and tt >= 1) else 0)])
                m_xqf[(tt, c)] = em.op("vector",
                                       lambda e, c=c: e.tensor_copy(
                                           out=xq_fm[:, c, :], in_=PT[:, c % 2, 0:128]),
                                       waits=[("tensor", m_tp)])
            if phases < 3:
                m_out = em.op("gpsimd",
                              lambda e, tt=tt: e.dma_start(
                                  out=dout[tt * 128:(tt + 1) * 128, 0:512],
                                  in_=xq_fm.rearrange("p c t -> p (c t)").bitcast(F32)[:, 0:512]),
                              waits=[("vector", m_xqf[(tt, 7)])], dma="out")
                continue
            for ici, (i0, iw) in enumerate(NIC):
                gp = ici % 2

                def mk_g(e, i0=i0, iw=iw, gp=gp):
                    inst = None
                    for c in range(8):
                        inst = e.matmul(PG[:, gp, 0:iw], xq_fm[:, c, :],
                                        wg_sb[:, c, i0:i0 + iw],
                                        start=(c == 0), stop=(c == 7))
                    return inst

                def mk_u(e, i0=i0, iw=iw, gp=gp):
                    inst = None
                    for c in range(8):
                        inst = e.matmul(PU[:, gp, 0:iw], xq_fm[:, c, :],
                                        wu_sb[:, c, i0:i0 + iw],
                                        start=(c == 0), stop=(c == 7))
                    return inst
                gw = [("vector", m_xqf[(tt, 7)])]
                if (tt, ici - 2) in m_sil:
                    gw.append(("scalar", (lambda k=(tt, ici - 2): m_sil[k])))
                    gw.append(("vector", (lambda k=(tt, ici - 2): m_h[k])))
                elif tt >= 1 and ici < 2:
                    gw.append(("scalar", (lambda k=(tt - 1, 4 + ici): m_sil[k])))
                    gw.append(("vector", (lambda k=(tt - 1, 4 + ici): m_h[k])))
                em.op("tensor", mk_g, waits=gw)
                m_gu[tt] = em.op("tensor", mk_u)
                m_sil[(tt, ici)] = em.op(
                    "scalar",
                    lambda e, i0=i0, iw=iw, gp=gp: e.activation(
                        out=sg_sil[:, 0:iw], in_=PG[:, gp, 0:iw], func=AF.Silu),
                    waits=[("tensor", m_gu[tt]),
                           ("vector", (lambda k=(tt, ici - 1): m_h[k]) if ici >= 1 else
                            ((lambda k=(tt - 1, 5): m_h[k]) if tt >= 1 else 0))])
                m_h[(tt, ici)] = em.op(
                    "vector",
                    lambda e, i0=i0, iw=iw, gp=gp: e.tensor_tensor(
                        out=hbuf[:, i0:i0 + iw], in0=sg_sil[:, 0:iw],
                        in1=PU[:, gp, 0:iw], op=ALU.mult),
                    waits=[("scalar", m_sil[(tt, ici)])])
            if phases < 4:
                m_out = em.op("gpsimd",
                              lambda e, tt=tt: e.dma_start(
                                  out=dout[tt * 128:(tt + 1) * 128, :],
                                  in_=hbuf[:, 0:1024]),
                              waits=[("vector", m_h[(tt, 5)])], dma="out")
                continue
            for hh in range(2):
                o0 = hh * 1408
                scrh = {"bm": scr_bm, "rc": scr_rc, "a": hbuf[:, o0:o0 + 1408],
                        "r1": scr_r1, "q": scr_q}
                m_hq[(tt, hh)] = emit_fp4_quant_signed(
                    em, hbuf[:, o0:o0 + 1408], hq_tm[:, o0:o0 + 1408], 128, 1408, scrh,
                    first_waits=[("vector", m_h[(tt, 5)])])
            if phases < 5:
                m_out = em.op("gpsimd",
                              lambda e, tt=tt: e.dma_start(
                                  out=dout[tt * 128:(tt + 1) * 128, 0:704],
                                  in_=hq_tm.bitcast(F32)),
                              waits=[("vector", m_hq[(tt, 1)])], dma="out")
                continue
            for c in range(22):
                m_tp = em.op("tensor",
                             lambda e, c=c: e.transpose(
                                 PT[:, c % 2, 0:128], hq_tm[:, c * 128:(c + 1) * 128], idb),
                             waits=[("vector", m_hq[(tt, 1)] if c == 0 else 0),
                                    ("vector", (lambda t=tt, c=c: m_hqf[(t, c - 2)]) if c >= 2 else 0),
                                    ("tensor", (lambda t=tt: m_down[t - 1]) if (c == 0 and tt >= 1) else 0)])
                m_hqf[(tt, c)] = em.op("vector",
                                       lambda e, c=c: e.tensor_copy(
                                           out=hq_fm[:, c, :], in_=PT[:, c % 2, 0:128]),
                                       waits=[("tensor", m_tp)])
            if phases < 6:
                m_out = em.op("gpsimd",
                              lambda e, tt=tt: e.dma_start(
                                  out=dout[tt * 128:(tt + 1) * 128, 0:704],
                                  in_=hq_fm.rearrange("p c t -> p (c t)").bitcast(F32)[:, 0:704]),
                              waits=[("vector", m_hqf[(tt, 21)])], dma="out")
                continue
            def mk_d(e):
                inst = None
                for c in range(22):
                    for half in range(2):
                        inst = e.matmul(PD[:, half * 512:(half + 1) * 512],
                                        hq_fm[:, c, :],
                                        wd_sb[:, c, half * 512:(half + 1) * 512],
                                        start=(c == 0), stop=(c == 21))
                return inst
            m_down[tt] = em.op("tensor", mk_d,
                               waits=[("vector", m_hqf[(tt, 21)]),
                                      ("vector", (m_x2f.get((tt - 1, 7), 0) if fused
                                                  else m_res.get(tt - 1, 0)))])
            m_r0 = em.op("vector",
                         lambda e, par=par: e.tensor_tensor(
                             out=resid, in0=x1t[:, par, :], in1=PD[:, :], op=ALU.add),
                         waits=[("tensor", m_down[tt])])
            em.op("vector", lambda e: e.tensor_reduce(out=mcol, in_=resid, axis=AX.X,
                                                      op=ALU.add))
            m_sq2 = em.op("scalar", lambda e: e.activation(out=sqb, in_=resid,
                                                           func=AF.Square),
                          waits=[("vector", m_r0), ("q_out", m_out),
                                 ("tensor", m_x2tp.get(tt - 1, 0) if fused else 0)])
            m_v1 = em.op("vector", lambda e: e.tensor_reduce(out=vcol, in_=sqb, axis=AX.X,
                                                             op=ALU.add),
                         waits=[("scalar", m_sq2)])
            em.op("vector", lambda e: e.tensor_single_scalar(out=mcol, in_=mcol,
                                                             scalar=1.0 / D, op=ALU.mult))
            m_v2 = em.op("vector", lambda e: e.tensor_single_scalar(out=vcol, in_=vcol,
                                                                    scalar=1.0 / D, op=ALU.mult))
            m_msq = em.op("scalar", lambda e: e.activation(out=rcol, in_=mcol,
                                                           func=AF.Square),
                          waits=[("vector", m_v2)])
            m_v3 = em.op("vector", lambda e: e.tensor_tensor(out=vcol, in0=vcol, in1=rcol,
                                                             op=ALU.subtract),
                         waits=[("scalar", m_msq)])
            m_ln = em.op("scalar", lambda e: e.activation(out=rcol, in_=vcol, func=AF.Ln,
                                                          bias=cst[:, 1:2]),
                         waits=[("vector", m_v3)])
            m_ln = em.op("scalar", lambda e: e.activation(out=rcol, in_=rcol, func=AF.Exp,
                                                          scale=-0.5))
            m_res[tt] = em.op("vector",
                              lambda e: e.tensor_scalar(
                                  out=sqb, in0=resid, scalar1=mcol[:, 0:1],
                                  scalar2=rcol[:, 0:1], op0=ALU.subtract, op1=ALU.mult),
                              waits=[("scalar", m_ln)])
            if not fused:
                m_out = em.op("gpsimd",
                              lambda e, tt=tt: e.dma_start(
                                  out=dout[tt * 128:(tt + 1) * 128, :], in_=sqb),
                              waits=[("vector", m_res[tt])], dma="out")
                continue
            PX2 = P["PD"]
            for oc in range(8):
                sl = oc % 2
                m_tp2 = em.op("tensor",
                              lambda e, oc=oc, sl=sl: e.transpose(
                                  PX2[:, sl * 512:sl * 512 + 128],
                                  sqb[:, oc * 128:(oc + 1) * 128], idf),
                              waits=[("vector", m_res[tt] if oc == 0 else 0),
                                     ("vector", m_x2f.get((tt, oc - 2), 0))])
                m_f = em.op("vector",
                            lambda e, sl=sl: e.tensor_copy(out=x2stg[:, sl, :],
                                                           in_=PX2[:, sl * 512:sl * 512 + 128]),
                            waits=[("tensor", m_tp2),
                                   ("q_bxo%d" % sl, m_x2d.get((tt, oc - 2), 0))])
                m_x2f[(tt, oc)] = em.op(
                    "vector",
                    lambda e, sl=sl: e.tensor_copy(out=x2stgb[:, sl, :],
                                                   in_=PX2[:, sl * 512:sl * 512 + 128]))
                em.op("sync",
                      lambda e, oc=oc, sl=sl, tt=tt: e.dma_start(
                          out=dx2o[:, oc, tt * 128:(tt + 1) * 128], in_=x2stg[:, sl, :]),
                      waits=[("vector", m_x2f[(tt, oc)])], dma="bxo%d" % sl)
                m_x2d[(tt, oc)] = em.op(
                    "sync",
                    lambda e, oc=oc, sl=sl, tt=tt: e.dma_start(
                        out=dx2s[:, oc, tt * 128:(tt + 1) * 128], in_=x2stgb[:, sl, :]),
                    dma="bxo%d" % sl)
            m_x2tp[tt] = em.cnt["tensor"]
        if fused:
            em.wait("sync", "q_bxo0", em.qcnt("bxo0"))
            em.wait("sync", "q_bxo1", em.qcnt("bxo1"))
        else:
            em.wait("gpsimd", "q_out", m_out)


def mk_P_B(PS1, PS2):
    return {
        "PG": PS1[:, 0:1024].rearrange("p (a b) -> p a b", b=512),
        "PU": PS1[:, 1024:2048].rearrange("p (a b) -> p a b", b=512),
        "PD": PS2[:, 0:1024],
        "PT": PS2[:, 1024:2048].bitcast(BF16).rearrange("p (a b) -> p a b", b=1024),
    }


def build_launch_B(ntiles=16, phases=9):
    nc = bass.Bass("TRN2", target_bir_lowering=False)
    DR = mk_dr_B(nc)
    DR["consts"] = nc.dram_tensor("consts", [128, 68], F32, kind="ExternalInput").ap()
    es = contextlib.ExitStack()
    with es:
        ar = es.enter_context(Arena(nc, cols=48850))
        PS1 = es.enter_context(nc.psum_tensor("PS1", [128, 2048], F32))[:, :]
        PS2 = es.enter_context(nc.psum_tensor("PS2", [128, 2048], F32))[:, :]
        P = mk_P_B(PS1, PS2)
        em = Emitter(nc, es)
        em.prefix = "B"
        blk = es.enter_context(nc.Block())
        emit_B(nc, em, ar, P, DR, ntiles=ntiles, phases=phases)
        em.build(blk)
    return nc


def host_launch_B(x1, inputs, wq, ntiles=16, phases=9):
    wgq, wuq, wdq = wq
    consts = np.zeros((128, 68), np.float32)
    consts[:, 0] = 1.0
    consts[:, 1] = EPS
    consts[:, 4:68] = 1.0
    identb = np.eye(128).astype(BF)
    wgq_T = np.ascontiguousarray(wgq.T)
    wuq_T = np.ascontiguousarray(wuq.T)
    wdq_T = np.ascontiguousarray(wdq.T)
    in_maps = []
    for c in range(8):
        b, hf = divmod(c, 2)
        in_maps.append({
            "x1_tm": np.ascontiguousarray(x1[b, hf * TOK:(hf + 1) * TOK, :]),
            "wgq_T": wgq_T, "wuq_T": wuq_T, "wdq_T": wdq_T,
            "identb": identb, "consts": consts,
        })
    nc = build_launch_B(ntiles=ntiles, phases=phases)
    res = run_bass_kernel_spmd(nc, in_maps, core_ids=list(range(8)))
    x2 = np.empty((B, S, D), np.float32)
    for c in range(8):
        b, hf = divmod(c, 2)
        x2[b, hf * TOK:(hf + 1) * TOK, :] = res.results[c]["x2_tm"]
    return x2


DIN = 2048
NST = 16


def mk_dr_C(nc, fused=False):
    ki = "ExternalInput"
    ko = "Internal" if fused else "ExternalOutput"
    d = {}
    if not fused:
        d["x2_fm"] = nc.dram_tensor("x2_fm", [D, S], F32, kind=ki).ap()
    for nm, shape, dt_ in (("winx", [128, NCH, DIN], BF16), ("winz", [128, NCH, 1024], BF16),
                           ("wxproj", [128, 16, 96], BF16), ("wdt", [64, 8, 128], BF16),
                           ("wconv", [128, 16, 4], F32), ("bconv", [128, 16], F32),
                           ("bdt", [128, 8], F32), ("Dcol", [128, 8], F32),
                           ("acomp", [128, 8, 16], F32), ("rep8", [128, 16, 128], BF16),
                           ("sel8", [128, 16, 128], BF16), ("bcpat", [32, 2, 128], BF16)):
        d[nm] = nc.dram_tensor(nm, shape, dt_, kind=ki).ap()
    d["yz_fm"] = nc.dram_tensor("yz_fm", [1024, S], BF16, kind=ko).ap()
    return d


def emit_C(nc, em, ar, P, DR, fused=False):
    if fused:
        dx2g = DR["x2_both"]
        m_xag = DR.get("m_xag", 0)
    else:
        dx2 = DR["x2_fm"]
        dwix, dwiz, dwxp, dwdt = DR["winx"], DR["winz"], DR["wxproj"], DR["wdt"]
    dwcv, dbcv, dbdt, dDc = DR["wconv"], DR["bconv"], DR["bdt"], DR["Dcol"]
    dac, drep, dsel, dbcp = DR["acomp"], DR["rep8"], DR["sel8"], DR["bcpat"]
    dcst = DR["consts"]
    dout = DR["yz_fm"]
    if True:
        winx = ar.view(0, [128, NCH, DIN], BF16)
        winz = ar.view(8192, [128, NCH, 1024], BF16)
        wxp = ar.view(12288, [128, 16, 96], BF16)
        wdt = ar.view(13056, [64, 8, 128], BF16)
        wcv = ar.view(13568, [128, 16, 4], F32)
        bcv = ar.view(13632, [128, 16], F32)
        bdt = ar.view(13648, [128, 8], F32)
        Dc = ar.view(13656, [128, 8], F32)
        aneg = ar.view(13664, [128, 8, 16], F32)
        rep8 = ar.view(13792, [128, 16, 128], BF16)
        sel8 = ar.view(14816, [128, 16, 128], BF16)
        bcp = ar.view(15840, [32, 2, 128], BF16)
        cst = ar.view(15968, [128, 68], F32)
        x2st = ar.view(16036, [128, NCH, 512], F32)
        x2bf = ar.view(20132, [128, 2, NCH, 512], BF16)
        xzb = ar.view(24228, [128, 16, 515], F32)
        cvb = ar.view(32468, [128, 2, 512], F32)
        xinbf = ar.view(33492, [128, 16, 512], BF16)
        xinhf = ar.view(37588, [128, 8, 512], F32)
        dbc = ar.view(41684, [96, 512], F32)
        dbcbf = ar.view(42196, [96, 512], BF16)
        bcbf = ar.view(42324, [32, 512], BF16)
        vbuf = ar.view(42452 + 128, [128, 512], F32)
        spb = ar.view(42964 + 128, [128, 512], F32)
        dtbf = ar.view(43476 + 128, [128, 8, 512], BF16)
        upbf = ar.view(45524 + 128, [128, 8, 512], BF16)
        brep = ar.view(47572 + 128, [128, 512], BF16)
        crep = ar.view(47828 + 128, [128, 512], BF16)
        dtas = ar.view(48084 + 128, [128, 512], F32)
        dab = ar.view(24228, [128, 2, 512], F32)
        ubb = ar.view(25252, [128, 2, 512], F32)
        sresb = ar.view(30548, [128, 4, 512], F32)
        mbfb = ar.view(26276, [128, 4, 512], BF16)
        state = ar.view(48212, [128, 128], F32)
        ybuf = ar.view(27940, [128, 512], F32)
        zsil = ar.view(28452, [128, 512], F32)
        yzbf = ar.view(28964, [128, 2, 512], BF16)
        halo = ar.view(48340, [128, 16, 3], F32)
        dtasb = ar.view(29524, [128, 2, 512], F32)
        PXZ, PB96, PDT = P["PXZ"], P["PB96"], P["PDT"]
        PDA, PUPa, PUPb, PY = P["PDA"], P["PUPa"], P["PUPb"], P["PY"]
        if fused:
            x2g0 = dx2g.rearrange("h (c p) t -> h p c t", p=128)

            def x2slice(tc):
                return x2g0[tc // 4][:, :, (tc % 4) * 512:(tc % 4) * 512 + 512]
        else:
            x2re = dx2.rearrange("(c p) t -> p c t", p=128) if dx2.ndim == 2 else dx2

            def x2slice(tc):
                if x2re.ndim == 3:
                    return x2re[:, :, tc * 512:(tc + 1) * 512]
                return x2re[:, :, tc // 4, (tc % 4) * 512:(tc % 4) * 512 + 512]

        ld_pairs = [(wcv, dwcv), (bcv, dbcv), (bdt, dbdt), (Dc, dDc),
                    (rep8, drep), (sel8, dsel), (bcp, dbcp), (cst, dcst)]
        if not fused:
            ld_pairs = [(winx, dwix), (winz, dwiz), (wxp, dwxp), (wdt, dwdt)] + ld_pairs
        for (dst_, src_) in ld_pairs:
            m_ld = em.op("sync", lambda e, d_=dst_, s_=src_: e.dma_start(out=d_, in_=s_),
                         dma="cld")
        m_wc = 0
        if fused:
            wstg = ar.view(16036, [128, 2, 2048], F32)
            wstg1 = ar.view(16036, [128, 2, 1024], F32)
            wstg2 = ar.view(16036, [128, 16, 96], F32)
            wstg3 = ar.view(16036, [64, 8, 128], F32)
            for k in range(4):
                m_l = em.op("sync", lambda e, k=k: e.dma_start(
                    out=wstg, in_=DR["winx_f"][:, 2 * k:2 * k + 2, :]),
                    waits=[("vector", m_wc)], dma="cwl")
                m_wc = em.op("vector", lambda e, k=k: e.tensor_copy(
                    out=winx[:, 2 * k:2 * k + 2, :], in_=wstg), waits=[("q_cwl", m_l)])
            for k in range(4):
                m_l = em.op("sync", lambda e, k=k: e.dma_start(
                    out=wstg1, in_=DR["winz_f"][:, 2 * k:2 * k + 2, :]),
                    waits=[("vector", m_wc)], dma="cwl")
                m_wc = em.op("vector", lambda e, k=k: e.tensor_copy(
                    out=winz[:, 2 * k:2 * k + 2, :], in_=wstg1), waits=[("q_cwl", m_l)])
            m_l = em.op("sync", lambda e: e.dma_start(out=wstg2, in_=DR["wxp_f"]),
                        waits=[("vector", m_wc)], dma="cwl")
            m_wc = em.op("vector", lambda e: e.tensor_copy(out=wxp, in_=wstg2),
                         waits=[("q_cwl", m_l)])
            m_l = em.op("sync", lambda e: e.dma_start(out=wstg3, in_=DR["wdt_f"]),
                        waits=[("vector", m_wc)], dma="cwl")
            m_wc = em.op("vector", lambda e: e.tensor_copy(out=wdt, in_=wstg3),
                         waits=[("q_cwl", m_l)])
        m_ac = em.op("sync", lambda e: e.dma_start(out=aneg, in_=dac), dma="cld")
        m_ae = em.op("scalar", lambda e: e.activation(
            out=aneg.rearrange("p a b -> p (a b)"), in_=aneg.rearrange("p a b -> p (a b)"),
            func=AF.Exp), waits=[("q_cld", m_ac)])
        m_an = em.op("vector", lambda e: e.tensor_single_scalar(
            out=aneg.rearrange("p a b -> p (a b)"), in_=aneg.rearrange("p a b -> p (a b)"),
            scalar=-1.0, op=ALU.mult), waits=[("scalar", m_ae)])
        em.op("vector", lambda e: e.memset(state, 0.0))
        m_hz = em.op("vector", lambda e: e.memset(halo, 0.0))

        m_x2 = {}
        m_cast = {}
        m_xz = {}
        m_ev = {}
        m_cv = {}
        m_sil = {}
        m_dbc = 0
        m_dbce = 0
        m_dt = {}
        m_sp = {}
        m_bc = 0
        m_brep = 0
        m_scan = {}
        m_y = {}
        m_yz = {}
        m_out = 0
        m_halo_save = m_hz
        m_dtamm = {}
        m_upmm = {}
        m_dtts = {}
        m_ub = {}
        m_cm = {}
        m_exp = {}
        m_ymm = {}
        m_zm = {}
        m_zs = {}
        for tc in range(8):
            par = tc % 2
            if fused:
                m_x2[tc] = em.op("sync",
                                 lambda e, tc=tc, par=par: e.dma_start(
                                     out=x2bf[:, par], in_=x2slice(tc)),
                                 waits=[("tensor", m_zm.get(tc - 2, 0)),
                                        ("q_cld", em.qcnt("cld") if tc == 0 else 0),
                                        ("gpsimd", m_xag if tc == 0 else 0)],
                                 dma="cld")
                m_cast[tc] = m_wc
            else:
                m_x2[tc] = em.op("sync",
                                 lambda e, tc=tc: e.dma_start(
                                     out=x2st, in_=x2slice(tc)),
                                 waits=[("vector", m_cast.get(tc - 1, 0)),
                                        ("q_cld", em.qcnt("cld") if tc == 0 else 0)], dma="cld")
                m_cast[tc] = em.op("vector",
                                   lambda e, par=par: e.tensor_copy(out=x2bf[:, par], in_=x2st),
                                   waits=[("q_cld", m_x2[tc]),
                                          ("tensor", m_zm.get(tc - 2, 0))])
            for oc in range(16):
                def mk_xz(e, oc=oc, par=par):
                    inst = None
                    for kc in range(8):
                        inst = e.matmul(PXZ[:, :], winx[:, kc, oc * 128:(oc + 1) * 128],
                                        x2bf[:, par, kc, :], start=(kc == 0), stop=(kc == 7))
                    return inst
                xzw = [("vector", m_cast[tc]),
                       ("q_cld", m_x2[tc] if fused else 0),
                       ("vector", m_ev.get((tc, oc - 1),
                                           m_ev.get((tc - 1, 15), 0) if tc > 0 else 0))]
                if oc == 0 and tc > 0:
                    xzw.append(("scalar", m_zs.get((tc - 1, 7), 0)))
                m_xz[(tc, oc)] = em.op("tensor", mk_xz, waits=xzw)
                m_ev[(tc, oc)] = em.op("vector",
                                       lambda e, oc=oc: e.tensor_copy(
                                           out=xzb[:, oc, 3:515], in_=PXZ[:, :]),
                                       waits=[("tensor", m_xz[(tc, oc)]),
                                              ("vector", m_halo_save if oc == 0 else 0),
                                              ("vector", m_scan.get((tc - 1, 7, 15), 0) if oc == 0 else 0)])
                m_ev[(tc, oc)] = em.op("vector",
                                       lambda e, oc=oc: e.tensor_copy(
                                           out=xzb[:, oc, 0:3], in_=halo[:, oc, :]))
            for oc in range(16):
                cpar = oc % 2

                def mk_cv0(e, oc=oc, cpar=cpar):
                    return e.tensor_scalar(out=cvb[:, cpar, :], in0=xzb[:, oc, 0:512],
                                           scalar1=wcv[:, oc:oc + 1, 0],
                                           scalar2=bcv[:, oc:oc + 1],
                                           op0=ALU.mult, op1=ALU.add)
                em.op("vector", mk_cv0, waits=[("scalar", m_sil.get((tc, oc - 2), 0))])
                for k in range(1, 4):
                    em.op("vector",
                          lambda e, oc=oc, k=k, cpar=cpar: e.scalar_tensor_tensor(
                              out=cvb[:, cpar, :], in0=xzb[:, oc, k:k + 512],
                              scalar=wcv[:, oc:oc + 1, k], in1=cvb[:, cpar, :],
                              op0=ALU.mult, op1=ALU.add))
                m_cv[(tc, oc)] = em.cnt["vector"]
                m_sil[(tc, oc)] = em.op("scalar",
                                        lambda e, oc=oc, cpar=cpar: e.activation(
                                            out=xinbf[:, oc, :], in_=cvb[:, cpar, :],
                                            func=AF.Silu),
                                        waits=[("vector", m_cv[(tc, oc)])])
                if oc < 8:
                    m_sil[(tc, oc)] = em.op("scalar",
                                            lambda e, oc=oc, cpar=cpar: e.activation(
                                                out=xinhf[:, oc, :], in_=cvb[:, cpar, :],
                                                func=AF.Silu))
            m_halo_save = em.op("vector",
                                lambda e: e.tensor_copy(out=halo, in_=xzb[:, :, 512:515]),
                                waits=[("scalar", m_sil[(tc, 15)])])
            def mk_dbc(e):
                inst = None
                for oc in range(16):
                    inst = e.matmul(PB96[:, :], wxp[:, oc, :], xinbf[:, oc, :],
                                    start=(oc == 0), stop=(oc == 15))
                return inst
            m_dbcm = em.op("tensor", mk_dbc,
                           waits=[("scalar", m_sil[(tc, 15)]),
                                  ("vector", m_dbce if tc > 0 else 0)])
            m_dbce0 = em.op("vector", lambda e: e.tensor_copy(out=dbc, in_=PB96[:, :]),
                            waits=[("tensor", m_dbcm)])
            em.op("vector", lambda e: e.tensor_copy(out=dbcbf, in_=dbc))
            m_dbce = em.op("vector", lambda e: e.tensor_copy(out=bcbf, in_=dbc[64:96, :]))
            for oc in range(8):
                def mk_dt(e, oc=oc):
                    return e.matmul(PDT[:, :], wdt[:, oc, :], dbcbf[0:64, :],
                                    start=True, stop=True)
                m_dtm = em.op("tensor", mk_dt,
                              waits=[("vector", m_dbce),
                                     ("vector", m_dtts.get((tc, oc - 1, 0), 0)),
                                     ("vector", m_sp.get((tc, oc - 1), 0))])
                em.op("vector",
                      lambda e, oc=oc: e.tensor_scalar(out=vbuf, in0=PDT[:, :],
                                                       scalar1=bdt[:, oc:oc + 1],
                                                       scalar2=None, op0=ALU.add),
                      waits=[("tensor", m_dtm)])
                m_e1 = em.op("scalar", lambda e: e.activation(out=spb, in_=vbuf,
                                                              func=AF.Exp, scale=-1.0),
                             waits=[("vector", em.cnt["vector"])])
                m_l1 = em.op("scalar", lambda e: e.activation(out=spb, in_=spb,
                                                              func=AF.Ln, bias=cst[:, 0:1]))
                em.op("vector", lambda e: e.tensor_single_scalar(out=vbuf, in_=vbuf,
                                                                 scalar=0.0, op=ALU.max),
                      waits=[("scalar", m_l1)])
                m_sp[(tc, oc)] = em.op("vector",
                                       lambda e, oc=oc: e.tensor_tensor(
                                           out=dtbf[:, oc, :], in0=vbuf, in1=spb,
                                           op=ALU.add))
                m_sp[(tc, oc)] = em.op("vector",
                                       lambda e, oc=oc: e.tensor_tensor(
                                           out=upbf[:, oc, :], in0=xinhf[:, oc, :],
                                           in1=dtbf[:, oc, :], op=ALU.mult))
            def mk_b(e):
                return e.matmul(PDA[:, 0, :], bcp[:, 0, :], bcbf,
                                start=True, stop=True)

            def mk_c(e):
                return e.matmul(PUPa, bcp[:, 1, :], bcbf,
                                start=True, stop=True)
            m_bm = em.op("tensor", mk_b, waits=[("vector", m_dbce),
                                                ("vector", m_ub.get((tc - 1, 7, 15), 0)),
                                                ("scalar", m_dtts.get((tc - 1, 7, 15), 0))])
            m_cmm = em.op("tensor", mk_c)
            m_brep = em.op("vector", lambda e: e.tensor_copy(out=brep, in_=PDA[:, 0, :]),
                           waits=[("tensor", m_bm),
                                  ("gpsimd", m_cm.get((tc - 1, 7, 15), 0))])
            m_crep = em.op("vector", lambda e: e.tensor_copy(out=crep, in_=PUPa),
                           waits=[("tensor", m_cmm)])
            def pget(d, tc_, oc_, o_, k):
                g_ = (tc_ * 8 + oc_) * 16 + o_ - k
                if g_ < 0:
                    return 0
                return d.get((g_ // 128, (g_ // 16) % 8, g_ % 16), 0)

            def emit_mm_group(ge, tc=tc):
                oc_e, o_e = ge // 16, ge % 16
                sl2e = o_e % 2

                def mk_da(e, oc=oc_e, o=o_e, sl2=sl2e):
                    return e.matmul(PDA[:, sl2, :], rep8[:, o, :], dtbf[:, oc, :],
                                    start=True, stop=True, skip_group_check=True)
                m_dtamm[(tc, oc_e, o_e)] = em.op(
                    "tensor", mk_da,
                    waits=[("vector", m_brep if ge == 0 else 0),
                           ("scalar", pget(m_dtts, tc, oc_e, o_e, 2))])

                def mk_up(e, oc=oc_e, o=o_e, sl2=sl2e):
                    return e.matmul(PUPa if sl2 == 0 else PUPb,
                                    rep8[:, o, :], upbf[:, oc, :],
                                    start=True, stop=True, skip_group_check=True)
                m_upmm[(tc, oc_e, o_e)] = em.op(
                    "tensor", mk_up,
                    waits=[("vector", m_crep if ge == 0 else 0),
                           ("vector", pget(m_ub, tc, oc_e, o_e, 2))])
                m_dtts[(tc, oc_e, o_e)] = em.op(
                    "scalar",
                    lambda e, oc=oc_e, o=o_e, sl2=sl2e: e.activation(
                        out=dtasb[:, sl2, :], in_=PDA[:, sl2, :], func=AF.Relu,
                        scale=aneg[:, oc, o:o + 1], bias=cst[:, 2:3]),
                    waits=[("tensor", m_dtamm[(tc, oc_e, o_e)])])

            def emit_exp(ge, tc=tc):
                oc_e, o_e = ge // 16, ge % 16
                m_exp[(tc, oc_e, o_e)] = em.op(
                    "scalar",
                    lambda e, sl2=o_e % 2: e.activation(
                        out=dab[:, sl2, :], in_=dtasb[:, sl2, :], func=AF.Exp,
                        bias=cst[:, 3:4]),
                    waits=[("vector", pget(m_scan, tc, oc_e, o_e, 2))])

            for gg in range(128):
                oc, o = gg // 16, gg % 16
                sl2 = o % 2
                sl4 = o % 4
                if gg == 0:
                    emit_mm_group(0)
                    emit_exp(0)
                    emit_mm_group(1)
                    emit_exp(1)
                m_ub[(tc, oc, o)] = em.op(
                    "vector",
                    lambda e, sl2=sl2: e.tensor_tensor(
                        out=ubb[:, sl2, :], in0=(PUPa if sl2 == 0 else PUPb),
                        in1=brep, op=ALU.mult),
                    waits=[("tensor", m_upmm[(tc, oc, o)])])
                g = gg
                m_scan[(tc, oc, o)] = em.op(
                    "vector",
                    lambda e, g=g, sl2=sl2, sl4=sl4: e.tensor_tensor_scan(
                        out=sresb[:, sl4, :], data0=dab[:, sl2, :],
                        data1=ubb[:, sl2, :],
                        initial=state[:, g:g + 1], op0=ALU.mult, op1=ALU.add),
                    waits=[("scalar", m_exp[(tc, oc, o)]),
                           ("gpsimd", pget(m_cm, tc, oc, o, 4))])
                em.op("vector",
                      lambda e, g=g, sl4=sl4: e.tensor_copy(
                          out=state[:, g:g + 1], in_=sresb[:, sl4, 511:512]))
                if gg + 2 < 128:
                    emit_mm_group(gg + 2)
                    emit_exp(gg + 2)
                m_cm[(tc, oc, o)] = em.op(
                    "gpsimd",
                    lambda e, sl4=sl4: e.tensor_tensor(
                        out=mbfb[:, sl4, :], in0=sresb[:, sl4, :], in1=crep,
                        op=ALU.mult),
                    waits=[("vector", m_scan[(tc, oc, o)]),
                           ("tensor", pget(m_ymm, tc, oc, o, 4))])

                def mk_y(e, oc=oc, o=o, sl4=sl4):
                    return e.matmul(PY[:, :], sel8[:, o, :], mbfb[:, sl4, :],
                                    start=(o == 0), stop=(o == 15),
                                    skip_group_check=True)
                m_ymm[(tc, oc, o)] = em.op("tensor", mk_y,
                                           waits=[("gpsimd", m_cm[(tc, oc, o)]),
                                                  ("vector", m_y.get((tc, oc - 1), 0) if o == 0 else 0)])
                if o < 15:
                    continue
                m_y[(tc, oc)] = em.op(
                    "vector",
                    lambda e, oc=oc: e.scalar_tensor_tensor(
                        out=ybuf, in0=xinhf[:, oc, :], scalar=Dc[:, oc:oc + 1],
                        in1=PY[:, :], op0=ALU.mult, op1=ALU.add),
                    waits=[("tensor", m_ymm[(tc, oc, 15)])])
                def mk_z(e, oc=oc, par=par):
                    inst = None
                    for kc in range(8):
                        inst = e.matmul(PXZ[:, :], winz[:, kc, oc * 128:(oc + 1) * 128],
                                        x2bf[:, par, kc, :], start=(kc == 0), stop=(kc == 7))
                    return inst
                m_zm[tc] = em.op("tensor", mk_z,
                                 waits=[("scalar", m_zs.get((tc, oc - 1),
                                                            m_zs.get((tc - 1, 7), 0))),
                                        ("vector", m_ev[(tc, 15)] if oc < 2 else 0)])
                m_zs[(tc, oc)] = em.op("scalar",
                                       lambda e, oc=oc: e.activation(
                                           out=zsil, in_=PXZ[:, :], func=AF.Silu),
                                       waits=[("tensor", m_zm[tc]),
                                              ("vector", m_yz.get((tc, oc - 1), 0))])
                m_yz[(tc, oc)] = em.op("vector",
                                       lambda e, oc=oc: e.tensor_tensor(
                                           out=yzbf[:, oc % 2, :], in0=ybuf, in1=zsil,
                                           op=ALU.mult),
                                       waits=[("scalar", m_zs[(tc, oc)]),
                                              ("q_out", m_out)])
                m_out = em.op("gpsimd",
                              lambda e, oc=oc, tc=tc: e.dma_start(
                                  out=dout[oc * 128:(oc + 1) * 128, tc * 512:(tc + 1) * 512],
                                  in_=yzbf[:, oc % 2, :]),
                              waits=[("vector", m_yz[(tc, oc)])], dma="out")
        em.wait("gpsimd", "q_out", m_out)


def mk_P_C(PS1, PS2):
    return {
        "PXZ": PS1[:, 0:512],
        "PDA": PS1[:, 512:1536].rearrange("p (a b) -> p a b", b=512),
        "PUPa": PS1[:, 1536:2048],
        "PUPb": PS2[:, 1536:2048],
        "PB96": PS2[0:96, 0:512],
        "PDT": PS2[:, 512:1024],
        "PY": PS2[:, 1024:1536],
    }


def build_launch_C():
    nc = bass.Bass("TRN2", target_bir_lowering=False)
    DR = mk_dr_C(nc)
    DR["consts"] = nc.dram_tensor("consts", [128, 68], F32, kind="ExternalInput").ap()
    es = contextlib.ExitStack()
    with es:
        ar = es.enter_context(Arena(nc, cols=48850))
        PS1 = es.enter_context(nc.psum_tensor("PS1", [128, 2048], F32))[:, :]
        PS2 = es.enter_context(nc.psum_tensor("PS2", [128, 2048], F32))[:, :]
        P = mk_P_C(PS1, PS2)
        em = Emitter(nc, es)
        em.prefix = "C"
        blk = es.enter_context(nc.Block())
        emit_C(nc, em, ar, P, DR)
        em.build(blk)
    return nc


def host_launch_C(x2, inputs):
    W_in = np.asarray(inputs["L1_attn_W_in"])
    W_conv = np.asarray(inputs["L1_attn_W_conv"])
    b_conv = np.asarray(inputs["L1_attn_b_conv"])
    W_xp = np.asarray(inputs["L1_attn_W_xproj"])
    W_dt = np.asarray(inputs["L1_attn_W_dt"])
    b_dt = np.asarray(inputs["L1_attn_b_dt"])
    A_log = np.asarray(inputs["L1_attn_A_log"])
    Dv = np.asarray(inputs["L1_attn_D"])
    consts = np.zeros((128, 68), np.float32)
    consts[:, 0] = 1.0
    consts[:, 1] = EPS
    consts[:, 4:68] = 1.0
    rep8 = np.zeros((128, 16, 128), BF)
    sel8 = np.zeros((128, 16, 128), BF)
    for o in range(16):
        for p in range(128):
            dd = 8 * o + (p % 8)
            rep8[dd, o, p] = 1
            sel8[p, o, dd] = 1
    bcp = np.zeros((32, 2, 128), BF)
    for p in range(128):
        bcp[p // 8, 0, p] = 1
        bcp[16 + p // 8, 1, p] = 1
    in_maps = []
    for c in range(8):
        b, hf = divmod(c, 2)
        own = slice(hf * 1024, (hf + 1) * 1024)
        oth = slice((1 - hf) * 1024, (2 - hf) * 1024)
        perm = np.r_[np.arange(hf * 1024, (hf + 1) * 1024),
                     np.arange((1 - hf) * 1024, (2 - hf) * 1024)]
        winx = np.concatenate([W_in[:, :2048][:, perm]], axis=1)
        winz = W_in[:, 2048:][:, own]
        wxp_p = W_xp[perm, :]
        wdt_h = W_dt[:, own]
        acomp = np.zeros((128, 8, 16), np.float32)
        for p in range(128):
            n, dj = p // 8, p % 8
            for ch in range(8):
                for o in range(16):
                    acomp[p, ch, o] = A_log[hf * 1024 + ch * 128 + 8 * o + dj, n]
        in_maps.append({
            "x2_fm": _fm(x2[b]),
            "winx": np.ascontiguousarray(
                winx.reshape(NCH, 128, 2048).transpose(1, 0, 2)).astype(BF),
            "winz": np.ascontiguousarray(
                winz.reshape(NCH, 128, 1024).transpose(1, 0, 2)).astype(BF),
            "wxproj": np.ascontiguousarray(
                wxp_p.reshape(16, 128, 96).transpose(1, 0, 2)).astype(BF),
            "wdt": np.ascontiguousarray(
                wdt_h.reshape(64, 8, 128)).astype(BF),
            "wconv": np.ascontiguousarray(
                W_conv[:, :2048][:, perm].T.reshape(16, 128, 4).transpose(1, 0, 2)),
            "bconv": np.ascontiguousarray(
                b_conv[perm].reshape(16, 128).T),
            "bdt": np.ascontiguousarray(b_dt[own].reshape(8, 128).T),
            "Dcol": np.ascontiguousarray(Dv[own].reshape(8, 128).T),
            "acomp": acomp,
            "rep8": rep8, "sel8": sel8, "bcpat": bcp, "consts": consts,
        })
    nc = build_launch_C()
    res = run_bass_kernel_spmd(nc, in_maps, core_ids=list(range(8)))
    yz = np.empty((B, DIN, S), np.float32)
    for c in range(8):
        b, hf = divmod(c, 2)
        yz[b, hf * 1024:(hf + 1) * 1024, :] = res.results[c]["yz_fm"].astype(np.float32)
    return yz


def mk_dr_D(nc, fused=False):
    ki = "ExternalInput"
    ko = "Internal" if fused else "ExternalOutput"
    d = {}
    if not fused:
        d["yz_in"] = nc.dram_tensor("yz_in", [DIN, TOK], BF16, kind=ki).ap()
        d["x2h_fm"] = nc.dram_tensor("x2h_fm", [D, TOK], F32, kind=ki).ap()
    d["s_hbm"] = nc.dram_tensor("s_hbmD", [2, 512], F32, kind="Internal").ap()
    d["W_out"] = nc.dram_tensor("W_out", [DIN, D], F32, kind=ki).ap()
    d["x3_fm"] = nc.dram_tensor("x3_fm", [D, TOK], F32, kind=ko).ap()
    return d


def emit_D(nc, em, ar, P, DR):
    dyz = DR["yz_in"]
    dwo = DR["W_out"]
    dx2 = DR["x2h_fm"]
    dst, dcst = DR["stpat"], DR["consts"]
    dout = DR["x3_fm"]
    s_hbm = DR["s_hbm"]
    if True:
        wout = ar.view(0, [128, 16, D], BF16)
        yzsb = ar.view(8192, [128, 16, TOK], BF16)
        resid = ar.view(24576, [128, NCH, TOK], F32)
        wstage = ar.view(40960, [128, 4, D], F32)
        x2st = ar.view(45056, [128, 2, 512], F32)
        stp = ar.view(46080, [128, 130], F32)
        cst = ar.view(46210, [128, 68], F32)
        sqt = ar.view(46278, [128, 512], F32)
        mrow = ar.view(46790, [1, 512], F32)
        s2row = ar.view(47302, [1, 512], F32)
        msq = ar.view(47303 + 511, [1, 512], F32)
        varr = ar.view(48326, [1, 512], F32)
        PA, PB = P["PA"], P["PB"]
        mrow = ar.view(46790, [1, 512], F32)
        s2row = ar.view(46791, [1, 512], F32, base=1) if False else ar.view(47302, [1, 512], F32)
        msq = ar.view(47814, [1, 512], F32)
        varr = ar.view(40960, [1, 512], F32)
        rstd_t = ar.view(41472, [1, 512], F32)
        mrs_t = ar.view(41984, [1, 512], F32)
        lnrep = ar.view(42496, [128, 2, 512], F32)

        x2re = dx2.rearrange("(c p) t -> p c t", p=128) if dx2.ndim == 2 else dx2
        outre = dout.rearrange("(c p) t -> p c t", p=128)

        em.op("sync", lambda e: e.dma_start(out=stp, in_=dst), dma="dld")
        em.op("sync", lambda e: e.dma_start(out=cst, in_=dcst), dma="dld")
        m_yzl = em.op("sync", lambda e: e.dma_start(
            out=yzsb, in_=dyz.rearrange("(c p) t -> p c t", p=128)
            if dyz.ndim == 2 else dyz), dma="dld")
        m_cast = 0
        for hf in range(4):
            m_ld = em.op("sync",
                         lambda e, hf=hf: e.dma_start(
                             out=wstage,
                             in_=dwo.rearrange("(c p) o -> p c o", p=128)[:, hf * 4:hf * 4 + 4, :]),
                         waits=[("vector", m_cast)], dma="dld")
            m_cast = em.op("vector",
                           lambda e, hf=hf: e.tensor_copy(
                               out=wout[:, hf * 4:hf * 4 + 4, :], in_=wstage),
                           waits=[("q_dld", m_ld)])
        m_wot = {}
        m_res = {}
        m_sq = {}
        m_stat = {}
        m_lnfree = 0
        m_xq = {}
        m_out = 0
        m_g8 = 0
        for j in range(32):
            qc, oc = divmod(j, 8)
            m_xq[j] = em.op("sync",
                            lambda e, qc=qc, oc=oc, j=j: e.dma_start(
                                out=x2st[:, j % 2, :], in_=x2re[:, oc, qc * 512:(qc + 1) * 512]),
                            waits=[("vector", m_res.get(j - 2, 0))],
                            dma="dxq%d" % (j % 2))
            m_wot[j] = em.op("tensor",
                             lambda e, qc=qc, oc=oc, j=j: [
                                 e.matmul(PA[:, (j % 2) * 512:(j % 2) * 512 + 512],
                                          wout[:, kc, oc * 128:(oc + 1) * 128],
                                          yzsb[:, kc, qc * 512:(qc + 1) * 512],
                                          start=(kc == 0), stop=(kc == 15))
                                 for kc in range(16)][-1],
                             waits=[("vector", m_cast), ("q_dld", em.qcnt("dld")),
                                    ("vector", m_res.get(j - 2, 0))])
            m_res[j] = em.op("vector",
                             lambda e, qc=qc, oc=oc, j=j: e.tensor_tensor(
                                 out=resid[:, oc, qc * 512:(qc + 1) * 512],
                                 in0=PA[:, (j % 2) * 512:(j % 2) * 512 + 512],
                                 in1=x2st[:, j % 2, :], op=ALU.add),
                             waits=[("tensor", m_wot[j]), ("q_dxq%d" % (j % 2), m_xq[j])])
            m_sq[j] = em.op("scalar",
                            lambda e, qc=qc, oc=oc: e.activation(
                                out=sqt, in_=resid[:, oc, qc * 512:(qc + 1) * 512],
                                func=AF.Square),
                            waits=[("vector", m_res[j]),
                                   ("tensor", m_stat.get(j - 1, 0))])

            def mk_stat(e, qc=qc, oc=oc):
                e.matmul(PB[:, 0:512], stp[:, 0:65],
                         resid[:, oc, qc * 512:(qc + 1) * 512],
                         start=(oc == 0), stop=False, skip_group_check=True)
                return e.matmul(PB[:, 0:512], stp[:, 65:130], sqt,
                                start=False, stop=(oc == 7), skip_group_check=True)
            m_stat[j] = em.op("tensor", mk_stat,
                              waits=[("scalar", m_sq[j]),
                                     ("vector", m_lnfree if oc == 0 else 0)])
            if oc == 7:
                m1 = em.op("vector",
                           lambda e: e.tensor_scalar(out=mrow, in0=PB[0:1, 0:512],
                                                     scalar1=1.0 / D, scalar2=None,
                                                     op0=ALU.mult),
                           waits=[("tensor", m_stat[j])])
                m_lnfree = em.op("vector",
                                 lambda e: e.tensor_scalar(out=s2row, in0=PB[64:65, 0:512],
                                                           scalar1=1.0 / D, scalar2=None,
                                                           op0=ALU.mult))
                m2 = em.op("scalar", lambda e: e.activation(out=msq, in_=mrow, func=AF.Square),
                           waits=[("vector", m_lnfree)])
                m3 = em.op("vector",
                           lambda e: e.tensor_tensor(out=varr, in0=s2row, in1=msq,
                                                     op=ALU.subtract),
                           waits=[("scalar", m2), ("vector", m_cast)])
                m4 = em.op("scalar",
                           lambda e: e.activation(out=msq, in_=varr, func=AF.Ln,
                                                  bias=cst[0:1, 1:2]),
                           waits=[("vector", m3)])
                m5 = em.op("scalar",
                           lambda e: e.activation(out=rstd_t, in_=msq, func=AF.Exp,
                                                  scale=-0.5))
                m6 = em.op("vector",
                           lambda e: e.tensor_tensor(out=mrs_t, in0=mrow, in1=rstd_t,
                                                     op=ALU.mult),
                           waits=[("scalar", m5)])
                m7 = em.op("gpsimd", lambda e: e.dma_start(out=s_hbm[0, :], in_=rstd_t[0:1, :]),
                           waits=[("vector", m6)], dma="lnb")
                m7 = em.op("gpsimd", lambda e: e.dma_start(out=s_hbm[1, :], in_=mrs_t[0:1, :]),
                           dma="lnb")
                m8 = em.op("gpsimd",
                           lambda e: e.dma_start(out=lnrep[:, 0, :],
                                                 in_=s_hbm[0:1, :].broadcast_to([128, 512])),
                           waits=[("q_lnb", m7)], dma="lnb")
                m8 = em.op("gpsimd",
                           lambda e: e.dma_start(out=lnrep[:, 1, :],
                                                 in_=s_hbm[1:2, :].broadcast_to([128, 512])),
                           dma="lnb")
                m_g8 = m8
                for oc2 in range(8):
                    em.op("vector",
                          lambda e, qc=qc, oc2=oc2: e.tensor_tensor(
                              out=resid[:, oc2, qc * 512:(qc + 1) * 512],
                              in0=resid[:, oc2, qc * 512:(qc + 1) * 512],
                              in1=lnrep[:, 0, :], op=ALU.mult),
                          waits=[("q_lnb", m8)])
                    m9 = em.op("vector",
                               lambda e, qc=qc, oc2=oc2: e.tensor_tensor(
                                   out=resid[:, oc2, qc * 512:(qc + 1) * 512],
                                   in0=resid[:, oc2, qc * 512:(qc + 1) * 512],
                                   in1=lnrep[:, 1, :], op=ALU.subtract))
                    m_out = em.op("gpsimd",
                                  lambda e, qc=qc, oc2=oc2: e.dma_start(
                                      out=outre[:, oc2, qc * 512:(qc + 1) * 512],
                                      in_=resid[:, oc2, qc * 512:(qc + 1) * 512]),
                                  waits=[("vector", m9)], dma="out")
        em.wait("gpsimd", "q_out", m_out)


def build_launch_D():
    nc = bass.Bass("TRN2", target_bir_lowering=False)
    DR = mk_dr_D(nc)
    DR["stpat"] = nc.dram_tensor("stpat", [128, 130], F32, kind="ExternalInput").ap()
    DR["consts"] = nc.dram_tensor("consts", [128, 68], F32, kind="ExternalInput").ap()
    es = contextlib.ExitStack()
    with es:
        ar = es.enter_context(Arena(nc, cols=48850))
        P = {"PA": es.enter_context(nc.psum_tensor("PA", [128, 2048], F32))[:, :],
             "PB": es.enter_context(nc.psum_tensor("PB", [65, 2048], F32))[:, :]}
        em = Emitter(nc, es)
        em.prefix = "Dp"
        blk = es.enter_context(nc.Block())
        emit_D(nc, em, ar, P, DR)
        em.build(blk)
    return nc


def host_launch_D(yz, x2, inputs):
    stpat = np.zeros((128, 130), np.float32)
    stpat[:, 0] = 1.0
    stpat[:, 129] = 1.0
    consts = np.zeros((128, 68), np.float32)
    consts[:, 0] = 1.0
    consts[:, 1] = EPS
    consts[:, 4:68] = 1.0
    yzbf = yz if yz.dtype == BF else yz
    in_maps = []
    for c in range(8):
        b, hf = divmod(c, 2)
        in_maps.append({
            "yz_in": np.ascontiguousarray(yz[b][:, hf * TOK:(hf + 1) * TOK]).astype(BF),
            "W_out": np.asarray(inputs["L1_attn_W_out"]),
            "x2h_fm": np.ascontiguousarray(_fm(x2[b])[:, hf * TOK:(hf + 1) * TOK]),
            "stpat": stpat, "consts": consts,
        })
    nc = build_launch_D()
    res = run_bass_kernel_spmd(nc, in_maps, core_ids=list(range(8)))
    x3 = np.empty((B, S, D), np.float32)
    for c in range(8):
        b, hf = divmod(c, 2)
        x3[b, hf * TOK:(hf + 1) * TOK, :] = res.results[c]["x3_fm"].T
    return x3


def mk_dr_E(nc, fused=False):
    ki = "ExternalInput"
    ko = "ExternalOutput"
    d = {}
    if not fused:
        d["x3_fm"] = nc.dram_tensor("x3_fm", [D, TOK], F32, kind=ki).ap()
    for nm, shape in (("gw", [128, 8, 16]), ("be1", [128, 16, 4]), ("be2", [16, D]),
                      ("be3", [128, 16, 4]), ("ws1", [D, D]), ("ws2", [D, D]),
                      ("ws3", [D, D]), ("bs1", [128, 8]), ("bs2", [128, 8]),
                      ("bs3", [128, 8]), ("identf", [128, 128])):
        d[nm] = nc.dram_tensor(nm, shape, F32, kind=ki).ap()
    d["gb"] = nc.dram_tensor("gb", [16], F32, kind=ki).ap()
    d["We1"] = nc.dram_tensor("We1", [16, D, 512], F32, kind=ki).ap()
    d["We2"] = nc.dram_tensor("We2", [16, 512, D], F32, kind=ki).ap()
    d["We3"] = nc.dram_tensor("We3", [16, D, 512], F32, kind=ki).ap()
    d["x4_fm"] = nc.dram_tensor("x4_fm", [D, TOK], F32, kind=ko).ap()
    d["s_hbmE"] = nc.dram_tensor("s_hbmE", [2, 512], F32, kind="Internal").ap()
    d["g_hbm"] = nc.dram_tensor("g_hbm", [16], F32, kind="Internal").ap()
    d["c_hbm"] = nc.dram_tensor("c_hbm", [16, 1024], F32, kind="Internal").ap()
    return d


def emit_E(nc, em, ar, P, DR):
    dx3 = DR["x3_fm"]
    dgw, dgb = DR["gw"], DR["gb"]
    dwe1, dwe2, dwe3 = DR["We1"], DR["We2"], DR["We3"]
    dbe1, dbe2, dbe3 = DR["be1"], DR["be2"], DR["be3"]
    dws1, dws2, dws3 = DR["ws1"], DR["ws2"], DR["ws3"]
    dbs1, dbs2, dbs3 = DR["bs1"], DR["bs2"], DR["bs3"]
    dst, dcst, didf = DR["stpat"], DR["consts"], DR["identf"]
    dout = DR["x4_fm"]
    s_hbm, g_hbm, c_hbm = DR["s_hbmE"], DR["g_hbm"], DR["c_hbm"]

    HTOK = 1024
    if True:
        x3h = ar.view(0, [128, NCH, HTOK], F32)
        x3bf = ar.view(8192, [128, NCH, HTOK], BF16)
        yacc = ar.view(12288, [128, NCH, HTOK], F32)
        webf = ar.view(20480, [128, 24576], BF16)
        wstage = ar.view(32768, [128, 4096], F32)
        chbuf = ar.view(36864, [128, 8, 512], BF16)
        ssil = ar.view(38912, [128, 512], F32)
        chf = ar.view(39424, [128, 512], F32)
        crep = ar.view(39936, [128, 2, 512], F32)
        gw = ar.view(40960, [128, 8, 16], F32)
        gbb = ar.view(41088, [128, 16], F32)
        lgs = ar.view(41104, [16, HTOK], F32)
        ltm = ar.view(42128, [128, 8, 16], F32)
        gsc1 = ar.view(42256, [128, 8, 16], F32)
        gsc2 = ar.view(42384, [128, 8, 16], F32)
        gred = ar.view(42512, [128, 8, 4], F32)
        gred2 = ar.view(42544, [128, 8, 4], F32)
        grow = ar.view(42576, [128, 8], F32)
        grow2 = ar.view(42584, [128, 8], F32)
        comb = ar.view(42592, [128, 8, 16], F32)
        combf_bf = ar.view(42128, [16, 1024], BF16)
        combf = ar.view(42720, [16, HTOK], F32)
        be1c = ar.view(43744, [128, 16, 4], F32)
        be3c = ar.view(43808, [128, 16, 4], F32)
        be2r = ar.view(43872, [16, D], BF16)
        bs1c = ar.view(44384, [128, 8], F32)
        bs2c = ar.view(44392, [128, 8], F32)
        bs3c = ar.view(44400, [128, 8], F32)
        stp = ar.view(44408, [128, 130], F32)
        cst = ar.view(44538, [128, 132], F32)
        idf = ar.view(44670, [128, 128], F32)
        resid = ar.view(44798, [128, 512], F32)
        sqt = ar.view(45310, [128, 512], F32)
        mrow = ar.view(45822, [1, 512], F32)
        s2row = ar.view(46334, [1, 512], F32)
        msq = ar.view(46846, [1, 512], F32)
        varr = ar.view(47358, [1, 512], F32)
        rstd_t = ar.view(47870, [1, 512], F32)
        mrs_t = ar.view(41104, [1, 512], F32)
        lnrep = ar.view(36864, [128, 2, 512], F32)

        PH1, PH3, PYO, PST = P["PH1"], P["PH3"], P["PYO"], P["PST"]
        x3re = dx3.rearrange("(c p) t -> p c t", p=128)
        outre = dout.rearrange("(c p) t -> p c t", p=128)

        for (d_, s_) in ((gw, dgw), (be1c, dbe1), (be3c, dbe3),
                         (bs1c, dbs1), (bs2c, dbs2), (bs3c, dbs3),
                         (stp, dst), (cst, dcst), (idf, didf)):
            em.op("sync", lambda e, d_=d_, s_=s_: e.dma_start(out=d_, in_=s_), dma="eld")
        m_gw = em.op("sync", lambda e: e.dma_start(out=g_hbm, in_=dgb), dma="eld")
        m_gbl = em.op("sync", lambda e: e.dma_start(
            out=gbb, in_=g_hbm.unsqueeze(0).broadcast_to([128, 16])),
            waits=[("q_eld", m_gw)], dma="eld")
        m_b2l = em.op("sync", lambda e: e.dma_start(out=combf, in_=dbe2), dma="eld")
        ELD_ALL = em.qcnt("eld")
        m_b2c = em.op("vector", lambda e: e.tensor_copy(out=be2r, in_=combf),
                      waits=[("q_eld", ELD_ALL)])

        ws1v = webf[:, 0:8192].rearrange("p (a b) -> p a b", b=1024)
        ws3v = webf[:, 8192:16384].rearrange("p (a b) -> p a b", b=1024)
        ws2v = webf[:, 16384:24576].rearrange("p (a b) -> p a b", b=1024)

        def we_views(sl):
            base = sl * 12288
            return (webf[:, base:base + 4096].rearrange("p (a b) -> p a b", b=512),
                    webf[:, base + 4096:base + 8192].rearrange("p (a b) -> p a b", b=512),
                    webf[:, base + 8192:base + 12288].rearrange("p (a b) -> p a b", b=1024))

        m_x3 = {}
        m_cast = {}
        m_tpe = {}
        m_gate = {}
        m_sil = {}
        m_chf = {}
        m_ch = {}
        m_yom = {}
        m_ya = {}
        m_out = 0
        m_lnfree = 0
        m_res = {}
        m_wc = {}
        m_crep = {}
        m_chbm = {}
        wslot = [0]
        lastchf = [0]
        lastya = [0]

        wsl = [0]

        def _stage_piece(srcv, dstv, extra_waits=()):
            s_ = wsl[0] & 1
            wsl[0] += 1
            ncols = 1
            for d_ in srcv.shape[1:]:
                ncols *= d_
            stg = wstage.rearrange("p (h n) -> p h n", h=2)[:, s_, 0:ncols]
            if len(srcv.shape) == 3:
                stg = stg.rearrange("p (a b) -> p a b", b=srcv.shape[2])
            wl = [("vector", m_wc.get(s_, 0))] + list(extra_waits)
            m_l = em.op("sync",
                        lambda e, stg=stg, srcv=srcv: e.dma_start(out=stg, in_=srcv),
                        waits=wl, dma="ew%d" % s_)
            m_wc[s_] = em.op("vector",
                             lambda e, stg=stg, dstv=dstv: e.tensor_copy(out=dstv, in_=stg),
                             waits=[("q_ew%d" % s_, m_l)])
            return m_wc[s_]

        def load_shared(hp):
            pieces = [(dws1, ws1v), (dws3, ws3v), (dws2, ws2v)]
            m_last = 0
            for pi, (dsrc, dstv) in enumerate(pieces):
                srcre = dsrc.rearrange("(c p) i -> p c i", p=128)
                for hf in range(4):
                    ew = []
                    if pi == 0 and hf == 0:
                        ew.append(("tensor", (lambda h=hp: m_yom.get((h - 1, 15, 7), 0))))
                    m_last = _stage_piece(srcre[:, hf * 2:hf * 2 + 2, :],
                                          dstv[:, hf * 2:hf * 2 + 2, :], ew)
            return m_last

        def load_expert(hp, ei):
            sl = wslot[0]
            wslot[0] ^= 1
            w1v, w3v, w2v = we_views(sl)
            pieces = [(dwe1[ei].rearrange("(c p) i -> p c i", p=128), w1v),
                      (dwe3[ei].rearrange("(c p) i -> p c i", p=128), w3v),
                      (dwe2[ei].rearrange("(c p) i -> p c i", p=128), w2v)]
            m_last = 0
            for pi, (srcre, dstv) in enumerate(pieces):
                nch_ = dstv.shape[1]
                for hf in range(2):
                    ew = []
                    if pi == 0 and hf == 0:
                        ew.append(("tensor", (lambda h=hp, i=ei: m_yom.get((h, i - 2, 7),
                                              m_yom.get((h - 1, 15, 7), 0) if i - 2 < 0 else 0))))
                    h0 = hf * (nch_ // 2)
                    h1 = (hf + 1) * (nch_ // 2)
                    m_last = _stage_piece(srcre[:, h0:h1, :], dstv[:, h0:h1, :], ew)
            return sl, m_last

        V = "vector"
        for hp in range(2):
            hcol = hp * HTOK
            m_x3[hp] = em.op("sync",
                             lambda e, hcol=hcol: e.dma_start(
                                 out=x3h, in_=x3re[:, :, hcol:hcol + HTOK]),
                             waits=[("q_out", m_out),
                                    ("tensor", (lambda h=hp: m_res.get((h - 1, "laststat"), 0)))],
                             dma="ex3")
            m_cast[hp] = em.op(V, lambda e: e.tensor_copy(out=x3bf, in_=x3h),
                               waits=[("q_ex3", m_x3[hp])])
            def mk_lg(e):
                inst = None
                for t2 in range(2):
                    for kc in range(8):
                        inst = e.matmul(PH1[0:16, t2, :], gw[:, kc, :],
                                        x3h[:, kc, t2 * 512:(t2 + 1) * 512],
                                        start=(kc == 0), stop=(kc == 7))
                return inst
            m_lgm = em.op("tensor", mk_lg,
                          waits=[("q_ex3", m_x3[hp]), ("q_eld", ELD_ALL if hp == 0 else 0),
                                 ("vector", (lambda h=hp: m_res.get((h - 1, "lastln"), 0))),
                                 ("scalar", (lambda h=hp: m_res.get((h - 1, "lastsq"), 0)))])
            m_lge = em.op(V, lambda e: e.tensor_copy(
                out=lgs.rearrange("p (a b) -> p a b", a=2), in_=PH1[0:16, :, :]),
                waits=[("tensor", m_lgm), ("vector", m_b2c if hp == 0 else 0)])
            for tl in range(8):
                m_tp = em.op("tensor",
                             lambda e, tl=tl: e.transpose(
                                 PH3[:, 0, 0:16], lgs[:, tl * 128:(tl + 1) * 128],
                                 idf[0:16, 0:16]),
                             waits=[("vector", m_lge if tl == 0 else 0),
                                    ("vector", m_tpe.get((hp, tl - 1), 0))])
                m_tpe[(hp, tl)] = em.op(V,
                                        lambda e, tl=tl: e.tensor_copy(
                                            out=ltm[:, tl, :], in_=PH3[:, 0, 0:16]),
                                        waits=[("tensor", m_tp)])
            g4 = gsc2.rearrange("p a (g f) -> p (a g) f", g=4)
            gredf = gred.rearrange("p a b -> p (a b)")
            gred2f = gred2.rearrange("p a b -> p (a b)")
            comb4 = comb.rearrange("p a (g f) -> p (a g) f", g=4)
            em.op(V, lambda e: e.tensor_reduce(out=grow, in_=ltm, axis=AX.X, op=ALU.max),
                  waits=[("vector", m_tpe[(hp, 7)])])
            em.op(V, lambda e: e.tensor_tensor(
                out=gsc1, in0=ltm, in1=grow.unsqueeze(2).broadcast_to([128, 8, 16]),
                op=ALU.subtract))
            m_ex = em.op("scalar", lambda e: e.activation(
                out=gsc1.rearrange("p a b -> p (a b)"),
                in_=gsc1.rearrange("p a b -> p (a b)"), func=AF.Exp),
                waits=[("vector", em.cnt["vector"])])
            em.op(V, lambda e: e.tensor_reduce(out=grow, in_=gsc1, axis=AX.X, op=ALU.add),
                  waits=[("scalar", m_ex)])
            em.op(V, lambda e: e.reciprocal(out=grow2, in_=grow))
            em.op(V, lambda e: e.tensor_tensor(
                out=gsc1, in0=gsc1, in1=grow2.unsqueeze(2).broadcast_to([128, 8, 16]),
                op=ALU.mult))
            em.op(V, lambda e: e.tensor_tensor(
                out=gsc2, in0=gsc1, in1=gbb.unsqueeze(1).broadcast_to([128, 8, 16]),
                op=ALU.add))
            em.op(V, lambda e: e.tensor_reduce(out=gredf, in_=g4, axis=AX.X, op=ALU.max))
            em.op(V, lambda e: e.tensor_tensor(
                out=comb4, in0=g4, in1=gredf.unsqueeze(2).broadcast_to([128, 32, 4]),
                op=ALU.is_ge))
            em.op(V, lambda e: e.tensor_scalar(
                out=comb.rearrange("p a b -> p (a b)"), in0=comb.rearrange("p a b -> p (a b)"),
                scalar1=-1e30, scalar2=None, op0=ALU.mult))
            em.op(V, lambda e: e.tensor_tensor(out=comb4, in0=comb4, in1=g4, op=ALU.add))
            em.op(V, lambda e: e.tensor_reduce(out=gred2f, in_=comb4, axis=AX.X, op=ALU.max))
            em.op(V, lambda e: e.tensor_tensor(out=gredf, in0=gredf, in1=gred2f, op=ALU.add))
            em.op(V, lambda e: e.tensor_reduce(out=grow, in_=gred, axis=AX.X, op=ALU.max))
            em.op(V, lambda e: e.tensor_tensor(
                out=gred2, in0=gred, in1=grow.unsqueeze(2).broadcast_to([128, 8, 4]),
                op=ALU.is_ge))
            em.op(V, lambda e: e.tensor_scalar(out=gred2f, in0=gred2f, scalar1=-1e30,
                                               scalar2=None, op0=ALU.mult))
            em.op(V, lambda e: e.tensor_tensor(out=gred2, in0=gred2, in1=gred, op=ALU.add))
            em.op(V, lambda e: e.tensor_reduce(out=grow2, in_=gred2, axis=AX.X, op=ALU.max))
            em.op(V, lambda e: e.tensor_tensor(
                out=gred2, in0=gred, in1=grow2.unsqueeze(2).broadcast_to([128, 8, 4]),
                op=ALU.is_ge))
            em.op(V, lambda e: e.tensor_scalar(out=gred2f, in0=gred2f, scalar1=1e30,
                                               scalar2=-1e30, op0=ALU.mult, op1=ALU.add))
            em.op(V, lambda e: e.tensor_tensor(
                out=g4, in0=g4, in1=gred2f.unsqueeze(2).broadcast_to([128, 32, 4]),
                op=ALU.add))
            em.op(V, lambda e: e.tensor_reduce(out=grow, in_=gsc2, axis=AX.X, op=ALU.max))
            em.op(V, lambda e: e.tensor_tensor(
                out=comb, in0=gsc2, in1=grow.unsqueeze(2).broadcast_to([128, 8, 16]),
                op=ALU.is_ge))
            em.op(V, lambda e: e.tensor_scalar(
                out=comb.rearrange("p a b -> p (a b)"), in0=comb.rearrange("p a b -> p (a b)"),
                scalar1=-1e30, scalar2=None, op0=ALU.mult))
            em.op(V, lambda e: e.tensor_tensor(out=comb, in0=comb, in1=gsc2, op=ALU.add))
            em.op(V, lambda e: e.tensor_reduce(out=grow2, in_=comb, axis=AX.X, op=ALU.max))
            em.op(V, lambda e: e.tensor_tensor(
                out=comb, in0=gsc2, in1=grow2.unsqueeze(2).broadcast_to([128, 8, 16]),
                op=ALU.is_ge))
            m_cmb = em.op(V, lambda e: e.tensor_tensor(out=comb, in0=comb, in1=gsc1,
                                                       op=ALU.mult))
            for tl in range(8):
                m_tp = em.op("tensor",
                             lambda e, tl=tl: e.transpose(
                                 PH3[0:16, 0, 0:128], comb[:, tl, :], idf),
                             waits=[("vector", m_cmb if tl == 0 else 0),
                                    ("vector", m_gate.get((hp, tl - 1), 0))])
                m_gate[(hp, tl)] = em.op(V,
                                         lambda e, tl=tl: e.tensor_copy(
                                             out=combf[:, tl * 128:(tl + 1) * 128],
                                             in_=PH3[0:16, 0, 0:128]),
                                         waits=[("tensor", m_tp)])
            m_chbm[hp] = em.op("gpsimd", lambda e: e.dma_start(out=c_hbm, in_=combf),
                               waits=[("vector", m_gate[(hp, 7)])], dma="cmb")
            m_cbf = em.op(V, lambda e: e.tensor_copy(out=combf_bf, in_=combf))
            for ei in range(-1, 16):
                if ei < 0:
                    m_w = load_shared(hp)
                    nic, wid = 8, 1024
                else:
                    sl, m_w = load_expert(hp, ei)
                    w1v, w3v, w2v = we_views(sl)
                    nic, wid = 4, 512
                    for tc in range(2):
                        m_crep[(hp, ei, tc)] = em.op(
                            "gpsimd",
                            lambda e, ei=ei, tc=tc: e.dma_start(
                                out=crep[:, tc, :],
                                in_=c_hbm[ei, tc * 512:(tc + 1) * 512]
                                .unsqueeze(0).broadcast_to([128, 512])),
                            waits=[("q_cmb", m_chbm[hp]),
                                   ("vector", (lambda h=hp, i=ei, t=tc: m_ch.get((h, i - 1, t, 3), 0)))],
                            dma="crep%d" % tc)
                for tc in range(2):
                    t0 = tc * 512
                    for ic in range(nic):
                        icp = ic % 2
                        w1 = ws1v if ei < 0 else w1v
                        w3 = ws3v if ei < 0 else w3v

                        def mk1(e, w1=w1, ic=ic, icp=icp, t0=t0):
                            inst = None
                            for kc in range(8):
                                inst = e.matmul(PH1[:, icp, :],
                                                w1[:, kc, ic * 128:(ic + 1) * 128]
                                                if w1 is not ws1v else w1[:, kc, ic * 128:(ic + 1) * 128],
                                                x3bf[:, kc, t0:t0 + 512],
                                                start=(kc == 0), stop=(kc == 7))
                            return inst
                        h1w = [("vector", m_w if (tc == 0 and ic == 0) else 0),
                               ("vector", m_cast[hp] if (ei < 0 and tc == 0 and ic == 0) else 0)]
                        if ic >= 2:
                            h1w.append(("scalar", m_sil[(hp, ei, tc, ic - 2)]))
                        else:
                            h1w.append(("vector", lastchf[0]))
                        m_h1 = em.op("tensor", mk1, waits=h1w)

                        def mk3(e, w3=w3, ic=ic, icp=icp, t0=t0):
                            inst = None
                            for kc in range(8):
                                inst = e.matmul(PH3[:, icp, :],
                                                w3[:, kc, ic * 128:(ic + 1) * 128],
                                                x3bf[:, kc, t0:t0 + 512],
                                                start=(kc == 0), stop=(kc == 7))
                            return inst
                        m_h3 = em.op("tensor", mk3,
                                     waits=[("vector", m_chf.get((hp, ei, tc, ic - 2), lastchf[0]))])
                        bcol = (bs1c[:, hp * 0:1] if False else bs1c[:, 0:1]) if ei < 0 else be1c[:, ei, ic:ic + 1]
                        bcol1 = bs1c[:, ic // 1:ic // 1 + 1] if ei < 0 else be1c[:, ei, ic:ic + 1]
                        m_sil[(hp, ei, tc, ic)] = em.op(
                            "scalar",
                            lambda e, ei=ei, ic=ic, icp=icp: e.activation(
                                out=ssil, in_=PH1[:, icp, :], func=AF.Silu,
                                bias=(bs1c[:, ic:ic + 1] if ei < 0 else be1c[:, ei, ic:ic + 1])),
                            waits=[("tensor", m_h1),
                                   ("vector", m_ch.get((hp, ei, tc, ic - 1), lastchf[0]))])
                        m_chf[(hp, ei, tc, ic)] = em.op(
                            V,
                            lambda e, ei=ei, ic=ic, icp=icp: e.scalar_tensor_tensor(
                                out=chf, in0=PH3[:, icp, :],
                                scalar=(bs3c[:, ic:ic + 1] if ei < 0 else be3c[:, ei, ic:ic + 1]),
                                in1=ssil, op0=ALU.add, op1=ALU.mult),
                            waits=[("tensor", m_h3),
                                   ("scalar", m_sil[(hp, ei, tc, ic)])])
                        if ei < 0:
                            m_ch[(hp, ei, tc, ic)] = em.op(
                                V, lambda e, ic=ic: e.tensor_copy(out=chbuf[:, ic, :], in_=chf))
                        else:
                            m_ch[(hp, ei, tc, ic)] = em.op(
                                V, lambda e, ic=ic, tc=tc: e.tensor_tensor(
                                    out=chbuf[:, ic, :], in0=chf, in1=crep[:, tc, :],
                                    op=ALU.mult),
                                waits=[("q_crep%d" % tc, m_crep[(hp, ei, tc)])])
                    lastchf[0] = m_ch[(hp, ei, tc, nic - 1)]
                    for oc in range(8):
                        ocp = oc % 2
                        w2 = ws2v if ei < 0 else w2v

                        def mk2(e, w2=w2, oc=oc, ocp=ocp, nic=nic, ei=ei, tc=tc):
                            inst = None
                            for ic in range(nic):
                                inst = e.matmul(PYO[:, ocp, :],
                                                w2[:, ic, oc * 128:(oc + 1) * 128],
                                                chbuf[:, ic, :],
                                                start=(ic == 0), stop=(ei < 0 or ic < nic - 1) and (ic == nic - 1),
                                                skip_group_check=True)
                            if ei >= 0:
                                inst = e.matmul(PYO[:, ocp, :], be2r[:, oc * 128:(oc + 1) * 128],
                                                combf_bf[:, tc * 512:(tc + 1) * 512],
                                                start=False, stop=True, skip_group_check=True)
                            return inst
                        m_yom[(hp, ei, oc)] = em.op(
                            "tensor", mk2,
                            waits=[("vector", m_ch[(hp, ei, tc, nic - 1)]),
                                   ("vector", m_ya.get((hp, ei, tc, oc - 2), lastya[0]))])
                        if ei < 0:
                            m_ya[(hp, ei, tc, oc)] = em.op(
                                V, lambda e, oc=oc, ocp=ocp, t0=t0: e.tensor_scalar(
                                    out=yacc[:, oc, t0:t0 + 512], in0=PYO[:, ocp, :],
                                    scalar1=bs2c[:, oc:oc + 1], scalar2=None, op0=ALU.add),
                                waits=[("tensor", m_yom[(hp, ei, oc)])])
                        else:
                            m_ya[(hp, ei, tc, oc)] = em.op(
                                V, lambda e, oc=oc, ocp=ocp, t0=t0: e.tensor_tensor(
                                    out=yacc[:, oc, t0:t0 + 512],
                                    in0=yacc[:, oc, t0:t0 + 512],
                                    in1=PYO[:, ocp, :], op=ALU.add),
                                waits=[("tensor", m_yom[(hp, ei, oc)])])
                        if oc == 7:
                            lastya[0] = m_ya[(hp, ei, tc, 7)]
            for tc in range(2):
                t0 = tc * 512
                m_stat_l = 0
                for oc in range(8):
                    m_r0 = em.op(V,
                                 lambda e, oc=oc, t0=t0: e.tensor_tensor(
                                     out=resid, in0=x3h[:, oc, t0:t0 + 512],
                                     in1=yacc[:, oc, t0:t0 + 512], op=ALU.add),
                                 waits=[("vector", m_ya[(hp, 15, tc, 7)]),
                                        ("tensor", m_stat_l)])
                    m_sq = em.op("scalar",
                                 lambda e: e.activation(out=sqt, in_=resid, func=AF.Square),
                                 waits=[("vector", m_r0)])
                    def mk_stat(e, oc=oc):
                        e.matmul(PST[:, :], stp[:, 0:65], resid,
                                 start=(oc == 0), stop=False, skip_group_check=True)
                        return e.matmul(PST[:, :], stp[:, 65:130], sqt,
                                        start=False, stop=(oc == 7), skip_group_check=True)
                    m_stat_l = em.op("tensor", mk_stat,
                                     waits=[("scalar", m_sq),
                                            ("vector", m_lnfree if oc == 0 else 0)])
                m_res[(hp, "laststat")] = m_stat_l
                m1 = em.op(V, lambda e: e.tensor_scalar(out=mrow, in0=PST[0:1, :],
                                                        scalar1=1.0 / D, scalar2=None,
                                                        op0=ALU.mult),
                           waits=[("tensor", m_stat_l)])
                m_lnfree = em.op(V, lambda e: e.tensor_scalar(out=s2row, in0=PST[64:65, :],
                                                              scalar1=1.0 / D, scalar2=None,
                                                              op0=ALU.mult))
                m2 = em.op("scalar", lambda e: e.activation(out=msq, in_=mrow, func=AF.Square),
                           waits=[("vector", m_lnfree)])
                m3 = em.op(V, lambda e: e.tensor_tensor(out=varr, in0=s2row, in1=msq,
                                                        op=ALU.subtract),
                           waits=[("scalar", m2)])
                m4 = em.op("scalar", lambda e: e.activation(out=msq, in_=varr, func=AF.Ln,
                                                            bias=cst[0:1, 1:2]),
                           waits=[("vector", m3)])
                m5 = em.op("scalar", lambda e: e.activation(out=rstd_t, in_=msq, func=AF.Exp,
                                                            scale=-0.5))
                m6 = em.op(V, lambda e: e.tensor_tensor(out=mrs_t, in0=mrow, in1=rstd_t,
                                                        op=ALU.mult),
                           waits=[("scalar", m5)])
                m7 = em.op("gpsimd", lambda e: e.dma_start(out=s_hbm[0, :], in_=rstd_t[0:1, :]),
                           waits=[("vector", m6)], dma="lnb")
                m7 = em.op("gpsimd", lambda e: e.dma_start(out=s_hbm[1, :], in_=mrs_t[0:1, :]),
                           dma="lnb")
                m8 = em.op("gpsimd",
                           lambda e: e.dma_start(out=lnrep[:, 0, :],
                                                 in_=s_hbm[0:1, :].broadcast_to([128, 512])),
                           waits=[("q_lnb", m7),
                                  ("vector", m_ch[(hp, 15, 1, 3)])], dma="lnb")
                m8 = em.op("gpsimd",
                           lambda e: e.dma_start(out=lnrep[:, 1, :],
                                                 in_=s_hbm[1:2, :].broadcast_to([128, 512])),
                           dma="lnb")
                m_sq2 = 0
                for oc2 in range(8):
                    m_r1 = em.op(V,
                                 lambda e, oc2=oc2, t0=t0: e.tensor_tensor(
                                     out=yacc[:, oc2, t0:t0 + 512],
                                     in0=x3h[:, oc2, t0:t0 + 512],
                                     in1=yacc[:, oc2, t0:t0 + 512], op=ALU.add),
                                 waits=[("q_lnb", m8)])
                    m_r1 = em.op(V,
                                 lambda e, oc2=oc2, t0=t0: e.tensor_tensor(
                                     out=yacc[:, oc2, t0:t0 + 512],
                                     in0=yacc[:, oc2, t0:t0 + 512],
                                     in1=lnrep[:, 0, :], op=ALU.mult))
                    m_r1 = em.op(V,
                                 lambda e, oc2=oc2, t0=t0: e.tensor_tensor(
                                     out=yacc[:, oc2, t0:t0 + 512],
                                     in0=yacc[:, oc2, t0:t0 + 512],
                                     in1=lnrep[:, 1, :], op=ALU.subtract))
                    m_out = em.op("gpsimd",
                                  lambda e, oc2=oc2, t0=t0, hcol=hcol: e.dma_start(
                                      out=outre[:, oc2, hcol + t0:hcol + t0 + 512],
                                      in_=yacc[:, oc2, t0:t0 + 512]),
                                  waits=[("vector", m_r1)], dma="out")
                m_res[(hp, "lastln")] = m_r1
                m_res[(hp, "lastsq")] = em.cnt["scalar"]
        em.wait("gpsimd", "q_out", m_out)


def mk_P_E(PS1, PS2):
    return {
        "PH1": PS1[:, 0:1024].rearrange("p (a b) -> p a b", b=512),
        "PH3": PS1[:, 1024:2048].rearrange("p (a b) -> p a b", b=512),
        "PYO": PS2[:, 0:1024].rearrange("p (a b) -> p a b", b=512),
        "PST": PS2[0:65, 1024:1536],
    }


def build_launch_E():
    nc = bass.Bass("TRN2", target_bir_lowering=False)
    DR = mk_dr_E(nc)
    DR["stpat"] = nc.dram_tensor("stpat", [128, 130], F32, kind="ExternalInput").ap()
    DR["consts"] = nc.dram_tensor("consts", [128, 132], F32, kind="ExternalInput").ap()
    es = contextlib.ExitStack()
    with es:
        ar = es.enter_context(Arena(nc, cols=48850))
        PS1 = es.enter_context(nc.psum_tensor("PS1", [128, 2048], F32))[:, :]
        PS2 = es.enter_context(nc.psum_tensor("PS2", [128, 2048], F32))[:, :]
        P = mk_P_E(PS1, PS2)
        em = Emitter(nc, es)
        em.prefix = "E"
        blk = es.enter_context(nc.Block())
        emit_E(nc, em, ar, P, DR)
        em.build(blk)
    return nc


def host_launch_E(x3, inputs):
    stpat = np.zeros((128, 130), np.float32)
    stpat[:, 0] = 1.0
    stpat[:, 129] = 1.0
    consts = np.zeros((128, 132), np.float32)
    consts[:, 0] = 1.0
    consts[:, 1] = EPS
    consts[:, 4:132] = 1.0
    identf = np.eye(128, dtype=np.float32)
    gwl = np.ascontiguousarray(
        np.asarray(inputs["L1_ffn_gate_w"]).T.reshape(8, 128, 16).transpose(1, 0, 2))
    be1l = np.ascontiguousarray(
        np.asarray(inputs["L1_ffn_be1"]).reshape(16, 4, 128).transpose(2, 0, 1))
    be3l = np.ascontiguousarray(
        np.asarray(inputs["L1_ffn_be3"]).reshape(16, 4, 128).transpose(2, 0, 1))
    bs1l = np.ascontiguousarray(np.asarray(inputs["L1_ffn_bs1"]).reshape(8, 128).T)
    bs2l = np.ascontiguousarray(np.asarray(inputs["L1_ffn_bs2"]).reshape(8, 128).T)
    bs3l = np.ascontiguousarray(np.asarray(inputs["L1_ffn_bs3"]).reshape(8, 128).T)
    in_maps = []
    for c in range(8):
        b, hf = divmod(c, 2)
        in_maps.append({
            "x3_fm": np.ascontiguousarray(_fm(x3[b])[:, hf * TOK:(hf + 1) * TOK]),
            "gw": gwl, "gb": np.asarray(inputs["L1_ffn_gate_b"]),
            "We1": np.asarray(inputs["L1_ffn_We1"]),
            "We2": np.asarray(inputs["L1_ffn_We2"]),
            "We3": np.asarray(inputs["L1_ffn_We3"]),
            "be1": be1l, "be2": np.asarray(inputs["L1_ffn_be2"]), "be3": be3l,
            "ws1": np.asarray(inputs["L1_ffn_ws1"]),
            "ws2": np.asarray(inputs["L1_ffn_ws2"]),
            "ws3": np.asarray(inputs["L1_ffn_ws3"]),
            "bs1": bs1l, "bs2": bs2l, "bs3": bs3l,
            "stpat": stpat, "consts": consts, "identf": identf,
        })
    nc = build_launch_E()
    res = run_bass_kernel_spmd(nc, in_maps, core_ids=list(range(8)))
    x4 = np.empty((B, S, D), np.float32)
    for c in range(8):
        b, hf = divmod(c, 2)
        x4[b, hf * TOK:(hf + 1) * TOK, :] = res.results[c]["x4_fm"].T
    return x4


RG_PAIRS = [[0, 1], [2, 3], [4, 5], [6, 7]]
RG_ALL = [[0, 1, 2, 3, 4, 5, 6, 7]]


def emit_D2(nc, em, ar, P, DR):
    dyz = DR["yz_fm"]
    dwo = DR["W_outh"]
    dP = DR["P_mix"]
    dmix = DR["mix_own"]
    dx2 = DR["x2_own"]
    dst_, dcst = DR["stpat"], DR["consts"]
    dout = DR["x3_fm"]
    s_hbm = DR["s_hbmD"]
    if True:
        wout = ar.view(0, [128, 8, D], BF16)
        yzsb = ar.view(4096, [128, 8, S], BF16)
        wstage = ar.view(20480, [128, 4, D], F32)
        pstg = ar.view(24576, [128, 2, 512], F32)
        xstg = ar.view(25600, [128, 2, 512], F32)
        resid = ar.view(26624, [128, NCH, TOK], F32)
        sqt = ar.view(43008, [128, 512], F32)
        stp = ar.view(43520, [128, 130], F32)
        cst = ar.view(43650, [128, 68], F32)
        mrow = ar.view(43718, [1, 512], F32)
        s2row = ar.view(44230, [1, 512], F32)
        msq = ar.view(44742, [1, 512], F32)
        varr = ar.view(45254, [1, 512], F32)
        rstd_t = ar.view(45766, [1, 512], F32)
        mrs_t = ar.view(46278, [1, 512], F32)
        lnrep = ar.view(46790, [128, 2, 512], F32)
        PA, PB = P["PA"], P["PB"]

        yzre = dyz.rearrange("(c p) t -> p c t", p=128)
        wore = dwo.rearrange("(c p) o -> p c o", p=128)
        mixre = dmix.rearrange("(c p) t -> p c t", p=128)
        x2re = dx2.rearrange("(c p) t -> p c t", p=128)
        outre = dout.rearrange("(c p) t -> p c t", p=128)

        em.op("sync", lambda e: e.dma_start(out=stp, in_=dst_), dma="dld")
        em.op("sync", lambda e: e.dma_start(out=cst, in_=dcst), dma="dld")
        m_yzl = em.op("sync", lambda e: e.dma_start(out=yzsb, in_=yzre), dma="dld")
        m_cast = 0
        for hf in range(2):
            m_ld = em.op("sync",
                         lambda e, hf=hf: e.dma_start(
                             out=wstage, in_=wore[:, hf * 4:hf * 4 + 4, :]),
                         waits=[("vector", m_cast)], dma="dld")
            m_cast = em.op("vector",
                           lambda e, hf=hf: e.tensor_copy(
                               out=wout[:, hf * 4:hf * 4 + 4, :], in_=wstage),
                           waits=[("q_dld", m_ld)])
        m_mm = {}
        m_pe = {}
        m_ps = {}
        for j in range(64):
            hp, r_ = divmod(j, 32)
            qc, oc = divmod(r_, 8)
            m_mm[j] = em.op("tensor",
                            lambda e, hp=hp, qc=qc, oc=oc, j=j: [
                                e.matmul(PA[:, (j % 2) * 512:(j % 2) * 512 + 512],
                                         wout[:, kc, oc * 128:(oc + 1) * 128],
                                         yzsb[:, kc,
                                              hp * 2048 + qc * 512:hp * 2048 + qc * 512 + 512],
                                         start=(kc == 0), stop=(kc == 7))
                                for kc in range(8)][-1],
                            waits=[("vector", m_cast),
                                   ("q_dld", em.qcnt("dld") if j == 0 else 0),
                                   ("vector", m_pe.get(j - 2, 0))])
            m_pe[j] = em.op("vector",
                            lambda e, j=j: e.tensor_copy(
                                out=pstg[:, j % 2, :],
                                in_=PA[:, (j % 2) * 512:(j % 2) * 512 + 512]),
                            waits=[("tensor", m_mm[j]),
                                   ("q_dp%d" % (j % 2), m_ps.get(j - 2, 0))])
            m_ps[j] = em.op("sync",
                            lambda e, hp=hp, qc=qc, oc=oc, j=j: e.dma_start(
                                out=dP[hp].rearrange("(c p) t -> p c t", p=128)[
                                    :, oc, qc * 512:(qc + 1) * 512],
                                in_=pstg[:, j % 2, :]),
                            waits=[("vector", m_pe[j])], dma="dp%d" % (j % 2))
        m_rs = em.op("gpsimd",
                     lambda e: e.collective_compute(
                         "ReduceScatter", ALU.add, replica_groups=RG_PAIRS,
                         ins=[dP[:, :, :]], outs=[dmix[:, :]]),
                     waits=[("q_dp0", em.qcnt("dp0")), ("q_dp1", em.qcnt("dp1"))])
        m_mx = {}
        m_res = {}
        m_sq = {}
        m_stat = {}
        m_lnfree = 0
        m_out = 0
        for j in range(32):
            qc, oc = divmod(j, 8)
            m_mx[j] = em.op("sync",
                            lambda e, qc=qc, oc=oc: e.dma_start(
                                out=resid[:, oc, qc * 512:(qc + 1) * 512],
                                in_=mixre[:, oc, qc * 512:(qc + 1) * 512]),
                            waits=[("gpsimd", m_rs if j == 0 else 0)],
                            dma="dmx")
            m_xq = em.op("sync",
                         lambda e, qc=qc, oc=oc, j=j: e.dma_start(
                             out=xstg[:, j % 2, :],
                             in_=x2re[:, oc, qc * 512:(qc + 1) * 512]),
                         waits=[("vector", m_res.get(j - 2, 0))],
                         dma="dxq%d" % (j % 2))
            m_res[j] = em.op("vector",
                             lambda e, qc=qc, oc=oc, j=j: e.tensor_tensor(
                                 out=resid[:, oc, qc * 512:(qc + 1) * 512],
                                 in0=resid[:, oc, qc * 512:(qc + 1) * 512],
                                 in1=xstg[:, j % 2, :], op=ALU.add),
                             waits=[("q_dmx", m_mx[j]),
                                    ("q_dxq%d" % (j % 2), m_xq)])
            m_sq[j] = em.op("scalar",
                            lambda e, qc=qc, oc=oc: e.activation(
                                out=sqt, in_=resid[:, oc, qc * 512:(qc + 1) * 512],
                                func=AF.Square),
                            waits=[("vector", m_res[j]),
                                   ("tensor", m_stat.get(j - 1, 0))])

            def mk_stat(e, qc=qc, oc=oc):
                e.matmul(PB[:, 0:512], stp[:, 0:65],
                         resid[:, oc, qc * 512:(qc + 1) * 512],
                         start=(oc == 0), stop=False, skip_group_check=True)
                return e.matmul(PB[:, 0:512], stp[:, 65:130], sqt,
                                start=False, stop=(oc == 7), skip_group_check=True)
            m_stat[j] = em.op("tensor", mk_stat,
                              waits=[("scalar", m_sq[j]),
                                     ("vector", m_lnfree if oc == 0 else 0)])
            if oc == 7:
                m1 = em.op("vector",
                           lambda e: e.tensor_scalar(out=mrow, in0=PB[0:1, 0:512],
                                                     scalar1=1.0 / D, scalar2=None,
                                                     op0=ALU.mult),
                           waits=[("tensor", m_stat[j])])
                m_lnfree = em.op("vector",
                                 lambda e: e.tensor_scalar(out=s2row, in0=PB[64:65, 0:512],
                                                           scalar1=1.0 / D, scalar2=None,
                                                           op0=ALU.mult))
                m2 = em.op("scalar", lambda e: e.activation(out=msq, in_=mrow,
                                                            func=AF.Square),
                           waits=[("vector", m_lnfree)])
                m3 = em.op("vector",
                           lambda e: e.tensor_tensor(out=varr, in0=s2row, in1=msq,
                                                     op=ALU.subtract),
                           waits=[("scalar", m2)])
                m4 = em.op("scalar",
                           lambda e: e.activation(out=msq, in_=varr, func=AF.Ln,
                                                  bias=cst[0:1, 1:2]),
                           waits=[("vector", m3)])
                m5 = em.op("scalar",
                           lambda e: e.activation(out=rstd_t, in_=msq, func=AF.Exp,
                                                  scale=-0.5))
                m6 = em.op("vector",
                           lambda e: e.tensor_tensor(out=mrs_t, in0=mrow, in1=rstd_t,
                                                     op=ALU.mult),
                           waits=[("scalar", m5)])
                m7 = em.op("gpsimd", lambda e: e.dma_start(out=s_hbm[0, :], in_=rstd_t[0:1, :]),
                           waits=[("vector", m6)], dma="lnb")
                m7 = em.op("gpsimd", lambda e: e.dma_start(out=s_hbm[1, :], in_=mrs_t[0:1, :]),
                           dma="lnb")
                m8 = em.op("gpsimd",
                           lambda e: e.dma_start(out=lnrep[:, 0, :],
                                                 in_=s_hbm[0:1, :].broadcast_to([128, 512])),
                           waits=[("q_lnb", m7)], dma="lnb")
                m8 = em.op("gpsimd",
                           lambda e: e.dma_start(out=lnrep[:, 1, :],
                                                 in_=s_hbm[1:2, :].broadcast_to([128, 512])),
                           dma="lnb")
                for oc2 in range(8):
                    em.op("vector",
                          lambda e, qc=qc, oc2=oc2: e.tensor_tensor(
                              out=resid[:, oc2, qc * 512:(qc + 1) * 512],
                              in0=resid[:, oc2, qc * 512:(qc + 1) * 512],
                              in1=lnrep[:, 0, :], op=ALU.mult),
                          waits=[("q_lnb", m8)])
                    m9 = em.op("vector",
                               lambda e, qc=qc, oc2=oc2: e.tensor_tensor(
                                   out=resid[:, oc2, qc * 512:(qc + 1) * 512],
                                   in0=resid[:, oc2, qc * 512:(qc + 1) * 512],
                                   in1=lnrep[:, 1, :], op=ALU.subtract))
                    m_out = em.op("gpsimd",
                                  lambda e, qc=qc, oc2=oc2: e.dma_start(
                                      out=outre[:, oc2, qc * 512:(qc + 1) * 512],
                                      in_=resid[:, oc2, qc * 512:(qc + 1) * 512]),
                                  waits=[("vector", m9)], dma="out")
        em.wait("gpsimd", "q_out", m_out)


def build_fused(upto=6, dbg=False):
    nc = bass.Bass("TRN2", target_bir_lowering=False)
    ki = "ExternalInput"
    DR = {}
    DBG = ("x1_tm", "x2_own", "yz_fm", "x3_fm")

    def dt(name, shape, dtype=F32, kind=ki):
        if dbg and kind == "Internal" and name in DBG:
            kind = "ExternalOutput"
        return nc.dram_tensor(name, shape, dtype, kind=kind).ap()

    DR["stpat"] = dt("stpat", [128, 130])
    consts = dt("consts", [128, 132])
    DR["identf"] = dt("identf", [128, 128])
    DR["identb"] = dt("identb", [128, 128], BF16)
    for nm in ("Wq", "Wk", "Wv", "Wo"):
        DR[nm] = dt(nm, [D, D])
    DR["x_fm"] = dt("x_fm", [D, S])
    DR["xq_fm"] = dt("xq_fm", [D, TOK])
    DR["bo"] = dt("bo", [128, NCH])
    DR["x1_fm"] = dt("x1_fm", [D, TOK], F32, "Internal")
    DR["k_hbm"] = dt("k_hbm", [D, S], BF16, "Internal")
    DR["v_hbm"] = dt("v_hbm", [S, D], BF16, "Internal")
    DR["s_hbm"] = dt("s_hbmA", [2, 512], F32, "Internal")
    DR["r_hbm"] = dt("r_hbm", [TOK], F32, "Internal")
    DR["x1_tm"] = dt("x1_tm", [TOK, D], F32, "Internal")
    DR["wg"] = dt("wg", [384, D])
    DR["wu"] = dt("wu", [384, D])
    DR["wd"] = dt("wd", [128, 2816])
    DR["cc_src"] = dt("cc_src", [1146880], BF16, "Internal")
    DR["cc_w"] = dt("cc_w", [8, 1146880], BF16, "Internal")
    DR["x2_own"] = dt("x2_own", [D, TOK], F32, "Internal")
    DR["x2s"] = dt("x2s", [D, TOK], BF16, "Internal")
    DR["x2_both"] = dt("x2_both", [2, D, TOK], BF16, "Internal")
    DR["winx_f"] = dt("winx_f", [128, NCH, DIN])
    DR["winz_f"] = dt("winz_f", [128, NCH, 1024])
    DR["wxp_f"] = dt("wxp_f", [128, 16, 96])
    DR["wdt_f"] = dt("wdt_f", [64, 8, 128])
    for nm, shape in (("wconv", [128, 16, 4]), ("bconv", [128, 16]),
                      ("bdt", [128, 8]), ("Dcol", [128, 8]), ("acomp", [128, 8, 16])):
        DR[nm] = dt(nm, shape)
    for nm, shape in (("rep8", [128, 16, 128]), ("sel8", [128, 16, 128]),
                      ("bcpat", [32, 2, 128])):
        DR[nm] = dt(nm, shape, BF16)
    DR["yz_fm"] = dt("yz_fm", [1024, S], BF16, "Internal")
    DR["W_outh"] = dt("W_outh", [1024, D])
    DR["P_mix"] = dt("P_mix", [2, D, TOK], F32, "Internal")
    DR["mix_own"] = dt("mix_own", [D, TOK], F32, "Internal")
    DR["s_hbmD"] = dt("s_hbmD", [2, 512], F32, "Internal")
    DR["x3_fm"] = dt("x3_fm", [D, TOK], F32, "Internal")
    for nm, shape in (("gw", [128, 8, 16]), ("be1", [128, 16, 4]), ("be2", [16, D]),
                      ("be3", [128, 16, 4]), ("ws1", [D, D]), ("ws2", [D, D]),
                      ("ws3", [D, D]), ("bs1", [128, 8]), ("bs2", [128, 8]),
                      ("bs3", [128, 8])):
        DR[nm] = dt(nm, shape)
    DR["gb"] = dt("gb", [16])
    DR["We1"] = dt("We1", [16, D, 512])
    DR["We2"] = dt("We2", [16, 512, D])
    DR["We3"] = dt("We3", [16, D, 512])
    DR["x4_fm"] = dt("x4_fm", [D, TOK], F32, "ExternalOutput")
    DR["s_hbmE"] = dt("s_hbmE", [2, 512], F32, "Internal")
    DR["g_hbm"] = dt("g_hbm", [16], F32, "Internal")
    DR["c_hbm"] = dt("c_hbm", [16, 1024], F32, "Internal")

    es = contextlib.ExitStack()
    with es:
        ar = es.enter_context(Arena(nc, cols=48850))
        PS1 = es.enter_context(nc.psum_tensor("PS1", [128, 2048], F32))[:, :]
        PS2 = es.enter_context(nc.psum_tensor("PS2", [128, 2048], F32))[:, :]
        em = Emitter(nc, es)
        em.prefix = ""
        blk = es.enter_context(nc.Block())
        emit_W(nc, em, ar, {"WT": PS1[:, 0:1024]}, DR, fused=True)
        em.barrier()
        m_wag = em.op("gpsimd",
                      lambda e: e.collective_compute(
                          "AllGather", ALU.bypass, replica_groups=RG_ALL,
                          ins=[DR["cc_src"][:]], outs=[DR["cc_w"][:, :]]),
                      waits=[("q_wst", em.qcnt("wst"))])
        if upto >= 2:
            DRA = dict(DR)
            DRA["consts"] = consts[:, 0:68]
            emit_A(nc, em, ar, {"PA": PS1, "PB": PS2[0:65, :]}, DRA, fused=True)
            em.barrier()
        if upto >= 3:
            DRB = dict(DR)
            DRB["consts"] = consts[:, 0:68]
            DRB["m_wag"] = m_wag
            emit_B(nc, em, ar, mk_P_B(PS1, PS2), DRB, fused=True)
            em.barrier()
            m_xag = em.op("gpsimd",
                          lambda e: e.collective_compute(
                              "AllGather", ALU.bypass, replica_groups=RG_PAIRS,
                              ins=[DR["x2s"][:, :]],
                              outs=[DR["x2_both"][:, :, :]]),
                          waits=[("q_bxo0", em.qcnt("bxo0")),
                                 ("q_bxo1", em.qcnt("bxo1"))])
        if upto >= 4:
            DRC = dict(DR)
            DRC["consts"] = consts[:, 0:68]
            DRC["m_xag"] = m_xag
            emit_C(nc, em, ar, mk_P_C(PS1, PS2), DRC, fused=True)
            em.barrier()
        if upto >= 5:
            DRD = dict(DR)
            DRD["consts"] = consts[:, 0:68]
            emit_D2(nc, em, ar, {"PA": PS1, "PB": PS2[0:65, :]}, DRD)
            em.barrier()
        if upto >= 6:
            DRE = dict(DR)
            DRE["consts"] = consts
            emit_E(nc, em, ar, mk_P_E(PS1, PS2), DRE)
        em.build(blk)
    return nc


def make_in_maps(inputs):
    stpat = np.zeros((128, 130), np.float32)
    stpat[:, 0] = 1.0
    stpat[:, 129] = 1.0
    consts = np.zeros((128, 132), np.float32)
    consts[:, 0] = 1.0
    consts[:, 1] = EPS
    consts[:, 2] = 10.0
    consts[:, 3] = -10.0
    consts[:, 4:132] = 1.0
    identf = np.eye(128, dtype=np.float32)
    identb = np.eye(128).astype(BF)
    x0 = np.asarray(inputs["L0_attn_x"])
    Wq = np.asarray(inputs["L0_attn_Wq"])
    Wk = np.asarray(inputs["L0_attn_Wk"])
    Wv = np.asarray(inputs["L0_attn_Wv"])
    Wo = np.asarray(inputs["L0_attn_Wo"])
    bo_l = np.ascontiguousarray(np.asarray(inputs["L0_attn_bo"]).reshape(NCH, 128).T)
    wgp = np.zeros((3072, D), np.float32)
    wgp[:2816] = np.asarray(inputs["L0_ffn_w_gate"])
    wup = np.zeros((3072, D), np.float32)
    wup[:2816] = np.asarray(inputs["L0_ffn_w_up"])
    wd_full = np.asarray(inputs["L0_ffn_w_down"])
    W_in = np.asarray(inputs["L1_attn_W_in"])
    W_conv = np.asarray(inputs["L1_attn_W_conv"])
    b_conv = np.asarray(inputs["L1_attn_b_conv"])
    W_xp = np.asarray(inputs["L1_attn_W_xproj"])
    W_dt = np.asarray(inputs["L1_attn_W_dt"])
    b_dt = np.asarray(inputs["L1_attn_b_dt"])
    A_log = np.asarray(inputs["L1_attn_A_log"])
    Dv = np.asarray(inputs["L1_attn_D"])
    W_out = np.asarray(inputs["L1_attn_W_out"])
    rep8 = np.zeros((128, 16, 128), BF)
    sel8 = np.zeros((128, 16, 128), BF)
    for o in range(16):
        for p in range(128):
            dd = 8 * o + (p % 8)
            rep8[dd, o, p] = 1
            sel8[p, o, dd] = 1
    bcp = np.zeros((32, 2, 128), BF)
    for p in range(128):
        bcp[p // 8, 0, p] = 1
        bcp[16 + p // 8, 1, p] = 1
    gwl = np.ascontiguousarray(
        np.asarray(inputs["L1_ffn_gate_w"]).T.reshape(8, 128, 16).transpose(1, 0, 2))
    be1l = np.ascontiguousarray(
        np.asarray(inputs["L1_ffn_be1"]).reshape(16, 4, 128).transpose(2, 0, 1))
    be3l = np.ascontiguousarray(
        np.asarray(inputs["L1_ffn_be3"]).reshape(16, 4, 128).transpose(2, 0, 1))
    bs1l = np.ascontiguousarray(np.asarray(inputs["L1_ffn_bs1"]).reshape(8, 128).T)
    bs2l = np.ascontiguousarray(np.asarray(inputs["L1_ffn_bs2"]).reshape(8, 128).T)
    bs3l = np.ascontiguousarray(np.asarray(inputs["L1_ffn_bs3"]).reshape(8, 128).T)

    in_maps = []
    for c in range(8):
        b, hf = divmod(c, 2)
        xb = _fm(x0[b])
        own = slice(hf * 1024, (hf + 1) * 1024)
        perm = np.r_[np.arange(hf * 1024, (hf + 1) * 1024),
                     np.arange((1 - hf) * 1024, (2 - hf) * 1024)]
        winx = W_in[:, :2048][:, perm]
        winz = W_in[:, 2048:][:, own]
        wxp_p = W_xp[perm, :]
        wdt_h = W_dt[:, own]
        acomp = np.zeros((128, 8, 16), np.float32)
        for p in range(128):
            n, dj = p // 8, p % 8
            for ch in range(8):
                for o in range(16):
                    acomp[p, ch, o] = A_log[hf * 1024 + ch * 128 + 8 * o + dj, n]
        in_maps.append({
            "x_fm": xb,
            "xq_fm": np.ascontiguousarray(xb[:, hf * TOK:(hf + 1) * TOK]),
            "Wq": Wq, "Wk": Wk, "Wv": Wv, "Wo": Wo, "bo": bo_l,
            "stpat": stpat, "consts": consts, "identf": identf, "identb": identb,
            "wg": np.ascontiguousarray(wgp[c * 384:(c + 1) * 384]),
            "wu": np.ascontiguousarray(wup[c * 384:(c + 1) * 384]),
            "wd": np.ascontiguousarray(wd_full[c * 128:(c + 1) * 128]),
            "winx_f": np.ascontiguousarray(winx.reshape(NCH, 128, DIN).transpose(1, 0, 2)),
            "winz_f": np.ascontiguousarray(winz.reshape(NCH, 128, 1024).transpose(1, 0, 2)),
            "wxp_f": np.ascontiguousarray(wxp_p.reshape(16, 128, 96).transpose(1, 0, 2)),
            "wdt_f": np.ascontiguousarray(wdt_h.reshape(64, 8, 128)),
            "wconv": np.ascontiguousarray(
                W_conv[:, :2048][:, perm].T.reshape(16, 128, 4).transpose(1, 0, 2)),
            "bconv": np.ascontiguousarray(b_conv[perm].reshape(16, 128).T),
            "bdt": np.ascontiguousarray(b_dt[own].reshape(8, 128).T),
            "Dcol": np.ascontiguousarray(Dv[own].reshape(8, 128).T),
            "acomp": acomp,
            "rep8": rep8, "sel8": sel8, "bcpat": bcp,
            "W_outh": np.ascontiguousarray(W_out[own, :]),
            "gw": gwl, "gb": np.asarray(inputs["L1_ffn_gate_b"]),
            "We1": np.asarray(inputs["L1_ffn_We1"]),
            "We2": np.asarray(inputs["L1_ffn_We2"]),
            "We3": np.asarray(inputs["L1_ffn_We3"]),
            "be1": be1l, "be2": np.asarray(inputs["L1_ffn_be2"]), "be3": be3l,
            "ws1": np.asarray(inputs["L1_ffn_ws1"]),
            "ws2": np.asarray(inputs["L1_ffn_ws2"]),
            "ws3": np.asarray(inputs["L1_ffn_ws3"]),
            "bs1": bs1l, "bs2": bs2l, "bs3": bs3l,
        })
    return in_maps


def kernel(**inputs):
    in_maps = make_in_maps(inputs)
    nc = build_fused()
    res = run_bass_kernel_spmd(nc, in_maps, core_ids=list(range(8)))
    x4 = np.empty((B, S, D), np.float32)
    for c in range(8):
        b, hf = divmod(c, 2)
        x4[b, hf * TOK:(hf + 1) * TOK, :] = res.results[c]["x4_fm"].T
    return x4


def kernel_unfused(**inputs):
    wq = host_launch_W(inputs)
    x1 = host_launch_A(inputs)
    x2 = host_launch_B(x1, inputs, wq)
    yz = host_launch_C(x2, inputs)
    x3 = host_launch_D(yz, x2, inputs)
    x4 = host_launch_E(x3, inputs)
    return x4
